# version 01
# baseline (speedup 1.0000x reference)
import contextlib
import numpy as np
import ml_dtypes
import concourse.bass as bass
import concourse.mybir as mybir
from concourse.bass_utils import run_bass_kernel_spmd

F32 = mybir.dt.float32
BF16 = mybir.dt.bfloat16
I32 = mybir.dt.int32
I16 = mybir.dt.int16
AF = mybir.ActivationFunctionType
ALU = mybir.AluOpType
AX = mybir.AxisListType

BF = ml_dtypes.bfloat16

S, D, HD, NH, NKV, CHK, NCHK = 8192, 2048, 128, 16, 4, 256, 32
NCORES = 8
TPC = S // NCORES
NTT = TPC // 128
DFF = 8192
NT = 224
MAGIC = 12582912.0
SCALE_QK = float(HD) ** -0.5
GELU_C = 0.7978845608028654
NEG = -1.0e30


def host_tables():
    t = {}
    t["eye_bf"] = np.eye(128).astype(BF)
    t["eye_f32"] = np.eye(128).astype(np.float32)
    t["lt_bf"] = (np.arange(128)[:, None] < np.arange(128)[None, :]).astype(BF)
    t["ut32_f32"] = (np.arange(32)[:, None] < np.arange(32)[None, :]).astype(np.float32)
    t["eye32_f32"] = np.eye(32).astype(np.float32)
    t["ones_col_bf"] = np.ones((128, 1), BF)
    t["ones_1x128_bf"] = np.ones((1, 128), BF)
    t["ones_1x128_f32"] = np.ones((1, 128), np.float32)
    t["ones_32c_f32"] = np.ones((32, 1), np.float32)
    t["ones_1x1_f32"] = np.ones((1, 1), np.float32)
    t["iota32"] = np.broadcast_to(np.arange(32, dtype=np.float32), (128, 32)).copy()
    t["iota128c"] = np.arange(128, dtype=np.float32).reshape(128, 1)
    col = np.arange(256)
    keyloc = (2 * (col % 128) + col // 128).astype(BF)
    t["keyloc_bf"] = np.broadcast_to(keyloc, (128, 256)).copy()
    t["cval"] = np.broadcast_to(np.arange(40, dtype=np.float32), (128, 40)).copy()
    t["bnd32"] = np.broadcast_to(128.0 * np.arange(NT, dtype=np.float32),
                                 (32, NT)).copy()
    t["qloc0"] = np.arange(128, dtype=np.float32).reshape(128, 1)
    t["qloc1"] = (np.arange(128, dtype=np.float32) + 128).reshape(128, 1)
    pos = np.arange(S, dtype=np.float64)
    inv = 1.0 / (10000.0 ** (np.arange(0, HD, 2, dtype=np.float64) / HD))
    ang = pos[:, None] * inv[None, :]
    t["cosT"] = (np.cos(ang) * SCALE_QK).astype(np.float32)
    t["sinT"] = (np.sin(ang) * SCALE_QK).astype(np.float32)
    t["cosTu"] = np.cos(ang).astype(np.float32)
    t["sinTu"] = np.sin(ang).astype(np.float32)
    return t


class Prog:

    ENGS = ("sync", "tensor", "vector", "scalar", "gpsimd")

    def __init__(self, nc):
        self.nc = nc
        self._sem_cm = nc.semaphore("bar")
        self.sem = self._sem_cm.__enter__()
        self.count = 0
        self.ops = {e: [] for e in self.ENGS}
        self.pending = {}
        self.wait_at = {e: None for e in self.ENGS}

    class _H:
        def __init__(self, p, name):
            object.__setattr__(self, "p", p)
            object.__setattr__(self, "name", name)

        def __getattr__(self, attr):
            p, name = self.p, self.name

            def wrapped(*a, **kw):
                if p.wait_at[name] is not None:
                    p.ops[name].append(("wait", p.wait_at[name]))
                    p.wait_at[name] = None
                isdma = attr in ("dma_start", "indirect_dma_start")
                p.ops[name].append(["op", attr, a, kw, 16 if isdma else 0])
                if isdma:
                    p.count += 16
                else:
                    p.pending[name] = len(p.ops[name]) - 1
            return wrapped

    def on(self, name):
        return Prog._H(self, name)

    def bar(self):
        for name, idx in self.pending.items():
            self.ops[name][idx][4] += 1
            self.count += 1
        self.pending = {}
        for e in self.ENGS:
            self.wait_at[e] = self.count

    def snap(self):
        for name, idx in list(self.pending.items()):
            self.ops[name][idx][4] += 1
            self.count += 1
        self.pending = {}
        return self.count

    def wait(self, eng, val):
        self.ops[eng].append(("wait", val))
        if self.wait_at[eng] is not None and self.wait_at[eng] <= val:
            self.wait_at[eng] = None

    def finish(self, eng="sync"):
        self.bar()
        self.ops[eng].append(("wait", self.count))
        self.wait_at[eng] = None

    def replay(self, block):
        def mk(name):
            def run(handle):
                eng = getattr(self.nc, name)
                for op in self.ops[name]:
                    if op[0] == "wait":
                        eng.wait_ge(self.sem, op[1])
                    else:
                        _, attr, a, kw, inc = op
                        ins = getattr(eng, attr)(*a, **kw)
                        if inc:
                            ins.then_inc(self.sem, inc)
            return run
        block.sync(mk("sync"))
        block.tensor(mk("tensor"))
        block.vector(mk("vector"))
        block.scalar(mk("scalar"))
        block.gpsimd(mk("gpsimd"))



def build_l1():
    nc = bass.Bass("TRN2", target_bir_lowering=False)
    X = nc.dram_tensor("x_rows", [TPC, D], F32, kind="ExternalInput").ap()
    EYE = nc.dram_tensor("eye_bf", [128, 128], BF16, kind="ExternalInput").ap()
    XN = nc.dram_tensor("xn_sl", [TPC, D], BF16, kind="ExternalOutput").ap()
    XNT = nc.dram_tensor("xnt_sl", [D, TPC], BF16, kind="ExternalOutput").ap()

    with (
        nc.sbuf_tensor("xt", [128, D], F32) as xt,
        nc.sbuf_tensor("scr", [128, D], F32) as scr,
        nc.sbuf_tensor("stat", [128, 4], F32) as stat,
        nc.sbuf_tensor("xnb", [128, D], BF16) as xnb,
        nc.sbuf_tensor("eye", [128, 128], BF16) as eye,
        nc.sbuf_tensor("tev", [128, 2048], BF16) as tev,
        nc.psum_tensor("pt0", [128, 128], BF16) as pt0,
        nc.psum_tensor("pt1", [128, 128], BF16) as pt1,
        nc.Block() as block,
    ):
        p = Prog(nc)
        sy, ve, sc, te = p.on("sync"), p.on("vector"), p.on("scalar"), p.on("tensor")
        sy.dma_start(out=eye[:], in_=EYE)
        p.bar()
        for i in range(NTT):
            sy.dma_start(out=xt[:], in_=X[i * 128:(i + 1) * 128, :])
            p.bar()
            ve.reduce_sum(stat[:, 0:1], xt[:], axis=AX.X)
            p.bar()
            ve.tensor_scalar(stat[:, 0:1], stat[:, 0:1], 1.0 / D, None, op0=ALU.mult)
            p.bar()
            ve.tensor_scalar(scr[:], xt[:], stat[:, 0:1], None, op0=ALU.subtract)
            p.bar()
            sc.activation(out=xt[:], in_=scr[:], func=AF.Square)
            p.bar()
            ve.reduce_sum(stat[:, 1:2], xt[:], axis=AX.X)
            p.bar()
            ve.tensor_scalar(stat[:, 1:2], stat[:, 1:2], 1.0 / D, 1e-5,
                             op0=ALU.mult, op1=ALU.add)
            p.bar()
            sc.activation(out=stat[:, 2:3], in_=stat[:, 1:2], func=AF.Sqrt)
            p.bar()
            ve.reciprocal(stat[:, 3:4], stat[:, 2:3])
            p.bar()
            ve.tensor_scalar(xnb[:], scr[:], stat[:, 3:4], None, op0=ALU.mult)
            p.bar()
            sy.dma_start(out=XN[i * 128:(i + 1) * 128, :], in_=xnb[:])
            for j in range(0, 16, 2):
                te.transpose(pt0[:], xnb[:, j * 128:(j + 1) * 128], eye[:])
                te.transpose(pt1[:], xnb[:, (j + 1) * 128:(j + 2) * 128], eye[:])
                p.bar()
                ve.tensor_copy(tev[:, j * 128:(j + 1) * 128], pt0[:])
                ve.tensor_copy(tev[:, (j + 1) * 128:(j + 2) * 128], pt1[:])
                p.bar()
            sc.dma_start(
                out=XNT[:, i * 128:(i + 1) * 128].rearrange("(c p) f -> p c f", p=128),
                in_=tev[:].rearrange("p (c f) -> p c f", f=128))
            p.bar()
        p.finish()
        p.replay(block)
    return nc


def stage_l1(inputs):
    t = host_tables()
    x = np.asarray(inputs["L0_attn_hidden_states"]).reshape(S, D)
    return [{
        "x_rows": np.ascontiguousarray(x[c * TPC:(c + 1) * TPC]).astype(np.float32),
        "eye_bf": t["eye_bf"],
    } for c in range(NCORES)]



TABLE_SPECS = [
    ("eye_bf", [128, 128], BF16), ("eye_f32", [128, 128], F32),
    ("lt_bf", [128, 128], BF16),
    ("ut32_f32", [32, 32], F32), ("eye32_f32", [32, 32], F32),
    ("ones_col_bf", [128, 1], BF16), ("ones_1x128_bf", [1, 128], BF16),
    ("ones_1x128_f32", [1, 128], F32), ("ones_32c_f32", [32, 1], F32),
    ("ones_1x1_f32", [1, 1], F32),
    ("iota32", [128, 32], F32), ("iota128c", [128, 1], F32),
    ("keyloc_bf", [128, 256], BF16), ("cval", [128, 40], F32),
    ("bnd32", [32, NT], F32), ("qloc0", [128, 1], F32), ("qloc1", [128, 1], F32),
]


def build_l2(debug=0, nown=64, nsel=NT, cut=99, safe=False):
    nc = bass.Bass("TRN2", target_bir_lowering=False)
    XNT = nc.dram_tensor("xnt_full", [D, S], BF16, kind="ExternalInput").ap()
    WQ = nc.dram_tensor("wq2", [D, 256], F32, kind="ExternalInput").ap()
    WK = nc.dram_tensor("wk1", [D, 128], F32, kind="ExternalInput").ap()
    WV = nc.dram_tensor("wv1", [D, 128], F32, kind="ExternalInput").ap()
    COST = nc.dram_tensor("cosT", [S, 64], F32, kind="ExternalInput").ap()
    SINT = nc.dram_tensor("sinT", [S, 64], F32, kind="ExternalInput").ap()
    COSU = nc.dram_tensor("cosTu", [S, 64], F32, kind="ExternalInput").ap()
    SINU = nc.dram_tensor("sinTu", [S, 64], F32, kind="ExternalInput").ap()
    tby = {nm: nc.dram_tensor(nm, shp, dt, kind="ExternalInput").ap()
           for nm, shp, dt in TABLE_SPECS}

    OT = nc.dram_tensor("ot_sl", [256, S], BF16, kind="ExternalOutput").ap()

    ik = "ExternalOutput" if debug else "Internal"
    QDs = [nc.dram_tensor(f"qd{h}", [S, 128], BF16, kind=ik).ap()
           for h in range(2)]
    KBL = nc.dram_tensor("kbl", [NCHK * 128, 256], BF16, kind=ik).ap()
    VP = nc.dram_tensor("vp", [NCHK * 128, 258], BF16, kind=ik).ap()
    LISTs = [nc.dram_tensor(f"list{h}", [NT * 128, 1], I16, kind=ik).ap()
             for h in range(2)]
    OSs = [nc.dram_tensor(f"os{h}", [4 * S, 129], F32,
                          kind=("ExternalOutput" if debug == 2 else "Internal")).ap()
           for h in range(2)]
    if debug:
        DBG = nc.dram_tensor("dbg", [128, 6 * NT], F32, kind="ExternalOutput").ap()
        DBGK = nc.dram_tensor("dbgk", [128, 32], F32, kind="ExternalOutput").ap()

    es = contextlib.ExitStack()

    def sb(name, shape, dt):
        return es.enter_context(nc.sbuf_tensor(name, shape, dt))

    with es:
        tb = {nm: sb("t_" + nm, shp, dt) for nm, shp, dt in TABLE_SPECS}
        wqb = sb("wqb", [128, 16 * 256], BF16)
        wkb = sb("wkb", [128, 16 * 128], BF16)
        wvb = sb("wvb", [128, 16 * 128], BF16)
        wstg = sb("wstg", [128, 16 * 256], F32)
        xnt_t = sb("xnt_t", [128, 2048], BF16)
        rop = sb("rop", [128, 128], F32)
        sc1 = sb("sc1", [128, 128], F32)
        sc2 = sb("sc2", [128, 128], F32)
        ctt = sb("ctt", [128, 64], F32)
        stt_ = sb("stt_", [128, 64], F32)
        cttu = sb("cttu", [128, 64], F32)
        sttu = sb("sttu", [128, 64], F32)
        ktokb = sb("ktokb", [128, 128], BF16)
        kfev = sb("kfev", [128, 128], BF16)
        kbev = sb("kbev", [128, 128], BF16)
        vev = sb("vev", [128, 129], BF16)
        qsbs = [sb(f"qsb{h}", [128, 64 * 128], BF16) for h in range(2)]
        kmacc = sb("kmacc", [128, 64], F32)
        kmean = sb("kmean", [128, 32], BF16)
        qft = sb("qft", [128, 128], BF16)
        gg = sb("gg", [128, 32], F32)
        tmask = sb("tmask", [128, 32], F32)
        m8 = sb("m8", [128, 8], F32)
        hraw = sb("hraw", [128, 32], F32)
        escr = sb("escr", [128, 32], F32)
        idxc = sb("idxc", [128, 1], F32)
        isbad = sb("isbad", [128, 1], I32)
        hcb = sb("hcb", [128, 3 * 32], BF16)
        hall = sb("hall", [128, 32], BF16)
        ssum = sb("ssum", [128, 32], F32)
        csrow = sb("csrow", [1, 32], BF16)
        csevf = sb("csevf", [1, 32], F32)
        ctabs = [sb(f"ctab{h}", [128, 192], F32) for h in range(2)]
        prels = [sb(f"prel{h}", [128, 192], F32) for h in range(2)]
        totf = sb("totf", [1, 32], F32)
        ceilr = sb("ceilr", [1, 32], F32)
        ceilc = sb("ceilc", [32, 1], F32)
        basec = sb("basec", [32, 1], F32)
        baser = sb("baser", [1, 32], F32)
        basebc = sb("basebc", [128, 32], F32)
        cmp32 = sb("cmp32", [32, NT], F32)
        tcrow = sb("tcrow", [1, NT], F32)
        tcb = sb("tcb", [128, NT], F32)
        hcf = sb("hcf", [128, 32], F32)
        pcol = sb("pcol", [128, 1], F32)
        posi = sb("posi", [128, 1], I32)
        svali = sb("svali", [128, 1], I16)
        svalf = sb("svalf", [128, 1], F32)
        zro16 = sb("zro16", [128, NT], I16)
        vtabs = [sb(f"vtab{h}", [128, NT], F32) for h in range(2)]
        qposs = [sb(f"qpos{h}", [128, NT], F32) for h in range(2)]
        gidxs = [sb(f"gidx{h}", [128, NT], I32) for h in range(2)]
        sidxs = [sb(f"sidx{h}", [128, NT], I32) for h in range(2)]
        kvidxs = [sb(f"kvidx{h}", [128, NT], I32) for h in range(2)]
        qrels = [sb(f"qrel{h}", [128, NT], F32) for h in range(2)]
        lraw = sb("lraw", [128, NT], I16)
        lrf = sb("lrf", [128, NT], F32)
        dscr = sb("dscr", [128, NT], F32)
        dscr2 = sb("dscr2", [128, NT], F32)
        qg = sb("qg", [128, 2 * 128], BF16)
        kst = sb("kst", [128, 2 * 256], BF16)
        vst = sb("vst", [128, 2 * 258], BF16)
        qftl = sb("qftl", [128, 2 * 128], BF16)
        expt = sb("expt", [128, 2 * 256], BF16)
        probs = sb("probs", [128, 2 * 256], BF16)
        ptev = sb("ptev", [128, 2 * 256], BF16)
        ostg = sb("ostg", [128, 2 * 129], F32)
        comb = sb("comb", [128, 4 * 129], F32)
        osum = sb("osum", [128, 129], F32)
        s01 = sb("s01", [128, 129], F32)
        s23 = sb("s23", [128, 129], F32)
        rz = sb("rz", [128, 1], F32)
        ofin = sb("ofin", [128, 128], BF16)
        otev = sb("otev", [128, 128], BF16)

        block = es.enter_context(nc.Block())
        p = Prog(nc)
        sy, te, ve, sc, gp = (p.on("sync"), p.on("tensor"), p.on("vector"),
                              p.on("scalar"), p.on("gpsimd"))

        for nm, _, _ in TABLE_SPECS:
            sy.dma_start(out=tb[nm][:], in_=tby[nm])
        sy.dma_start(out=wstg[:].rearrange("p (c f) -> p c f", f=256),
                     in_=WQ.rearrange("(c p) f -> p c f", p=128))
        p.bar()
        ve.tensor_copy(wqb[:], wstg[:])
        p.bar()
        sy.dma_start(out=wstg[:, 0:2048].rearrange("p (c f) -> p c f", f=128),
                     in_=WK.rearrange("(c p) f -> p c f", p=128))
        p.bar()
        ve.tensor_copy(wkb[:], wstg[:, 0:2048])
        p.bar()
        sy.dma_start(out=wstg[:, 0:2048].rearrange("p (c f) -> p c f", f=128),
                     in_=WV.rearrange("(c p) f -> p c f", p=128))
        p.bar()
        ve.tensor_copy(wvb[:], wstg[:, 0:2048])
        ve.memset(vev[:, 128:129], 1.0)
        ve.memset(zro16[:], 0)
        ve.memset(kmacc[:], 0.0)
        p.bar()
        for hh in range(2):
            sy.dma_start(out=LISTs[hh].rearrange("(t p) one -> t (p one)", p=128)[0:128, :],
                         in_=zro16[:, 0:128])
            sy.dma_start(out=LISTs[hh].rearrange("(t p) one -> t (p one)", p=128)[128:224, :],
                         in_=zro16[0:96, 0:128])
        p.bar()

        with nc.psum_tensor("bk", [128, 128], F32) as bk, \
             nc.psum_tensor("bv", [128, 128], F32) as bv, \
             nc.psum_tensor("bqa", [128, 128], F32) as bqa, \
             nc.psum_tensor("bqb", [128, 128], F32) as bqb, \
             nc.psum_tensor("bkT", [128, 128], BF16) as bkT:
            for t in range(64):
                sy.dma_start(out=xnt_t[:].rearrange("p (c f) -> p c f", f=128),
                             in_=XNT[:, t * 128:(t + 1) * 128].rearrange(
                                 "(c p) f -> p c f", p=128))
                sc.dma_start(out=ctt[:], in_=COST[t * 128:(t + 1) * 128, :])
                sc.dma_start(out=stt_[:], in_=SINT[t * 128:(t + 1) * 128, :])
                sc.dma_start(out=cttu[:], in_=COSU[t * 128:(t + 1) * 128, :])
                sc.dma_start(out=sttu[:], in_=SINU[t * 128:(t + 1) * 128, :])
                p.bar()
                for j in range(16):
                    st, sp = (j == 0), (j == 15)
                    xb = xnt_t[:, j * 128:(j + 1) * 128]
                    te.matmul(bk[:], xb, wkb[:, j * 128:(j + 1) * 128],
                              start=st, stop=sp)
                    te.matmul(bv[:], xb, wvb[:, j * 128:(j + 1) * 128],
                              start=st, stop=sp)
                    te.matmul(bqa[:], xb, wqb[:, j * 256:j * 256 + 128],
                              start=st, stop=sp)
                    te.matmul(bqb[:], xb, wqb[:, j * 256 + 128:(j + 1) * 256],
                              start=st, stop=sp)
                p.bar()
                ve.tensor_mul(rop[:, 0:64], bk[:, 0:64], cttu[:])
                p.bar()
                ve.tensor_mul(rop[:, 64:128], bk[:, 64:128], cttu[:])
                p.bar()
                ve.tensor_mul(sc1[:, 0:64], bk[:, 64:128], sttu[:])
                p.bar()
                ve.tensor_mul(sc2[:, 0:64], bk[:, 0:64], sttu[:])
                p.bar()
                ve.tensor_sub(rop[:, 0:64], rop[:, 0:64], sc1[:, 0:64])
                p.bar()
                ve.tensor_add(rop[:, 64:128], rop[:, 64:128], sc2[:, 0:64])
                p.bar()
                ve.tensor_copy(ktokb[:], rop[:])
                p.bar()
                te.transpose(bkT[:], ktokb[:], tb["eye_bf"][:])
                p.bar()
                ve.tensor_copy(kfev[:], bkT[:])
                p.bar()
                ve.reduce_sum(kmacc[:, t:t + 1], kfev[:], axis=AX.X)
                ve.tensor_copy(kbev[:].rearrange("d (s k) -> d s k", s=2),
                               kfev[:].rearrange("d (k s) -> d s k", s=2))
                p.bar()
                ch_, hf = t // 2, t % 2
                sy.dma_start(
                    out=KBL[ch_ * 128:(ch_ + 1) * 128, :].rearrange(
                        "d (s k) -> d s k", s=2)[:, :, hf * 64:(hf + 1) * 64],
                    in_=kbev[:].rearrange("d (s k) -> d s k", s=2))
                ve.tensor_copy(vev[:, 0:128], bv[:])
                p.bar()
                sy.dma_start(
                    out=VP[t * 64:(t + 1) * 64, :].rearrange(
                        "r (s c) -> (r s) c", s=2),
                    in_=vev[:])
                for h, bq in ((0, bqa), (1, bqb)):
                    ve.tensor_mul(rop[:, 0:64], bq[:, 0:64], ctt[:])
                    p.bar()
                    ve.tensor_mul(rop[:, 64:128], bq[:, 64:128], ctt[:])
                    p.bar()
                    ve.tensor_mul(sc1[:, 0:64], bq[:, 64:128], stt_[:])
                    p.bar()
                    ve.tensor_mul(sc2[:, 0:64], bq[:, 0:64], stt_[:])
                    p.bar()
                    ve.tensor_sub(rop[:, 0:64], rop[:, 0:64], sc1[:, 0:64])
                    p.bar()
                    ve.tensor_add(rop[:, 64:128], rop[:, 64:128], sc2[:, 0:64])
                    p.bar()
                    ve.tensor_copy(qsbs[h][:, t * 128:(t + 1) * 128], rop[:])
                    p.bar()
                    sy.dma_start(out=QDs[h][t * 128:(t + 1) * 128, :],
                                 in_=qsbs[h][:, t * 128:(t + 1) * 128])
                p.bar()
        ve.tensor_add(dscr[:, 0:32], kmacc[:, 0:64:2], kmacc[:, 1:64:2])
        p.bar()
        ve.tensor_scalar(kmean[:], dscr[:, 0:32], 1.0 / 256, None, op0=ALU.mult)
        p.bar()

        with nc.psum_tensor("pqf", [128, 128], BF16) as pqf, \
             nc.psum_tensor("pg", [128, 32], F32) as pg, \
             nc.psum_tensor("pcs", [1, 32], F32) as pcs, \
             nc.psum_tensor("pr", [128, 32], F32) as pr, \
             nc.psum_tensor("ptot", [128, 32], F32) as ptot, \
             nc.psum_tensor("ptc", [128, NT], F32) as ptc, \
             nc.psum_tensor("ptr", [128, 128], F32) as ptr:
            for h in range(2):
                for qt in range(64):
                    qc = qt // 2
                    te.transpose(pqf[:], qsbs[h][:, qt * 128:(qt + 1) * 128],
                                 tb["eye_bf"][:])
                    p.bar()
                    ve.tensor_copy(qft[:], pqf[:])
                    p.bar()
                    te.matmul(pg[:], qft[:], kmean[:], start=True, stop=True)
                    if qt % 2 == 0:
                        ve.tensor_scalar(tmask[:], tb["iota32"][:], qc - 0.5, NEG,
                                         op0=ALU.is_ge, op1=ALU.mult)
                    p.bar()
                    ve.tensor_add(gg[:], pg[:], tmask[:])
                    p.bar()
                    ve.max(out=m8[:], in_=gg[:])
                    p.bar()
                    for j in (1, 2, 3):
                        cjc = m8[:, j - 1:j]
                        ve.tensor_scalar(hraw[:], gg[:], cjc, None, op0=ALU.is_equal)
                        p.bar()
                        ve.tensor_mul(escr[:], hraw[:], tb["iota32"][:])
                        ve.tensor_scalar(isbad[:], cjc, -1.0e29, None, op0=ALU.is_le)
                        p.bar()
                        ve.reduce_sum(idxc[:], escr[:], axis=AX.X)
                        p.bar()
                        ve.select(idxc[:], isbad[:],
                                  tb["cval"][:, qc + j:qc + j + 1], idxc[:])
                        p.bar()
                        ve.tensor_copy(
                            ctabs[h][:, qt * 3 + j - 1:qt * 3 + j], idxc[:])
                        ve.tensor_scalar(hcb[:, (j - 1) * 32:j * 32],
                                         tb["iota32"][:], idxc[:], None,
                                         op0=ALU.is_equal)
                        p.bar()
                    ve.tensor_add(hall[:], hcb[:, 0:32], hcb[:, 32:64])
                    p.bar()
                    ve.tensor_add(hall[:], hall[:], hcb[:, 64:96])
                    p.bar()
                    te.matmul(pr[:], tb["lt_bf"][:], hall[:], start=True, stop=True)
                    p.bar()
                    ve.tensor_copy(ssum[:], pr[:])
                    p.bar()
                    if qt > 0:
                        ve.tensor_add(ssum[:], ssum[:], ptot[:])
                        p.bar()
                    for j in (1, 2, 3):
                        ve.tensor_mul(hcf[:], hcb[:, (j - 1) * 32:j * 32], ssum[:])
                        p.bar()
                        ve.reduce_sum(prels[h][:, qt * 3 + j - 1:qt * 3 + j],
                                      hcf[:], axis=AX.X)
                        p.bar()
                    for j in (1, 2, 3):
                        te.matmul(pcs[0:1, :], tb["ones_col_bf"][:],
                                  hcb[:, (j - 1) * 32:j * 32],
                                  start=(j == 1), stop=(j == 3))
                    p.bar()
                    ve.tensor_copy(csevf[0:1, :], pcs[0:1, :])
                    p.bar()
                    ve.tensor_copy(csrow[0:1, :], csevf[0:1, :])
                    p.bar()
                    te.matmul(ptot[:], tb["ones_1x128_bf"][:], csrow[0:1, :],
                              start=(qt == 0), stop=(qt == 63))
                    p.bar()
                ve.tensor_copy(totf[0:1, :], ptot[0:1, :])
                p.bar()
                ve.tensor_scalar(ceilr[0:1, :], totf[0:1, :], 1.0 / 128,
                                 0.49609375, op0=ALU.mult, op1=ALU.add)
                p.bar()
                ve.tensor_scalar(ceilr[0:1, :], ceilr[0:1, :], MAGIC, None,
                                 op0=ALU.add)
                p.bar()
                ve.tensor_scalar(ceilr[0:1, :], ceilr[0:1, :], MAGIC, 128.0,
                                 op0=ALU.subtract, op1=ALU.mult)
                p.bar()
                te.matmul(pg[0:32, 0:1], ceilr[0:1, :], tb["ones_1x1_f32"][:],
                          start=True, stop=True)
                p.bar()
                ve.tensor_copy(ceilc[:], pg[0:32, 0:1])
                p.bar()
                te.matmul(pr[0:32, 0:1], tb["ut32_f32"][:], ceilc[:],
                          start=True, stop=True)
                p.bar()
                ve.tensor_copy(basec[:], pr[0:32, 0:1])
                p.bar()
                te.matmul(pg[0:1, 0:32], basec[:], tb["eye32_f32"][:],
                          start=True, stop=True)
                p.bar()
                ve.tensor_copy(baser[0:1, :], pg[0:1, 0:32])
                p.bar()
                te.matmul(pr[:], tb["ones_1x128_f32"][:], baser[0:1, :],
                          start=True, stop=True)
                p.bar()
                ve.tensor_copy(basebc[:], pr[:])
                ve.tensor_scalar(cmp32[:], tb["bnd32"][:], basec[:], None,
                                 op0=ALU.is_ge)
                p.bar()
                te.matmul(ptc[0:1, :], tb["ones_32c_f32"][:], cmp32[:],
                          start=True, stop=True)
                p.bar()
                ve.tensor_scalar(tcrow[0:1, :], ptc[0:1, :], 1.0, None,
                                 op0=ALU.subtract)
                p.bar()
                te.matmul(ptc[:], tb["ones_1x128_f32"][:], tcrow[0:1, :],
                          start=True, stop=True)
                p.bar()
                ve.tensor_copy(tcb[:], ptc[:])
                p.bar()
                for qt in range(64):
                    for j in (1, 2, 3):
                        cix = qt * 3 + j - 1
                        ve.tensor_scalar(hcf[:], tb["iota32"][:],
                                         ctabs[h][:, cix:cix + 1], None,
                                         op0=ALU.is_equal)
                        p.bar()
                        ve.tensor_mul(hcf[:], hcf[:], basebc[:])
                        p.bar()
                        ve.reduce_sum(pcol[:], hcf[:], axis=AX.X)
                        p.bar()
                        ve.tensor_add(pcol[:], pcol[:], prels[h][:, cix:cix + 1])
                        p.bar()
                        ve.tensor_copy(posi[:], pcol[:])
                        ve.tensor_scalar(svalf[:], tb["iota128c"][:], 4.0,
                                         float(4 * qt * 128 + j),
                                         op0=ALU.mult, op1=ALU.add)
                        p.bar()
                        ve.tensor_copy(svali[:], svalf[:])
                        p.bar()
                        gp.indirect_dma_start(
                            out=LISTs[h],
                            out_offset=bass.IndirectOffsetOnAxis(posi[:], 0),
                            in_=svali[:],
                            in_offset=None,
                            oob_is_err=False)
                        p.bar()
                for blk, nblk in ((0, 128), (1, NT - 128)):
                    sy.dma_start(out=lraw[0:nblk, 0:128],
                                 in_=LISTs[h].rearrange("(t p) 1 -> t p", p=128)[
                                     blk * 128:blk * 128 + nblk, :])
                    p.bar()
                    ve.tensor_copy(lrf[0:nblk, 0:128], lraw[0:nblk, 0:128])
                    p.bar()
                    te.transpose(ptr[:, 0:nblk], lrf[0:nblk, 0:128],
                                 tb["eye_f32"][0:nblk, 0:nblk])
                    p.bar()
                    ve.tensor_copy(vtabs[h][:, blk * 128:blk * 128 + nblk],
                                   ptr[:, 0:nblk])
                    p.bar()
                vt = vtabs[h]
                ve.tensor_scalar(dscr[:], vt[:], 1.5, 0.25,
                                 op0=ALU.subtract, op1=ALU.mult)
                p.bar()
                ve.tensor_scalar(dscr[:], dscr[:], MAGIC, None, op0=ALU.add)
                p.bar()
                ve.tensor_scalar(qposs[h][:], dscr[:], MAGIC, None, op0=ALU.subtract)
                p.bar()
                ve.tensor_scalar_max(dscr[:], qposs[h][:], 0.0)
                p.bar()
                ve.tensor_copy(gidxs[h][:], dscr[:])
                ve.tensor_scalar(dscr[:], vt[:], 0.5, 1.0e7,
                                 op0=ALU.is_le, op1=ALU.mult)
                p.bar()
                ve.tensor_add(dscr2[:], vt[:], dscr[:])
                p.bar()
                ve.tensor_copy(sidxs[h][:], dscr2[:])
                ve.tensor_scalar(dscr[:], tcb[:], 128.0, None, op0=ALU.mult)
                p.bar()
                ve.tensor_add(dscr[:], dscr[:],
                              tb["iota128c"][:].to_broadcast([128, NT]))
                p.bar()
                ve.tensor_copy(kvidxs[h][:], dscr[:])
                ve.tensor_scalar(dscr2[:], tcb[:], 256.0, None, op0=ALU.mult)
                p.bar()
                ve.tensor_sub(qrels[h][:], qposs[h][:], dscr2[:])
                p.bar()

        if debug == 1:
            ve.tensor_copy(dscr[:], gidxs[0][:])
            p.bar()
            sy.dma_start(out=DBG[:, 0:NT], in_=dscr[:])
            ve.tensor_copy(dscr[:], sidxs[0][:])
            p.bar()
            sy.dma_start(out=DBG[:, NT:2 * NT], in_=dscr[:])
            ve.tensor_copy(dscr[:], kvidxs[0][:])
            p.bar()
            sy.dma_start(out=DBG[:, 2 * NT:3 * NT], in_=dscr[:])
            sy.dma_start(out=DBG[:, 3 * NT:4 * NT], in_=qrels[0][:])
            sy.dma_start(out=DBG[:, 4 * NT:5 * NT], in_=vtabs[0][:])
            sy.dma_start(out=DBG[:, 5 * NT:6 * NT], in_=tcb[:])
            ve.tensor_copy(gg[:], kmean[:])
            p.bar()
            sy.dma_start(out=DBGK, in_=gg[:])
            p.finish()
            p.replay(block)
            return nc

        with nc.psum_tensor("bqf0", [128, 128], BF16) as bqf0, \
             nc.psum_tensor("bqf1", [128, 128], BF16) as bqf1, \
             nc.psum_tensor("bs0", [128, 256], F32) as bs0, \
             nc.psum_tensor("bs1", [128, 256], F32) as bs1, \
             nc.psum_tensor("bt0", [128, 256], BF16) as bt0, \
             nc.psum_tensor("bt1", [128, 256], BF16) as bt1, \
             nc.psum_tensor("bo0", [128, 129], F32) as bo0, \
             nc.psum_tensor("bo1", [128, 129], F32) as bo1:
            bss, bts, bos = [bs0, bs1], [bt0, bt1], [bo0, bo1]
            bqfs = [bqf0, bqf1]
            hist = {k: {} for k in ("gath", "qft", "smm", "exp", "prb",
                                    "ptv", "oev", "sct")}

            def attn_unit(u, h, kind, idx):
                par = u % 2
                qgp = qg[:, par * 128:(par + 1) * 128]
                kstp = kst[:, par * 256:(par + 1) * 256]
                vstp = vst[:, par * 258:(par + 1) * 258]
                qftp = qftl[:, par * 128:(par + 1) * 128]
                exptp = expt[:, par * 256:(par + 1) * 256]
                prbp = probs[:, par * 256:(par + 1) * 256]
                ptevp = ptev[:, par * 256:(par + 1) * 256]
                ostgp = ostg[:, par * 129:(par + 1) * 129]
                pqfp = bqfs[par][:]

                w = hist["sct"].get(u - 2)
                if kind == "sel":
                    if w is not None:
                        p.wait("gpsimd", w)
                    gp.indirect_dma_start(
                        out=qgp, out_offset=None, in_=QDs[h],
                        in_offset=bass.IndirectOffsetOnAxis(
                            gidxs[h][:, idx:idx + 1], 0))
                    gp.indirect_dma_start(
                        out=kstp, out_offset=None, in_=KBL,
                        in_offset=bass.IndirectOffsetOnAxis(
                            kvidxs[h][:, idx:idx + 1], 0))
                    gp.indirect_dma_start(
                        out=vstp, out_offset=None, in_=VP,
                        in_offset=bass.IndirectOffsetOnAxis(
                            kvidxs[h][:, idx:idx + 1], 0))
                else:
                    c0 = idx // 2
                    if w is not None:
                        p.wait("sync", w)
                        p.wait("scalar", w)
                    sy.dma_start(out=kstp, in_=KBL[c0 * 128:(c0 + 1) * 128, :])
                    sc.dma_start(out=vstp, in_=VP[c0 * 128:(c0 + 1) * 128, :])
                hist["gath"][u] = p.snap()
                if safe:
                    p.bar()
                if cut <= 1:
                    for k in hist:
                        hist[k][u] = p.snap()
                    return

                p.wait("tensor", hist["gath"][u])
                if u >= 2:
                    p.wait("tensor", hist["qft"][u - 2])
                if kind == "sel":
                    te.transpose(pqfp, qgp, tb["eye_bf"][:])
                else:
                    te.transpose(pqfp, qsbs[h][:, idx * 128:(idx + 1) * 128],
                                 tb["eye_bf"][:])
                s_tp = p.snap()
                p.wait("vector", s_tp)
                ve.tensor_copy(qftp, pqfp)
                hist["qft"][u] = p.snap()
                if safe:
                    p.bar()
                if cut <= 2:
                    for k in ("smm", "exp", "prb", "ptv", "oev", "sct"):
                        hist[k][u] = p.snap()
                    return

                p.wait("tensor", hist["qft"][u])
                if u >= 2:
                    p.wait("tensor", hist["exp"][u - 2])
                te.matmul(bss[par][:], qftp, kstp, start=True, stop=True)
                hist["smm"][u] = p.snap()
                if safe:
                    p.bar()
                if cut <= 3:
                    for k in ("exp", "prb", "ptv", "oev", "sct"):
                        hist[k][u] = p.snap()
                    return

                p.wait("scalar", hist["smm"][u])
                sc.activation(out=exptp, in_=bss[par][:], func=AF.Exp)
                hist["exp"][u] = p.snap()
                if safe:
                    p.bar()
                if cut <= 4:
                    for k in ("prb", "ptv", "oev", "sct"):
                        hist[k][u] = p.snap()
                    return

                p.wait("vector", hist["exp"][u])
                if kind == "sel":
                    ve.scalar_tensor_tensor(
                        out=prbp, in0=tb["keyloc_bf"][:],
                        scalar=qrels[h][:, idx:idx + 1], in1=exptp,
                        op0=ALU.is_le, op1=ALU.mult)
                else:
                    ql = tb["qloc0"] if idx % 2 == 0 else tb["qloc1"]
                    ve.scalar_tensor_tensor(
                        out=prbp, in0=tb["keyloc_bf"][:], scalar=ql[:],
                        in1=exptp, op0=ALU.is_le, op1=ALU.mult)
                hist["prb"][u] = p.snap()
                if safe:
                    p.bar()
                if cut <= 5:
                    for k in ("ptv", "oev", "sct"):
                        hist[k][u] = p.snap()
                    return

                p.wait("tensor", hist["prb"][u])
                if u >= 2:
                    p.wait("tensor", hist["ptv"][u - 2])
                te.transpose(bts[par][:, 0:128], prbp[:, 0:128], tb["eye_bf"][:])
                te.transpose(bts[par][:, 128:256], prbp[:, 128:256], tb["eye_bf"][:])
                s_pt = p.snap()
                p.wait("vector", s_pt)
                ve.tensor_copy(ptevp, bts[par][:])
                hist["ptv"][u] = p.snap()
                if safe:
                    p.bar()
                if cut <= 6:
                    for k in ("oev", "sct"):
                        hist[k][u] = p.snap()
                    return

                p.wait("tensor", hist["ptv"][u])
                if u >= 2:
                    p.wait("tensor", hist["oev"][u - 2])
                te.matmul(bos[par][:], ptevp[:, 0:128], vstp[:, 0:129],
                          start=True, stop=False)
                te.matmul(bos[par][:], ptevp[:, 128:256], vstp[:, 129:258],
                          start=False, stop=True)
                s_pv = p.snap()
                p.wait("vector", s_pv)
                if u >= 2:
                    p.wait("vector", hist["sct"][u - 2])
                ve.tensor_copy(ostgp, bos[par][:])
                hist["oev"][u] = p.snap()
                if safe:
                    p.bar()
                if cut <= 7:
                    hist["sct"][u] = p.snap()
                    return

                if kind == "sel":
                    p.wait("gpsimd", hist["oev"][u])
                    gp.indirect_dma_start(
                        out=OSs[h], out_offset=bass.IndirectOffsetOnAxis(
                            sidxs[h][:, idx:idx + 1], 0),
                        in_=ostgp, in_offset=None, oob_is_err=False)
                else:
                    p.wait("sync", hist["oev"][u])
                    q0 = idx * 128
                    sy.dma_start(
                        out=OSs[h][4 * q0:4 * q0 + 512, :].rearrange(
                            "(q j) e -> q j e", j=4)[:, 0, :],
                        in_=ostgp)
                hist["sct"][u] = p.snap()
                if safe:
                    p.bar()

            u = 0
            for h in range(2):
                for idx in range(nown):
                    attn_unit(u, h, "own", idx)
                    u += 1
                for idx in range(nsel):
                    attn_unit(u, h, "sel", idx)
                    u += 1
            p.bar()

            for h in range(2 if debug != 2 else 0):
                for qt in range(64):
                    sy.dma_start(out=comb[:].rearrange("q (j e) -> q j e", e=129),
                                 in_=OSs[h][qt * 512:(qt + 1) * 512, :].rearrange(
                                     "(q j) e -> q j e", j=4))
                    p.bar()
                    ve.tensor_add(s01[:], comb[:, 0:129], comb[:, 129:258])
                    p.bar()
                    ve.tensor_add(s23[:], comb[:, 258:387], comb[:, 387:516])
                    p.bar()
                    ve.tensor_add(osum[:], s01[:], s23[:])
                    p.bar()
                    ve.reciprocal(rz[:], osum[:, 128:129])
                    p.bar()
                    ve.tensor_scalar(ofin[:], osum[:, 0:128], rz[:], None,
                                     op0=ALU.mult)
                    p.bar()
                    te.transpose(bqf0[:], ofin[:], tb["eye_bf"][:])
                    p.bar()
                    ve.tensor_copy(otev[:], bqf0[:])
                    p.bar()
                    sy.dma_start(out=OT[h * 128:(h + 1) * 128,
                                        qt * 128:(qt + 1) * 128], in_=otev[:])
                    p.bar()
        p.finish()
        p.replay(block)
    return nc


def stage_l2(inputs, xnt_full):
    t = host_tables()
    wq = np.asarray(inputs["L0_attn_wq"], np.float32)
    wk = np.asarray(inputs["L0_attn_wk"], np.float32)
    wv = np.asarray(inputs["L0_attn_wv"], np.float32)
    maps = []
    for c in range(NCORES):
        m = {
            "xnt_full": xnt_full,
            "wq2": np.ascontiguousarray(wq[:, 2 * c * 128:(2 * c + 2) * 128]),
            "wk1": np.ascontiguousarray(wk[:, (c // 2) * 128:(c // 2 + 1) * 128]),
            "wv1": np.ascontiguousarray(wv[:, (c // 2) * 128:(c // 2 + 1) * 128]),
            "cosT": t["cosT"], "sinT": t["sinT"],
            "cosTu": t["cosTu"], "sinTu": t["sinTu"],
        }
        for nm, _, _ in TABLE_SPECS:
            m[nm] = t[nm]
        maps.append(m)
    return maps


def kernel(**inputs):
    raise NotImplementedError



def build_l3():
    nc = bass.Bass("TRN2", target_bir_lowering=False)
    X = nc.dram_tensor("x_rows", [TPC, D], F32, kind="ExternalInput").ap()
    XNS = nc.dram_tensor("xn_sl", [TPC, D], BF16, kind="ExternalInput").ap()
    OTF = nc.dram_tensor("ot_full", [D, TPC], BF16, kind="ExternalInput").ap()
    WO = nc.dram_tensor("wo", [D, D], F32, kind="ExternalInput").ap()
    W1 = nc.dram_tensor("w1", [DFF, D], F32, kind="ExternalInput").ap()
    B1 = nc.dram_tensor("b1", [1, DFF], F32, kind="ExternalInput").ap()
    W2 = nc.dram_tensor("w2", [D, DFF], F32, kind="ExternalInput").ap()
    B2 = nc.dram_tensor("b2", [1, D], F32, kind="ExternalInput").ap()
    LNW = nc.dram_tensor("lnw", [1, D], F32, kind="ExternalInput").ap()
    LNB = nc.dram_tensor("lnb", [1, D], F32, kind="ExternalInput").ap()
    EYE = nc.dram_tensor("eye_bf", [128, 128], BF16, kind="ExternalInput").ap()
    OUT = nc.dram_tensor("out_rows", [TPC, D], F32, kind="ExternalOutput").ap()

    W1QT = nc.dram_tensor("w1qt", [D, DFF], BF16, kind="Internal").ap()
    W2QT = nc.dram_tensor("w2qt", [DFF, D], BF16, kind="Internal").ap()
    HQT = nc.dram_tensor("hqt", [D, TPC], BF16, kind="Internal").ap()
    Y1QT = nc.dram_tensor("y1qt", [DFF, TPC], BF16, kind="Internal").ap()
    Y2 = nc.dram_tensor("y2", [TPC, D], F32, kind="Internal").ap()

    es = contextlib.ExitStack()

    def sb(name, shape, dt):
        return es.enter_context(nc.sbuf_tensor(name, shape, dt))

    with es:
        eye = sb("eye", [128, 128], BF16)
        wld = sb("wld", [128, 4096], F32)
        qscr = sb("qscr", [128, 4096], F32)
        amx = sb("amx", [128, 128], F32)
        amx2 = sb("amx2", [128, 128], F32)
        qout = sb("qout", [128, 4096], BF16)
        tstage = sb("tstage", [128, 4096], BF16)
        hqtile = sb("hqtile", [128, 2048], BF16)
        xf = sb("xf", [128, D], F32)
        stat = sb("stat", [128, 4], F32)
        wbc = sb("wbc", [128, D], F32)
        bbc = sb("bbc", [128, D], F32)
        b1bc = sb("b1bc", [128, 2048], F32)
        b2bc = sb("b2bc", [128, D], F32)
        hb = sb("hb", [128, D], BF16)
        wbig = sb("wbig", [128, 16 * 1024], BF16)
        tb16 = sb("tb16", [128, 512], BF16)
        g1 = sb("g1", [128, 512], F32)
        g2 = sb("g2", [128, 512], F32)
        g3 = sb("g3", [128, 512], F32)
        y1tile = sb("y1tile", [128, 64 * 128], BF16)
        xt = sb("xt", [128, D], F32)
        y2t = sb("y2t", [128, D], F32)
        fft = sb("fft", [128, D], BF16)
        ot16 = sb("ot16", [128, 16 * 128], BF16)
        outt = sb("outt", [128, D], F32)

        block = es.enter_context(nc.Block())
        p = Prog(nc)
        sy, te, ve, sc, gp = (p.on("sync"), p.on("tensor"), p.on("vector"),
                              p.on("scalar"), p.on("gpsimd"))

        def quant(src, wdt, dst):
            nw = wdt // 32
            src3 = src.rearrange("p (w k) -> p w k", k=32)
            ve.tensor_reduce(amx[:, 0:nw], src3, axis=AX.X, op=ALU.max,
                             apply_absolute_value=True)
            p.bar()
            ve.tensor_scalar_max(amx[:, 0:nw], amx[:, 0:nw], 1e-30)
            p.bar()
            ami = amx[:, 0:nw].bitcast(I32)
            ve.tensor_scalar(ami, ami, 0x7F800000, None, op0=ALU.bitwise_and)
            p.bar()
            am2i = amx2[:, 0:nw].bitcast(I32)
            ve.tensor_scalar(am2i, ami, -1, 2181038080, op0=ALU.mult, op1=ALU.add)
            p.bar()
            ve.tensor_scalar(ami, ami, 50331648, None, op0=ALU.subtract)
            p.bar()
            inv3 = amx2[:, 0:nw].unsqueeze(2).to_broadcast([128, nw, 32])
            scl3 = amx[:, 0:nw].unsqueeze(2).to_broadcast([128, nw, 32])
            t3 = qscr[:, 0:wdt].rearrange("p (w k) -> p w k", k=32)
            ve.tensor_mul(t3, src3, inv3)
            p.bar()
            ve.tensor_scalar(qscr[:, 0:wdt], qscr[:, 0:wdt], MAGIC, None, op0=ALU.add)
            p.bar()
            ve.tensor_scalar(qscr[:, 0:wdt], qscr[:, 0:wdt], MAGIC, None,
                             op0=ALU.subtract)
            p.bar()
            ve.tensor_scalar(qscr[:, 0:wdt], qscr[:, 0:wdt], 127.0, -128.0,
                             op0=ALU.min, op1=ALU.max)
            p.bar()
            ve.tensor_mul(dst.rearrange("p (w k) -> p w k", k=32), t3, scl3)
            p.bar()

        def trans_block(src_bf, n128, dststage):
            for j in range(0, n128, 2):
                te.transpose(tp0[:], src_bf[:, j * 128:(j + 1) * 128], eye[:])
                if j + 1 < n128:
                    te.transpose(tp1[:], src_bf[:, (j + 1) * 128:(j + 2) * 128],
                                 eye[:])
                p.bar()
                ve.tensor_copy(dststage[:, j * 128:(j + 1) * 128], tp0[:])
                if j + 1 < n128:
                    ve.tensor_copy(dststage[:, (j + 1) * 128:(j + 2) * 128], tp1[:])
                p.bar()

        sy.dma_start(out=eye[:], in_=EYE)
        sy.dma_start(out=wbc[:], in_=LNW.to_broadcast([128, D]))
        sy.dma_start(out=bbc[:], in_=LNB.to_broadcast([128, D]))
        sy.dma_start(out=b2bc[:], in_=B2.to_broadcast([128, D]))
        p.bar()

        with nc.psum_tensor("tp0", [128, 128], BF16) as tp0, \
             nc.psum_tensor("tp1", [128, 128], BF16) as tp1, \
             nc.psum_tensor("mm0", [128, 512], F32) as mm0, \
             nc.psum_tensor("mm1", [128, 512], F32) as mm1:
            mms = [mm0, mm1]

            for i in range(64):
                sy.dma_start(out=wld[:, 0:D], in_=W1[i * 128:(i + 1) * 128, :])
                p.bar()
                quant(wld[:, 0:D], D, qout[:, 0:D])
                trans_block(qout[:, 0:D], 16, tstage)
                sy.dma_start(
                    out=W1QT[:, i * 128:(i + 1) * 128].rearrange(
                        "(c p) f -> p c f", p=128),
                    in_=tstage[:, 0:D].rearrange("p (c f) -> p c f", f=128))
                p.bar()
            for i in range(16):
                for hf in range(2):
                    sy.dma_start(out=wld[:],
                                 in_=W2[i * 128:(i + 1) * 128,
                                        hf * 4096:(hf + 1) * 4096])
                    p.bar()
                    quant(wld[:], 4096, qout[:])
                    trans_block(qout[:], 32, tstage)
                    sy.dma_start(
                        out=W2QT[hf * 4096:(hf + 1) * 4096,
                                 i * 128:(i + 1) * 128].rearrange(
                            "(c p) f -> p c f", p=128),
                        in_=tstage[:].rearrange("p (c f) -> p c f", f=128))
                    p.bar()

            for i in range(NTT):
                sy.dma_start(out=hb[:], in_=XNS[i * 128:(i + 1) * 128, :])
                p.bar()
                ve.tensor_copy(xf[:], hb[:])
                p.bar()
                ve.reduce_sum(stat[:, 0:1], xf[:], axis=AX.X)
                p.bar()
                ve.tensor_scalar(stat[:, 0:1], stat[:, 0:1], 1.0 / D, None,
                                 op0=ALU.mult)
                p.bar()
                ve.tensor_scalar(xf[:], xf[:], stat[:, 0:1], None, op0=ALU.subtract)
                p.bar()
                sc.activation(out=qscr[:, 0:D], in_=xf[:], func=AF.Square)
                p.bar()
                ve.reduce_sum(stat[:, 1:2], qscr[:, 0:D], axis=AX.X)
                p.bar()
                ve.tensor_scalar(stat[:, 1:2], stat[:, 1:2], 1.0 / D, 1e-5,
                                 op0=ALU.mult, op1=ALU.add)
                p.bar()
                sc.activation(out=stat[:, 2:3], in_=stat[:, 1:2], func=AF.Sqrt)
                p.bar()
                ve.reciprocal(stat[:, 3:4], stat[:, 2:3])
                p.bar()
                ve.tensor_scalar(xf[:], xf[:], stat[:, 3:4], None, op0=ALU.mult)
                p.bar()
                ve.tensor_mul(xf[:], xf[:], wbc[:])
                p.bar()
                ve.tensor_add(hb[:], xf[:], bbc[:])
                p.bar()
                quant(hb[:], D, hqtile[:])
                trans_block(hqtile[:], 16, tstage)
                sy.dma_start(
                    out=HQT[:, i * 128:(i + 1) * 128].rearrange(
                        "(c p) f -> p c f", p=128),
                    in_=tstage[:, 0:D].rearrange("p (c f) -> p c f", f=128))
                p.bar()

            for e8 in range(8):
                sy.dma_start(out=wbig[:].rearrange("p (c f) -> p c f", f=1024),
                             in_=W1QT[:, e8 * 1024:(e8 + 1) * 1024].rearrange(
                                 "(c p) f -> p c f", p=128))
                sc.dma_start(out=b1bc[:, 0:1024],
                             in_=B1[:, e8 * 1024:(e8 + 1) * 1024]
                             .to_broadcast([128, 1024]))
                p.bar()
                for i in range(NTT):
                    sy.dma_start(out=hqtile[:].rearrange("p (c f) -> p c f", f=128),
                                 in_=HQT[:, i * 128:(i + 1) * 128].rearrange(
                                     "(c p) f -> p c f", p=128))
                    p.bar()
                    for sub in range(2):
                        mm = mms[sub % 2]
                        for j in range(16):
                            te.matmul(mm[:],
                                      hqtile[:, j * 128:(j + 1) * 128],
                                      wbig[:, j * 1024 + sub * 512:
                                           j * 1024 + sub * 512 + 512],
                                      start=(j == 0), stop=(j == 15))
                        p.bar()
                        bsl = b1bc[:, sub * 512:(sub + 1) * 512]
                        ve.tensor_add(tb16[:], mm[:], bsl)
                        p.bar()
                        gp.tensor_mul(g1[:], tb16[:], tb16[:])
                        p.bar()
                        ve.tensor_scalar(g1[:], g1[:], 0.044715, 1.0,
                                         op0=ALU.mult, op1=ALU.add)
                        p.bar()
                        gp.tensor_mul(g2[:], tb16[:], g1[:])
                        p.bar()
                        sc.activation(out=g3[:], in_=g2[:], func=AF.Tanh,
                                      scale=GELU_C)
                        p.bar()
                        ve.tensor_scalar(g3[:], g3[:], 1.0, 0.5,
                                         op0=ALU.add, op1=ALU.mult)
                        p.bar()
                        ve.tensor_mul(tb16[:], tb16[:], g3[:])
                        p.bar()
                        quant(tb16[:], 512, qout[:, 0:512])
                        trans_block(qout[:, 0:512], 4, tstage)
                        sy.dma_start(
                            out=Y1QT[e8 * 1024 + sub * 512:
                                     e8 * 1024 + (sub + 1) * 512,
                                     i * 128:(i + 1) * 128].rearrange(
                                "(b pp) f -> pp b f", pp=128),
                            in_=tstage[:, 0:512].rearrange("p (b f) -> p b f", f=128))
                        p.bar()

            for e2 in range(8):
                sy.dma_start(out=wbig[:].rearrange("p (c f) -> p c f", f=256),
                             in_=W2QT[:, e2 * 256:(e2 + 1) * 256].rearrange(
                                 "(c p) f -> p c f", p=128))
                p.bar()
                for i in range(NTT):
                    sy.dma_start(out=y1tile[:].rearrange("p (c f) -> p c f", f=128),
                                 in_=Y1QT[:, i * 128:(i + 1) * 128].rearrange(
                                     "(c p) f -> p c f", p=128))
                    p.bar()
                    for j in range(64):
                        te.matmul(mms[i % 2][:, 0:256],
                                  y1tile[:, j * 128:(j + 1) * 128],
                                  wbig[:, j * 256:(j + 1) * 256],
                                  start=(j == 0), stop=(j == 63))
                    p.bar()
                    ve.tensor_copy(g1[:, 0:256], mms[i % 2][:, 0:256])
                    p.bar()
                    sy.dma_start(out=Y2[i * 128:(i + 1) * 128,
                                        e2 * 256:(e2 + 1) * 256], in_=g1[:, 0:256])
                    p.bar()

            for half in range(2):
                for cch in range(16):
                    sy.dma_start(out=wld[:, 0:1024],
                                 in_=WO[cch * 128:(cch + 1) * 128,
                                        half * 1024:(half + 1) * 1024])
                    p.bar()
                    ve.tensor_copy(wbig[:, cch * 1024:(cch + 1) * 1024],
                                   wld[:, 0:1024])
                    p.bar()
                for i in range(NTT):
                    sy.dma_start(out=ot16[:].rearrange("p (c f) -> p c f", f=128),
                                 in_=OTF[:, i * 128:(i + 1) * 128].rearrange(
                                     "(c p) f -> p c f", p=128))
                    sc.dma_start(out=xt[:], in_=X[i * 128:(i + 1) * 128, :])
                    sc.dma_start(out=y2t[:], in_=Y2[i * 128:(i + 1) * 128, :])
                    sc.dma_start(out=hb[:], in_=XNS[i * 128:(i + 1) * 128, :])
                    p.bar()
                    ve.tensor_add(fft[:], y2t[:], b2bc[:])
                    p.bar()
                    ve.tensor_add(fft[:], fft[:], hb[:])
                    p.bar()
                    for sub in range(2):
                        mm = mms[sub % 2]
                        for j in range(16):
                            te.matmul(mm[:], ot16[:, j * 128:(j + 1) * 128],
                                      wbig[:, j * 1024 + sub * 512:
                                           j * 1024 + sub * 512 + 512],
                                      start=(j == 0), stop=(j == 15))
                        p.bar()
                        sl = slice(half * 1024 + sub * 512,
                                   half * 1024 + (sub + 1) * 512)
                        ve.tensor_add(outt[:, sl], mm[:], xt[:, sl])
                        p.bar()
                        ve.tensor_add(outt[:, sl], outt[:, sl], fft[:, sl])
                        p.bar()
                    sy.dma_start(out=OUT[i * 128:(i + 1) * 128,
                                         half * 1024:(half + 1) * 1024],
                                 in_=outt[:, half * 1024:(half + 1) * 1024])
                    p.bar()
        p.finish()
        p.replay(block)
    return nc


def stage_l3(inputs, xn_full, ot_full):
    t = host_tables()
    x = np.asarray(inputs["L0_attn_hidden_states"]).reshape(S, D).astype(np.float32)
    maps = []
    for c in range(NCORES):
        maps.append({
            "x_rows": np.ascontiguousarray(x[c * TPC:(c + 1) * TPC]),
            "xn_sl": np.ascontiguousarray(xn_full[c * TPC:(c + 1) * TPC]),
            "ot_full": np.ascontiguousarray(ot_full[:, c * TPC:(c + 1) * TPC]),
            "wo": np.asarray(inputs["L0_attn_wo"], np.float32),
            "w1": np.asarray(inputs["L0_ffn_W1"], np.float32),
            "b1": np.asarray(inputs["L0_ffn_b1"], np.float32).reshape(1, DFF),
            "w2": np.asarray(inputs["L0_ffn_W2"], np.float32),
            "b2": np.asarray(inputs["L0_ffn_b2"], np.float32).reshape(1, D),
            "lnw": np.asarray(inputs["L0_ffn_ln_w"], np.float32).reshape(1, D),
            "lnb": np.asarray(inputs["L0_ffn_ln_b"], np.float32).reshape(1, D),
            "eye_bf": t["eye_bf"],
        })
    return maps


def kernel(**inputs):
    nc1 = build_l1()
    res1 = run_bass_kernel_spmd(nc1, stage_l1(inputs), core_ids=list(range(NCORES)))
    xn_full = np.concatenate([res1.results[c]["xn_sl"] for c in range(NCORES)], axis=0)
    xnt_full = np.concatenate([res1.results[c]["xnt_sl"] for c in range(NCORES)],
                              axis=1)
    nc2 = build_l2(safe=True)
    res2 = run_bass_kernel_spmd(nc2, stage_l2(inputs, xnt_full),
                                core_ids=list(range(NCORES)))
    ot_full = np.concatenate([res2.results[c]["ot_sl"] for c in range(NCORES)],
                             axis=0)
    nc3 = build_l3()
    res3 = run_bass_kernel_spmd(nc3, stage_l3(inputs, xn_full, ot_full),
                                core_ids=list(range(NCORES)))
    out = np.concatenate([res3.results[c]["out_rows"] for c in range(NCORES)],
                         axis=0)
    return out.reshape(1, S, D).astype(np.float32)


# version 02
# speedup vs baseline: 1.0808x; 1.0808x over previous
import contextlib
import numpy as np
import ml_dtypes
import concourse.bass as bass
import concourse.mybir as mybir
from concourse.bass_utils import run_bass_kernel_spmd

F32 = mybir.dt.float32
BF16 = mybir.dt.bfloat16
I32 = mybir.dt.int32
I16 = mybir.dt.int16
AF = mybir.ActivationFunctionType
ALU = mybir.AluOpType
AX = mybir.AxisListType

BF = ml_dtypes.bfloat16

S, D, HD, NH, NKV, CHK, NCHK = 8192, 2048, 128, 16, 4, 256, 32
NCORES = 8
TPC = S // NCORES
NTT = TPC // 128
DFF = 8192
NT = 224
MAGIC = 12582912.0
SCALE_QK = float(HD) ** -0.5
GELU_C = 0.7978845608028654
NEG = -1.0e30


def host_tables():
    t = {}
    t["eye_bf"] = np.eye(128).astype(BF)
    t["eye_f32"] = np.eye(128).astype(np.float32)
    t["lt_bf"] = (np.arange(128)[:, None] < np.arange(128)[None, :]).astype(BF)
    t["ut32_f32"] = (np.arange(32)[:, None] < np.arange(32)[None, :]).astype(np.float32)
    t["eye32_f32"] = np.eye(32).astype(np.float32)
    t["ones_col_bf"] = np.ones((128, 1), BF)
    t["ones_1x128_bf"] = np.ones((1, 128), BF)
    t["ones_1x128_f32"] = np.ones((1, 128), np.float32)
    t["ones_32c_f32"] = np.ones((32, 1), np.float32)
    t["ones_1x1_f32"] = np.ones((1, 1), np.float32)
    t["iota32"] = np.broadcast_to(np.arange(32, dtype=np.float32), (128, 32)).copy()
    t["iota128c"] = np.arange(128, dtype=np.float32).reshape(128, 1)
    col = np.arange(256)
    keyloc = (2 * (col % 128) + col // 128).astype(BF)
    t["keyloc_bf"] = np.broadcast_to(keyloc, (128, 256)).copy()
    t["cval"] = np.broadcast_to(np.arange(40, dtype=np.float32), (128, 40)).copy()
    t["bnd32"] = np.broadcast_to(128.0 * np.arange(NT, dtype=np.float32),
                                 (32, NT)).copy()
    sv = np.zeros((128, 192), np.float32)
    for qt in range(64):
        for j in (1, 2, 3):
            sv[:, qt * 3 + j - 1] = 4 * (qt * 128 + np.arange(128)) + j
    t["svaltab"] = sv
    t["qloc0"] = np.arange(128, dtype=np.float32).reshape(128, 1)
    t["qloc1"] = (np.arange(128, dtype=np.float32) + 128).reshape(128, 1)
    pos = np.arange(S, dtype=np.float64)
    inv = 1.0 / (10000.0 ** (np.arange(0, HD, 2, dtype=np.float64) / HD))
    ang = pos[:, None] * inv[None, :]
    t["cosT"] = (np.cos(ang) * SCALE_QK).astype(np.float32)
    t["sinT"] = (np.sin(ang) * SCALE_QK).astype(np.float32)
    t["cosTu"] = np.cos(ang).astype(np.float32)
    t["sinTu"] = np.sin(ang).astype(np.float32)
    return t


class Prog:

    ENGS = ("sync", "tensor", "vector", "scalar", "gpsimd")

    def __init__(self, nc):
        self.nc = nc
        self._sem_cm = nc.semaphore("bar")
        self.sem = self._sem_cm.__enter__()
        self.count = 0
        self.ops = {e: [] for e in self.ENGS}
        self.pending = {}
        self.wait_at = {e: None for e in self.ENGS}

    class _H:
        def __init__(self, p, name):
            object.__setattr__(self, "p", p)
            object.__setattr__(self, "name", name)

        def __getattr__(self, attr):
            p, name = self.p, self.name

            def wrapped(*a, **kw):
                if p.wait_at[name] is not None:
                    p.ops[name].append(("wait", p.wait_at[name]))
                    p.wait_at[name] = None
                isdma = attr in ("dma_start", "indirect_dma_start")
                p.ops[name].append(["op", attr, a, kw, 16 if isdma else 0])
                if isdma:
                    p.count += 16
                else:
                    p.pending[name] = len(p.ops[name]) - 1
            return wrapped

    def on(self, name):
        return Prog._H(self, name)

    def bar(self):
        for name, idx in self.pending.items():
            self.ops[name][idx][4] += 1
            self.count += 1
        self.pending = {}
        for e in self.ENGS:
            self.wait_at[e] = self.count

    def snap(self):
        for name, idx in list(self.pending.items()):
            self.ops[name][idx][4] += 1
            self.count += 1
        self.pending = {}
        return self.count

    def wait(self, eng, val):
        self.ops[eng].append(("wait", val))
        if self.wait_at[eng] is not None and self.wait_at[eng] <= val:
            self.wait_at[eng] = None

    def finish(self, eng="sync"):
        self.bar()
        self.ops[eng].append(("wait", self.count))
        self.wait_at[eng] = None

    def replay(self, block):
        def mk(name):
            def run(handle):
                eng = getattr(self.nc, name)
                for op in self.ops[name]:
                    if op[0] == "wait":
                        eng.wait_ge(self.sem, op[1])
                    else:
                        _, attr, a, kw, inc = op
                        ins = getattr(eng, attr)(*a, **kw)
                        if inc:
                            ins.then_inc(self.sem, inc)
            return run
        block.sync(mk("sync"))
        block.tensor(mk("tensor"))
        block.vector(mk("vector"))
        block.scalar(mk("scalar"))
        block.gpsimd(mk("gpsimd"))



def build_l1():
    nc = bass.Bass("TRN2", target_bir_lowering=False)
    X = nc.dram_tensor("x_rows", [TPC, D], F32, kind="ExternalInput").ap()
    EYE = nc.dram_tensor("eye_bf", [128, 128], BF16, kind="ExternalInput").ap()
    XN = nc.dram_tensor("xn_sl", [TPC, D], BF16, kind="ExternalOutput").ap()
    XNT = nc.dram_tensor("xnt_sl", [D, TPC], BF16, kind="ExternalOutput").ap()

    with (
        nc.sbuf_tensor("xt", [128, D], F32) as xt,
        nc.sbuf_tensor("scr", [128, D], F32) as scr,
        nc.sbuf_tensor("stat", [128, 4], F32) as stat,
        nc.sbuf_tensor("xnb", [128, D], BF16) as xnb,
        nc.sbuf_tensor("eye", [128, 128], BF16) as eye,
        nc.sbuf_tensor("tev", [128, 2048], BF16) as tev,
        nc.psum_tensor("pt0", [128, 128], BF16) as pt0,
        nc.psum_tensor("pt1", [128, 128], BF16) as pt1,
        nc.Block() as block,
    ):
        p = Prog(nc)
        sy, ve, sc, te = p.on("sync"), p.on("vector"), p.on("scalar"), p.on("tensor")
        sy.dma_start(out=eye[:], in_=EYE)
        p.bar()
        for i in range(NTT):
            sy.dma_start(out=xt[:], in_=X[i * 128:(i + 1) * 128, :])
            p.bar()
            ve.reduce_sum(stat[:, 0:1], xt[:], axis=AX.X)
            p.bar()
            ve.tensor_scalar(stat[:, 0:1], stat[:, 0:1], 1.0 / D, None, op0=ALU.mult)
            p.bar()
            ve.tensor_scalar(scr[:], xt[:], stat[:, 0:1], None, op0=ALU.subtract)
            p.bar()
            sc.activation(out=xt[:], in_=scr[:], func=AF.Square)
            p.bar()
            ve.reduce_sum(stat[:, 1:2], xt[:], axis=AX.X)
            p.bar()
            ve.tensor_scalar(stat[:, 1:2], stat[:, 1:2], 1.0 / D, 1e-5,
                             op0=ALU.mult, op1=ALU.add)
            p.bar()
            sc.activation(out=stat[:, 2:3], in_=stat[:, 1:2], func=AF.Sqrt)
            p.bar()
            ve.reciprocal(stat[:, 3:4], stat[:, 2:3])
            p.bar()
            ve.tensor_scalar(xnb[:], scr[:], stat[:, 3:4], None, op0=ALU.mult)
            p.bar()
            sy.dma_start(out=XN[i * 128:(i + 1) * 128, :], in_=xnb[:])
            for j in range(0, 16, 2):
                te.transpose(pt0[:], xnb[:, j * 128:(j + 1) * 128], eye[:])
                te.transpose(pt1[:], xnb[:, (j + 1) * 128:(j + 2) * 128], eye[:])
                p.bar()
                ve.tensor_copy(tev[:, j * 128:(j + 1) * 128], pt0[:])
                ve.tensor_copy(tev[:, (j + 1) * 128:(j + 2) * 128], pt1[:])
                p.bar()
            sc.dma_start(
                out=XNT[:, i * 128:(i + 1) * 128].rearrange("(c p) f -> p c f", p=128),
                in_=tev[:].rearrange("p (c f) -> p c f", f=128))
            p.bar()
        p.finish()
        p.replay(block)
    return nc


def stage_l1(inputs):
    t = host_tables()
    x = np.asarray(inputs["L0_attn_hidden_states"]).reshape(S, D)
    return [{
        "x_rows": np.ascontiguousarray(x[c * TPC:(c + 1) * TPC]).astype(np.float32),
        "eye_bf": t["eye_bf"],
    } for c in range(NCORES)]



TABLE_SPECS = [
    ("eye_bf", [128, 128], BF16), ("eye_f32", [128, 128], F32),
    ("lt_bf", [128, 128], BF16),
    ("ut32_f32", [32, 32], F32), ("eye32_f32", [32, 32], F32),
    ("ones_col_bf", [128, 1], BF16), ("ones_1x128_bf", [1, 128], BF16),
    ("ones_1x128_f32", [1, 128], F32), ("ones_32c_f32", [32, 1], F32),
    ("ones_1x1_f32", [1, 1], F32),
    ("iota32", [128, 32], F32), ("iota128c", [128, 1], F32),
    ("keyloc_bf", [128, 256], BF16), ("cval", [128, 40], F32),
    ("bnd32", [32, NT], F32), ("qloc0", [128, 1], F32), ("qloc1", [128, 1], F32),
    ("svaltab", [128, 192], F32),
]


def build_l2(debug=0, nown=64, nsel=NT, cut=99, safe=False):
    nc = bass.Bass("TRN2", target_bir_lowering=False)
    XNT = nc.dram_tensor("xnt_full", [D, S], BF16, kind="ExternalInput").ap()
    WQ = nc.dram_tensor("wq2", [D, 256], F32, kind="ExternalInput").ap()
    WK = nc.dram_tensor("wk1", [D, 128], F32, kind="ExternalInput").ap()
    WV = nc.dram_tensor("wv1", [D, 128], F32, kind="ExternalInput").ap()
    COST = nc.dram_tensor("cosT", [S, 64], F32, kind="ExternalInput").ap()
    SINT = nc.dram_tensor("sinT", [S, 64], F32, kind="ExternalInput").ap()
    COSU = nc.dram_tensor("cosTu", [S, 64], F32, kind="ExternalInput").ap()
    SINU = nc.dram_tensor("sinTu", [S, 64], F32, kind="ExternalInput").ap()
    tby = {nm: nc.dram_tensor(nm, shp, dt, kind="ExternalInput").ap()
           for nm, shp, dt in TABLE_SPECS}

    OT = nc.dram_tensor("ot_sl", [256, S], BF16, kind="ExternalOutput").ap()

    ik = "ExternalOutput" if debug else "Internal"
    QDs = [nc.dram_tensor(f"qd{h}", [S, 128], BF16, kind=ik).ap()
           for h in range(2)]
    KBL = nc.dram_tensor("kbl", [NCHK * 128, 256], BF16, kind=ik).ap()
    VP = nc.dram_tensor("vp", [NCHK * 128, 258], BF16, kind=ik).ap()
    LISTs = [nc.dram_tensor(f"list{h}", [NT * 128, 1], I16, kind=ik).ap()
             for h in range(2)]
    OSs = [nc.dram_tensor(f"os{h}", [4 * S, 129], F32,
                          kind=("ExternalOutput" if debug == 2 else "Internal")).ap()
           for h in range(2)]
    if debug:
        DBG = nc.dram_tensor("dbg", [128, 6 * NT], F32, kind="ExternalOutput").ap()
        DBGK = nc.dram_tensor("dbgk", [128, 32], F32, kind="ExternalOutput").ap()

    es = contextlib.ExitStack()

    def sb(name, shape, dt):
        return es.enter_context(nc.sbuf_tensor(name, shape, dt))

    with es:
        tb = {nm: sb("t_" + nm, shp, dt) for nm, shp, dt in TABLE_SPECS}
        wqb = sb("wqb", [128, 16 * 256], BF16)
        wkb = sb("wkb", [128, 16 * 128], BF16)
        wvb = sb("wvb", [128, 16 * 128], BF16)
        wstg = sb("wstg", [128, 16 * 256], F32)
        xnt_t = sb("xnt_t", [128, 2048], BF16)
        rop = sb("rop", [128, 128], F32)
        sc1 = sb("sc1", [128, 128], F32)
        sc2 = sb("sc2", [128, 128], F32)
        ctt = sb("ctt", [128, 64], F32)
        stt_ = sb("stt_", [128, 64], F32)
        cttu = sb("cttu", [128, 64], F32)
        sttu = sb("sttu", [128, 64], F32)
        ktokb = sb("ktokb", [128, 128], BF16)
        kfev = sb("kfev", [128, 128], BF16)
        kbev = sb("kbev", [128, 128], BF16)
        vev = sb("vev", [128, 129], BF16)
        qsbs = [sb(f"qsb{h}", [128, 64 * 128], BF16) for h in range(2)]
        kmacc = sb("kmacc", [128, 64], F32)
        kmean = sb("kmean", [128, 32], BF16)
        qft = sb("qft", [128, 128], BF16)
        gg = sb("gg", [128, 32], F32)
        tmask = sb("tmask", [128, 32], F32)
        m8 = sb("m8", [128, 8], F32)
        hraw = sb("hraw", [128, 32], F32)
        escr = sb("escr", [128, 32], F32)
        idxc = sb("idxc", [128, 1], F32)
        isbad = sb("isbad", [128, 1], I32)
        hcb = sb("hcb", [128, 3 * 32], BF16)
        hall = sb("hall", [128, 32], BF16)
        ssum = sb("ssum", [128, 32], F32)
        csrow = sb("csrow", [1, 32], BF16)
        csevf = sb("csevf", [1, 32], F32)
        ctabs = [sb(f"ctab{h}", [128, 192], F32) for h in range(2)]
        prels = [sb(f"prel{h}", [128, 192], F32) for h in range(2)]
        totf = sb("totf", [1, 32], F32)
        ceilr = sb("ceilr", [1, 32], F32)
        ceilc = sb("ceilc", [32, 1], F32)
        basec = sb("basec", [32, 1], F32)
        baser = sb("baser", [1, 32], F32)
        basebc = sb("basebc", [128, 32], F32)
        cmp32 = sb("cmp32", [32, NT], F32)
        tcrow = sb("tcrow", [1, NT], F32)
        tcb = sb("tcb", [128, NT], F32)
        hcf = sb("hcf", [128, 32], F32)
        pcol = sb("pcol", [128, 1], F32)
        posi = sb("posi", [128, 1], I32)
        svali = sb("svali", [128, 1], I16)
        svalf = sb("svalf", [128, 1], F32)
        zro16 = sb("zro16", [128, NT], I16)
        vtabs = [sb(f"vtab{h}", [128, NT], F32) for h in range(2)]
        qposs = [sb(f"qpos{h}", [128, NT], F32) for h in range(2)]
        gidxs = [sb(f"gidx{h}", [128, NT], I32) for h in range(2)]
        sidxs = [sb(f"sidx{h}", [128, NT], I32) for h in range(2)]
        kvidxs = [sb(f"kvidx{h}", [128, NT], I32) for h in range(2)]
        qrels = [sb(f"qrel{h}", [128, NT], F32) for h in range(2)]
        lraw = sb("lraw", [128, NT], I16)
        hcall = sb("hcall", [128, 192 * 32], F32)
        posi192 = sb("posi192", [128, 192], I32)
        svali192 = sb("svali192", [128, 192], I16)
        lrf = sb("lrf", [128, NT], F32)
        dscr = sb("dscr", [128, NT], F32)
        dscr2 = sb("dscr2", [128, NT], F32)
        qg = sb("qg", [128, 2 * 128], BF16)
        kst = sb("kst", [128, 2 * 256], BF16)
        vst = sb("vst", [128, 2 * 258], BF16)
        qftl = sb("qftl", [128, 2 * 128], BF16)
        expt = sb("expt", [128, 2 * 256], BF16)
        probs = sb("probs", [128, 2 * 256], BF16)
        ptev = sb("ptev", [128, 2 * 256], BF16)
        ostg = sb("ostg", [128, 2 * 129], F32)
        comb = sb("comb", [128, 4 * 129], F32)
        osum = sb("osum", [128, 129], F32)
        s01 = sb("s01", [128, 129], F32)
        s23 = sb("s23", [128, 129], F32)
        rz = sb("rz", [128, 1], F32)
        ofin = sb("ofin", [128, 128], BF16)
        otev = sb("otev", [128, 128], BF16)

        block = es.enter_context(nc.Block())
        p = Prog(nc)
        sy, te, ve, sc, gp = (p.on("sync"), p.on("tensor"), p.on("vector"),
                              p.on("scalar"), p.on("gpsimd"))

        for nm, _, _ in TABLE_SPECS:
            sy.dma_start(out=tb[nm][:], in_=tby[nm])
        sy.dma_start(out=wstg[:].rearrange("p (c f) -> p c f", f=256),
                     in_=WQ.rearrange("(c p) f -> p c f", p=128))
        p.bar()
        ve.tensor_copy(wqb[:], wstg[:])
        p.bar()
        sy.dma_start(out=wstg[:, 0:2048].rearrange("p (c f) -> p c f", f=128),
                     in_=WK.rearrange("(c p) f -> p c f", p=128))
        p.bar()
        ve.tensor_copy(wkb[:], wstg[:, 0:2048])
        p.bar()
        sy.dma_start(out=wstg[:, 0:2048].rearrange("p (c f) -> p c f", f=128),
                     in_=WV.rearrange("(c p) f -> p c f", p=128))
        p.bar()
        ve.tensor_copy(wvb[:], wstg[:, 0:2048])
        ve.tensor_copy(svali192[:], tb["svaltab"][:])
        ve.memset(vev[:, 128:129], 1.0)
        ve.memset(zro16[:], 0)
        ve.memset(kmacc[:], 0.0)
        p.bar()
        for hh in range(2):
            sy.dma_start(out=LISTs[hh].rearrange("(t p) one -> t (p one)", p=128)[0:128, :],
                         in_=zro16[:, 0:128])
            sy.dma_start(out=LISTs[hh].rearrange("(t p) one -> t (p one)", p=128)[128:224, :],
                         in_=zro16[0:96, 0:128])
        p.bar()

        with nc.psum_tensor("bk", [128, 128], F32) as bk, \
             nc.psum_tensor("bv", [128, 128], F32) as bv, \
             nc.psum_tensor("bqa", [128, 128], F32) as bqa, \
             nc.psum_tensor("bqb", [128, 128], F32) as bqb, \
             nc.psum_tensor("bkT", [128, 128], BF16) as bkT:
            for t in range(64):
                sy.dma_start(out=xnt_t[:].rearrange("p (c f) -> p c f", f=128),
                             in_=XNT[:, t * 128:(t + 1) * 128].rearrange(
                                 "(c p) f -> p c f", p=128))
                sc.dma_start(out=ctt[:], in_=COST[t * 128:(t + 1) * 128, :])
                sc.dma_start(out=stt_[:], in_=SINT[t * 128:(t + 1) * 128, :])
                sc.dma_start(out=cttu[:], in_=COSU[t * 128:(t + 1) * 128, :])
                sc.dma_start(out=sttu[:], in_=SINU[t * 128:(t + 1) * 128, :])
                p.bar()
                for j in range(16):
                    st, sp = (j == 0), (j == 15)
                    xb = xnt_t[:, j * 128:(j + 1) * 128]
                    te.matmul(bk[:], xb, wkb[:, j * 128:(j + 1) * 128],
                              start=st, stop=sp)
                    te.matmul(bv[:], xb, wvb[:, j * 128:(j + 1) * 128],
                              start=st, stop=sp)
                    te.matmul(bqa[:], xb, wqb[:, j * 256:j * 256 + 128],
                              start=st, stop=sp)
                    te.matmul(bqb[:], xb, wqb[:, j * 256 + 128:(j + 1) * 256],
                              start=st, stop=sp)
                p.bar()
                ve.tensor_mul(rop[:, 0:64], bk[:, 0:64], cttu[:])
                p.bar()
                ve.tensor_mul(rop[:, 64:128], bk[:, 64:128], cttu[:])
                p.bar()
                ve.tensor_mul(sc1[:, 0:64], bk[:, 64:128], sttu[:])
                p.bar()
                ve.tensor_mul(sc2[:, 0:64], bk[:, 0:64], sttu[:])
                p.bar()
                ve.tensor_sub(rop[:, 0:64], rop[:, 0:64], sc1[:, 0:64])
                p.bar()
                ve.tensor_add(rop[:, 64:128], rop[:, 64:128], sc2[:, 0:64])
                p.bar()
                ve.tensor_copy(ktokb[:], rop[:])
                p.bar()
                te.transpose(bkT[:], ktokb[:], tb["eye_bf"][:])
                p.bar()
                ve.tensor_copy(kfev[:], bkT[:])
                p.bar()
                ve.reduce_sum(kmacc[:, t:t + 1], kfev[:], axis=AX.X)
                ve.tensor_copy(kbev[:].rearrange("d (s k) -> d s k", s=2),
                               kfev[:].rearrange("d (k s) -> d s k", s=2))
                p.bar()
                ch_, hf = t // 2, t % 2
                sy.dma_start(
                    out=KBL[ch_ * 128:(ch_ + 1) * 128, :].rearrange(
                        "d (s k) -> d s k", s=2)[:, :, hf * 64:(hf + 1) * 64],
                    in_=kbev[:].rearrange("d (s k) -> d s k", s=2))
                ve.tensor_copy(vev[:, 0:128], bv[:])
                p.bar()
                sy.dma_start(
                    out=VP[t * 64:(t + 1) * 64, :].rearrange(
                        "r (s c) -> (r s) c", s=2),
                    in_=vev[:])
                for h, bq in ((0, bqa), (1, bqb)):
                    ve.tensor_mul(rop[:, 0:64], bq[:, 0:64], ctt[:])
                    p.bar()
                    ve.tensor_mul(rop[:, 64:128], bq[:, 64:128], ctt[:])
                    p.bar()
                    ve.tensor_mul(sc1[:, 0:64], bq[:, 64:128], stt_[:])
                    p.bar()
                    ve.tensor_mul(sc2[:, 0:64], bq[:, 0:64], stt_[:])
                    p.bar()
                    ve.tensor_sub(rop[:, 0:64], rop[:, 0:64], sc1[:, 0:64])
                    p.bar()
                    ve.tensor_add(rop[:, 64:128], rop[:, 64:128], sc2[:, 0:64])
                    p.bar()
                    ve.tensor_copy(qsbs[h][:, t * 128:(t + 1) * 128], rop[:])
                    p.bar()
                    sy.dma_start(out=QDs[h][t * 128:(t + 1) * 128, :],
                                 in_=qsbs[h][:, t * 128:(t + 1) * 128])
                p.bar()
        ve.tensor_add(dscr[:, 0:32], kmacc[:, 0:64:2], kmacc[:, 1:64:2])
        p.bar()
        ve.tensor_scalar(kmean[:], dscr[:, 0:32], 1.0 / 256, None, op0=ALU.mult)
        p.bar()

        with nc.psum_tensor("pqf", [128, 128], BF16) as pqf, \
             nc.psum_tensor("pg", [128, 32], F32) as pg, \
             nc.psum_tensor("pcs", [1, 32], F32) as pcs, \
             nc.psum_tensor("pr", [128, 32], F32) as pr, \
             nc.psum_tensor("ptot", [128, 32], F32) as ptot, \
             nc.psum_tensor("ptc", [128, NT], F32) as ptc, \
             nc.psum_tensor("ptr", [128, 128], F32) as ptr:
            for h in range(2):
                for qt in range(64):
                    qc = qt // 2
                    te.transpose(pqf[:], qsbs[h][:, qt * 128:(qt + 1) * 128],
                                 tb["eye_bf"][:])
                    p.bar()
                    ve.tensor_copy(qft[:], pqf[:])
                    p.bar()
                    te.matmul(pg[:], qft[:], kmean[:], start=True, stop=True)
                    if qt % 2 == 0:
                        ve.tensor_scalar(tmask[:], tb["iota32"][:], qc - 0.5, NEG,
                                         op0=ALU.is_ge, op1=ALU.mult)
                    p.bar()
                    ve.tensor_add(gg[:], pg[:], tmask[:])
                    p.bar()
                    ve.max(out=m8[:], in_=gg[:])
                    p.bar()
                    for j in (1, 2, 3):
                        cjc = m8[:, j - 1:j]
                        ve.tensor_scalar(hraw[:], gg[:], cjc, None, op0=ALU.is_equal)
                        p.bar()
                        ve.tensor_mul(escr[:], hraw[:], tb["iota32"][:])
                        ve.tensor_scalar(isbad[:], cjc, -1.0e29, None, op0=ALU.is_le)
                        p.bar()
                        ve.reduce_sum(idxc[:], escr[:], axis=AX.X)
                        p.bar()
                        ve.select(idxc[:], isbad[:],
                                  tb["cval"][:, qc + j:qc + j + 1], idxc[:])
                        p.bar()
                        ve.tensor_copy(
                            ctabs[h][:, qt * 3 + j - 1:qt * 3 + j], idxc[:])
                        ve.tensor_scalar(hcb[:, (j - 1) * 32:j * 32],
                                         tb["iota32"][:], idxc[:], None,
                                         op0=ALU.is_equal)
                        p.bar()
                    ve.tensor_add(hall[:], hcb[:, 0:32], hcb[:, 32:64])
                    p.bar()
                    ve.tensor_add(hall[:], hall[:], hcb[:, 64:96])
                    p.bar()
                    te.matmul(pr[:], tb["lt_bf"][:], hall[:], start=True, stop=True)
                    p.bar()
                    ve.tensor_copy(ssum[:], pr[:])
                    p.bar()
                    if qt > 0:
                        ve.tensor_add(ssum[:], ssum[:], ptot[:])
                        p.bar()
                    for j in (1, 2, 3):
                        ve.tensor_mul(hcf[:], hcb[:, (j - 1) * 32:j * 32], ssum[:])
                        p.bar()
                        ve.reduce_sum(prels[h][:, qt * 3 + j - 1:qt * 3 + j],
                                      hcf[:], axis=AX.X)
                        p.bar()
                    for j in (1, 2, 3):
                        te.matmul(pcs[0:1, :], tb["ones_col_bf"][:],
                                  hcb[:, (j - 1) * 32:j * 32],
                                  start=(j == 1), stop=(j == 3))
                    p.bar()
                    ve.tensor_copy(csevf[0:1, :], pcs[0:1, :])
                    p.bar()
                    ve.tensor_copy(csrow[0:1, :], csevf[0:1, :])
                    p.bar()
                    te.matmul(ptot[:], tb["ones_1x128_bf"][:], csrow[0:1, :],
                              start=(qt == 0), stop=(qt == 63))
                    p.bar()
                ve.tensor_copy(totf[0:1, :], ptot[0:1, :])
                p.bar()
                ve.tensor_scalar(ceilr[0:1, :], totf[0:1, :], 1.0 / 128,
                                 0.49609375, op0=ALU.mult, op1=ALU.add)
                p.bar()
                ve.tensor_scalar(ceilr[0:1, :], ceilr[0:1, :], MAGIC, None,
                                 op0=ALU.add)
                p.bar()
                ve.tensor_scalar(ceilr[0:1, :], ceilr[0:1, :], MAGIC, 128.0,
                                 op0=ALU.subtract, op1=ALU.mult)
                p.bar()
                te.matmul(pg[0:32, 0:1], ceilr[0:1, :], tb["ones_1x1_f32"][:],
                          start=True, stop=True)
                p.bar()
                ve.tensor_copy(ceilc[:], pg[0:32, 0:1])
                p.bar()
                te.matmul(pr[0:32, 0:1], tb["ut32_f32"][:], ceilc[:],
                          start=True, stop=True)
                p.bar()
                ve.tensor_copy(basec[:], pr[0:32, 0:1])
                p.bar()
                te.matmul(pg[0:1, 0:32], basec[:], tb["eye32_f32"][:],
                          start=True, stop=True)
                p.bar()
                ve.tensor_copy(baser[0:1, :], pg[0:1, 0:32])
                p.bar()
                te.matmul(pr[:], tb["ones_1x128_f32"][:], baser[0:1, :],
                          start=True, stop=True)
                p.bar()
                ve.tensor_copy(basebc[:], pr[:])
                ve.tensor_scalar(cmp32[:], tb["bnd32"][:], basec[:], None,
                                 op0=ALU.is_ge)
                p.bar()
                te.matmul(ptc[0:1, :], tb["ones_32c_f32"][:], cmp32[:],
                          start=True, stop=True)
                p.bar()
                ve.tensor_scalar(tcrow[0:1, :], ptc[0:1, :], 1.0, None,
                                 op0=ALU.subtract)
                p.bar()
                te.matmul(ptc[:], tb["ones_1x128_f32"][:], tcrow[0:1, :],
                          start=True, stop=True)
                p.bar()
                ve.tensor_copy(tcb[:], ptc[:])
                p.bar()
                hc3 = hcall[:].rearrange("p (t k) -> p t k", k=32)
                i3 = tb["iota32"][:].unsqueeze(1).to_broadcast([128, 192, 32])
                c3 = ctabs[h][:].unsqueeze(2).to_broadcast([128, 192, 32])
                b3 = basebc[:].unsqueeze(1).to_broadcast([128, 192, 32])
                ve.tensor_tensor(hc3, i3, c3, op=ALU.is_equal)
                p.bar()
                ve.tensor_mul(hc3, hc3, b3)
                p.bar()
                ve.tensor_reduce(dscr[:, 0:192], hc3, axis=AX.X, op=ALU.add)
                p.bar()
                ve.tensor_add(dscr[:, 0:192], dscr[:, 0:192], prels[h][:])
                p.bar()
                ve.tensor_copy(posi192[:], dscr[:, 0:192])
                p.bar()
                for cix in range(192):
                    gp.indirect_dma_start(
                        out=LISTs[h],
                        out_offset=bass.IndirectOffsetOnAxis(
                            posi192[:, cix:cix + 1], 0),
                        in_=svali192[:, cix:cix + 1],
                        in_offset=None,
                        oob_is_err=False)
                p.bar()
                for blk, nblk in ((0, 128), (1, NT - 128)):
                    sy.dma_start(out=lraw[0:nblk, 0:128],
                                 in_=LISTs[h].rearrange("(t p) 1 -> t p", p=128)[
                                     blk * 128:blk * 128 + nblk, :])
                    p.bar()
                    ve.tensor_copy(lrf[0:nblk, 0:128], lraw[0:nblk, 0:128])
                    p.bar()
                    te.transpose(ptr[:, 0:nblk], lrf[0:nblk, 0:128],
                                 tb["eye_f32"][0:nblk, 0:nblk])
                    p.bar()
                    ve.tensor_copy(vtabs[h][:, blk * 128:blk * 128 + nblk],
                                   ptr[:, 0:nblk])
                    p.bar()
                vt = vtabs[h]
                ve.tensor_scalar(dscr[:], vt[:], 1.5, 0.25,
                                 op0=ALU.subtract, op1=ALU.mult)
                p.bar()
                ve.tensor_scalar(dscr[:], dscr[:], MAGIC, None, op0=ALU.add)
                p.bar()
                ve.tensor_scalar(qposs[h][:], dscr[:], MAGIC, None, op0=ALU.subtract)
                p.bar()
                ve.tensor_scalar_max(dscr[:], qposs[h][:], 0.0)
                p.bar()
                ve.tensor_copy(gidxs[h][:], dscr[:])
                ve.tensor_scalar(dscr[:], vt[:], 0.5, 1.0e7,
                                 op0=ALU.is_le, op1=ALU.mult)
                p.bar()
                ve.tensor_add(dscr2[:], vt[:], dscr[:])
                p.bar()
                ve.tensor_copy(sidxs[h][:], dscr2[:])
                ve.tensor_scalar(dscr[:], tcb[:], 128.0, None, op0=ALU.mult)
                p.bar()
                ve.tensor_add(dscr[:], dscr[:],
                              tb["iota128c"][:].to_broadcast([128, NT]))
                p.bar()
                ve.tensor_copy(kvidxs[h][:], dscr[:])
                ve.tensor_scalar(dscr2[:], tcb[:], 256.0, None, op0=ALU.mult)
                p.bar()
                ve.tensor_sub(qrels[h][:], qposs[h][:], dscr2[:])
                p.bar()

        if debug == 1:
            ve.tensor_copy(dscr[:], gidxs[0][:])
            p.bar()
            sy.dma_start(out=DBG[:, 0:NT], in_=dscr[:])
            ve.tensor_copy(dscr[:], sidxs[0][:])
            p.bar()
            sy.dma_start(out=DBG[:, NT:2 * NT], in_=dscr[:])
            ve.tensor_copy(dscr[:], kvidxs[0][:])
            p.bar()
            sy.dma_start(out=DBG[:, 2 * NT:3 * NT], in_=dscr[:])
            sy.dma_start(out=DBG[:, 3 * NT:4 * NT], in_=qrels[0][:])
            sy.dma_start(out=DBG[:, 4 * NT:5 * NT], in_=vtabs[0][:])
            sy.dma_start(out=DBG[:, 5 * NT:6 * NT], in_=tcb[:])
            ve.tensor_copy(gg[:], kmean[:])
            p.bar()
            sy.dma_start(out=DBGK, in_=gg[:])
            p.finish()
            p.replay(block)
            return nc

        with nc.psum_tensor("bqf0", [128, 128], BF16) as bqf0, \
             nc.psum_tensor("bqf1", [128, 128], BF16) as bqf1, \
             nc.psum_tensor("bs0", [128, 256], F32) as bs0, \
             nc.psum_tensor("bs1", [128, 256], F32) as bs1, \
             nc.psum_tensor("bt0", [128, 256], BF16) as bt0, \
             nc.psum_tensor("bt1", [128, 256], BF16) as bt1, \
             nc.psum_tensor("bo0", [128, 129], F32) as bo0, \
             nc.psum_tensor("bo1", [128, 129], F32) as bo1:
            bss, bts, bos = [bs0, bs1], [bt0, bt1], [bo0, bo1]
            bqfs = [bqf0, bqf1]
            hist = {k: {} for k in ("gath", "qft", "smm", "exp", "prb",
                                    "ptv", "oev", "sct")}

            def attn_unit(u, h, kind, idx):
                par = u % 2
                qgp = qg[:, par * 128:(par + 1) * 128]
                kstp = kst[:, par * 256:(par + 1) * 256]
                vstp = vst[:, par * 258:(par + 1) * 258]
                qftp = qftl[:, par * 128:(par + 1) * 128]
                exptp = expt[:, par * 256:(par + 1) * 256]
                prbp = probs[:, par * 256:(par + 1) * 256]
                ptevp = ptev[:, par * 256:(par + 1) * 256]
                ostgp = ostg[:, par * 129:(par + 1) * 129]
                pqfp = bqfs[par][:]

                w = hist["sct"].get(u - 2)
                if kind == "sel":
                    if w is not None:
                        p.wait("gpsimd", w)
                    gp.indirect_dma_start(
                        out=qgp, out_offset=None, in_=QDs[h],
                        in_offset=bass.IndirectOffsetOnAxis(
                            gidxs[h][:, idx:idx + 1], 0))
                    gp.indirect_dma_start(
                        out=kstp, out_offset=None, in_=KBL,
                        in_offset=bass.IndirectOffsetOnAxis(
                            kvidxs[h][:, idx:idx + 1], 0))
                    gp.indirect_dma_start(
                        out=vstp, out_offset=None, in_=VP,
                        in_offset=bass.IndirectOffsetOnAxis(
                            kvidxs[h][:, idx:idx + 1], 0))
                else:
                    c0 = idx // 2
                    if w is not None:
                        p.wait("sync", w)
                        p.wait("scalar", w)
                    sy.dma_start(out=kstp, in_=KBL[c0 * 128:(c0 + 1) * 128, :])
                    sc.dma_start(out=vstp, in_=VP[c0 * 128:(c0 + 1) * 128, :])
                hist["gath"][u] = p.snap()
                if safe:
                    p.bar()
                if cut <= 1:
                    for k in hist:
                        hist[k][u] = p.snap()
                    return

                p.wait("tensor", hist["gath"][u])
                if u >= 2:
                    p.wait("tensor", hist["qft"][u - 2])
                if kind == "sel":
                    te.transpose(pqfp, qgp, tb["eye_bf"][:])
                else:
                    te.transpose(pqfp, qsbs[h][:, idx * 128:(idx + 1) * 128],
                                 tb["eye_bf"][:])
                s_tp = p.snap()
                p.wait("vector", s_tp)
                ve.tensor_copy(qftp, pqfp)
                hist["qft"][u] = p.snap()
                if safe:
                    p.bar()
                if cut <= 2:
                    for k in ("smm", "exp", "prb", "ptv", "oev", "sct"):
                        hist[k][u] = p.snap()
                    return

                p.wait("tensor", hist["qft"][u])
                if u >= 2:
                    p.wait("tensor", hist["exp"][u - 2])
                te.matmul(bss[par][:], qftp, kstp, start=True, stop=True)
                hist["smm"][u] = p.snap()
                if safe:
                    p.bar()
                if cut <= 3:
                    for k in ("exp", "prb", "ptv", "oev", "sct"):
                        hist[k][u] = p.snap()
                    return

                p.wait("scalar", hist["smm"][u])
                sc.activation(out=exptp, in_=bss[par][:], func=AF.Exp)
                hist["exp"][u] = p.snap()
                if safe:
                    p.bar()
                if cut <= 4:
                    for k in ("prb", "ptv", "oev", "sct"):
                        hist[k][u] = p.snap()
                    return

                p.wait("vector", hist["exp"][u])
                if kind == "sel":
                    ve.scalar_tensor_tensor(
                        out=prbp, in0=tb["keyloc_bf"][:],
                        scalar=qrels[h][:, idx:idx + 1], in1=exptp,
                        op0=ALU.is_le, op1=ALU.mult)
                else:
                    ql = tb["qloc0"] if idx % 2 == 0 else tb["qloc1"]
                    ve.scalar_tensor_tensor(
                        out=prbp, in0=tb["keyloc_bf"][:], scalar=ql[:],
                        in1=exptp, op0=ALU.is_le, op1=ALU.mult)
                hist["prb"][u] = p.snap()
                if safe:
                    p.bar()
                if cut <= 5:
                    for k in ("ptv", "oev", "sct"):
                        hist[k][u] = p.snap()
                    return

                p.wait("tensor", hist["prb"][u])
                if u >= 2:
                    p.wait("tensor", hist["ptv"][u - 2])
                te.transpose(bts[par][:, 0:128], prbp[:, 0:128], tb["eye_bf"][:])
                te.transpose(bts[par][:, 128:256], prbp[:, 128:256], tb["eye_bf"][:])
                s_pt = p.snap()
                p.wait("vector", s_pt)
                ve.tensor_copy(ptevp, bts[par][:])
                hist["ptv"][u] = p.snap()
                if safe:
                    p.bar()
                if cut <= 6:
                    for k in ("oev", "sct"):
                        hist[k][u] = p.snap()
                    return

                p.wait("tensor", hist["ptv"][u])
                if u >= 2:
                    p.wait("tensor", hist["oev"][u - 2])
                te.matmul(bos[par][:], ptevp[:, 0:128], vstp[:, 0:129],
                          start=True, stop=False)
                te.matmul(bos[par][:], ptevp[:, 128:256], vstp[:, 129:258],
                          start=False, stop=True)
                s_pv = p.snap()
                p.wait("vector", s_pv)
                if u >= 2:
                    p.wait("vector", hist["sct"][u - 2])
                ve.tensor_copy(ostgp, bos[par][:])
                hist["oev"][u] = p.snap()
                if safe:
                    p.bar()
                if cut <= 7:
                    hist["sct"][u] = p.snap()
                    return

                if kind == "sel":
                    p.wait("gpsimd", hist["oev"][u])
                    gp.indirect_dma_start(
                        out=OSs[h], out_offset=bass.IndirectOffsetOnAxis(
                            sidxs[h][:, idx:idx + 1], 0),
                        in_=ostgp, in_offset=None, oob_is_err=False)
                else:
                    p.wait("sync", hist["oev"][u])
                    q0 = idx * 128
                    sy.dma_start(
                        out=OSs[h][4 * q0:4 * q0 + 512, :].rearrange(
                            "(q j) e -> q j e", j=4)[:, 0, :],
                        in_=ostgp)
                hist["sct"][u] = p.snap()
                if safe:
                    p.bar()

            u = 0
            for h in range(2):
                for idx in range(nown):
                    attn_unit(u, h, "own", idx)
                    u += 1
                for idx in range(nsel):
                    attn_unit(u, h, "sel", idx)
                    u += 1
            p.bar()

            for h in range(2 if debug != 2 else 0):
                for qt in range(64):
                    sy.dma_start(out=comb[:].rearrange("q (j e) -> q j e", e=129),
                                 in_=OSs[h][qt * 512:(qt + 1) * 512, :].rearrange(
                                     "(q j) e -> q j e", j=4))
                    p.bar()
                    ve.tensor_add(s01[:], comb[:, 0:129], comb[:, 129:258])
                    p.bar()
                    ve.tensor_add(s23[:], comb[:, 258:387], comb[:, 387:516])
                    p.bar()
                    ve.tensor_add(osum[:], s01[:], s23[:])
                    p.bar()
                    ve.reciprocal(rz[:], osum[:, 128:129])
                    p.bar()
                    ve.tensor_scalar(ofin[:], osum[:, 0:128], rz[:], None,
                                     op0=ALU.mult)
                    p.bar()
                    te.transpose(bqf0[:], ofin[:], tb["eye_bf"][:])
                    p.bar()
                    ve.tensor_copy(otev[:], bqf0[:])
                    p.bar()
                    sy.dma_start(out=OT[h * 128:(h + 1) * 128,
                                        qt * 128:(qt + 1) * 128], in_=otev[:])
                    p.bar()
        p.finish()
        p.replay(block)
    return nc


def stage_l2(inputs, xnt_full):
    t = host_tables()
    wq = np.asarray(inputs["L0_attn_wq"], np.float32)
    wk = np.asarray(inputs["L0_attn_wk"], np.float32)
    wv = np.asarray(inputs["L0_attn_wv"], np.float32)
    maps = []
    for c in range(NCORES):
        m = {
            "xnt_full": xnt_full,
            "wq2": np.ascontiguousarray(wq[:, 2 * c * 128:(2 * c + 2) * 128]),
            "wk1": np.ascontiguousarray(wk[:, (c // 2) * 128:(c // 2 + 1) * 128]),
            "wv1": np.ascontiguousarray(wv[:, (c // 2) * 128:(c // 2 + 1) * 128]),
            "cosT": t["cosT"], "sinT": t["sinT"],
            "cosTu": t["cosTu"], "sinTu": t["sinTu"],
        }
        for nm, _, _ in TABLE_SPECS:
            m[nm] = t[nm]
        maps.append(m)
    return maps


def kernel(**inputs):
    raise NotImplementedError



def build_l3():
    nc = bass.Bass("TRN2", target_bir_lowering=False)
    X = nc.dram_tensor("x_rows", [TPC, D], F32, kind="ExternalInput").ap()
    XNS = nc.dram_tensor("xn_sl", [TPC, D], BF16, kind="ExternalInput").ap()
    OTF = nc.dram_tensor("ot_full", [D, TPC], BF16, kind="ExternalInput").ap()
    WO = nc.dram_tensor("wo", [D, D], F32, kind="ExternalInput").ap()
    W1 = nc.dram_tensor("w1", [DFF, D], F32, kind="ExternalInput").ap()
    B1 = nc.dram_tensor("b1", [1, DFF], F32, kind="ExternalInput").ap()
    W2 = nc.dram_tensor("w2", [D, DFF], F32, kind="ExternalInput").ap()
    B2 = nc.dram_tensor("b2", [1, D], F32, kind="ExternalInput").ap()
    LNW = nc.dram_tensor("lnw", [1, D], F32, kind="ExternalInput").ap()
    LNB = nc.dram_tensor("lnb", [1, D], F32, kind="ExternalInput").ap()
    EYE = nc.dram_tensor("eye_bf", [128, 128], BF16, kind="ExternalInput").ap()
    OUT = nc.dram_tensor("out_rows", [TPC, D], F32, kind="ExternalOutput").ap()

    W1QT = nc.dram_tensor("w1qt", [D, DFF], BF16, kind="Internal").ap()
    W2QT = nc.dram_tensor("w2qt", [DFF, D], BF16, kind="Internal").ap()
    HQT = nc.dram_tensor("hqt", [D, TPC], BF16, kind="Internal").ap()
    Y1QT = nc.dram_tensor("y1qt", [DFF, TPC], BF16, kind="Internal").ap()
    Y2 = nc.dram_tensor("y2", [TPC, D], F32, kind="Internal").ap()

    es = contextlib.ExitStack()

    def sb(name, shape, dt):
        return es.enter_context(nc.sbuf_tensor(name, shape, dt))

    with es:
        eye = sb("eye", [128, 128], BF16)
        wld = sb("wld", [128, 4096], F32)
        qscr = sb("qscr", [128, 4096], F32)
        amx = sb("amx", [128, 128], F32)
        amx2 = sb("amx2", [128, 128], F32)
        qout = sb("qout", [128, 4096], BF16)
        tstage = sb("tstage", [128, 4096], BF16)
        hqtile = sb("hqtile", [128, 2048], BF16)
        xf = sb("xf", [128, D], F32)
        stat = sb("stat", [128, 4], F32)
        wbc = sb("wbc", [128, D], F32)
        bbc = sb("bbc", [128, D], F32)
        b1bc = sb("b1bc", [128, 2048], F32)
        b2bc = sb("b2bc", [128, D], F32)
        hb = sb("hb", [128, D], BF16)
        wbig = sb("wbig", [128, 16 * 1024], BF16)
        tb16 = sb("tb16", [128, 512], BF16)
        g1 = sb("g1", [128, 512], F32)
        g2 = sb("g2", [128, 512], F32)
        g3 = sb("g3", [128, 512], F32)
        y1tile = sb("y1tile", [128, 64 * 128], BF16)
        xt = sb("xt", [128, D], F32)
        y2t = sb("y2t", [128, D], F32)
        fft = sb("fft", [128, D], BF16)
        ot16 = sb("ot16", [128, 16 * 128], BF16)
        outt = sb("outt", [128, D], F32)

        block = es.enter_context(nc.Block())
        p = Prog(nc)
        sy, te, ve, sc, gp = (p.on("sync"), p.on("tensor"), p.on("vector"),
                              p.on("scalar"), p.on("gpsimd"))

        def quant(src, wdt, dst):
            nw = wdt // 32
            src3 = src.rearrange("p (w k) -> p w k", k=32)
            ve.tensor_reduce(amx[:, 0:nw], src3, axis=AX.X, op=ALU.max,
                             apply_absolute_value=True)
            p.bar()
            ve.tensor_scalar_max(amx[:, 0:nw], amx[:, 0:nw], 1e-30)
            p.bar()
            ami = amx[:, 0:nw].bitcast(I32)
            ve.tensor_scalar(ami, ami, 0x7F800000, None, op0=ALU.bitwise_and)
            p.bar()
            am2i = amx2[:, 0:nw].bitcast(I32)
            ve.tensor_scalar(am2i, ami, -1, 2181038080, op0=ALU.mult, op1=ALU.add)
            p.bar()
            ve.tensor_scalar(ami, ami, 50331648, None, op0=ALU.subtract)
            p.bar()
            inv3 = amx2[:, 0:nw].unsqueeze(2).to_broadcast([128, nw, 32])
            scl3 = amx[:, 0:nw].unsqueeze(2).to_broadcast([128, nw, 32])
            t3 = qscr[:, 0:wdt].rearrange("p (w k) -> p w k", k=32)
            ve.tensor_mul(t3, src3, inv3)
            p.bar()
            ve.tensor_scalar(qscr[:, 0:wdt], qscr[:, 0:wdt], MAGIC, None, op0=ALU.add)
            p.bar()
            ve.tensor_scalar(qscr[:, 0:wdt], qscr[:, 0:wdt], MAGIC, None,
                             op0=ALU.subtract)
            p.bar()
            ve.tensor_scalar(qscr[:, 0:wdt], qscr[:, 0:wdt], 127.0, -128.0,
                             op0=ALU.min, op1=ALU.max)
            p.bar()
            ve.tensor_mul(dst.rearrange("p (w k) -> p w k", k=32), t3, scl3)
            p.bar()

        def trans_block(src_bf, n128, dststage):
            for j in range(0, n128, 2):
                te.transpose(tp0[:], src_bf[:, j * 128:(j + 1) * 128], eye[:])
                if j + 1 < n128:
                    te.transpose(tp1[:], src_bf[:, (j + 1) * 128:(j + 2) * 128],
                                 eye[:])
                p.bar()
                ve.tensor_copy(dststage[:, j * 128:(j + 1) * 128], tp0[:])
                if j + 1 < n128:
                    ve.tensor_copy(dststage[:, (j + 1) * 128:(j + 2) * 128], tp1[:])
                p.bar()

        sy.dma_start(out=eye[:], in_=EYE)
        sy.dma_start(out=wbc[:], in_=LNW.to_broadcast([128, D]))
        sy.dma_start(out=bbc[:], in_=LNB.to_broadcast([128, D]))
        sy.dma_start(out=b2bc[:], in_=B2.to_broadcast([128, D]))
        p.bar()

        with nc.psum_tensor("tp0", [128, 128], BF16) as tp0, \
             nc.psum_tensor("tp1", [128, 128], BF16) as tp1, \
             nc.psum_tensor("mm0", [128, 512], F32) as mm0, \
             nc.psum_tensor("mm1", [128, 512], F32) as mm1:
            mms = [mm0, mm1]

            for i in range(64):
                sy.dma_start(out=wld[:, 0:D], in_=W1[i * 128:(i + 1) * 128, :])
                p.bar()
                quant(wld[:, 0:D], D, qout[:, 0:D])
                trans_block(qout[:, 0:D], 16, tstage)
                sy.dma_start(
                    out=W1QT[:, i * 128:(i + 1) * 128].rearrange(
                        "(c p) f -> p c f", p=128),
                    in_=tstage[:, 0:D].rearrange("p (c f) -> p c f", f=128))
                p.bar()
            for i in range(16):
                for hf in range(2):
                    sy.dma_start(out=wld[:],
                                 in_=W2[i * 128:(i + 1) * 128,
                                        hf * 4096:(hf + 1) * 4096])
                    p.bar()
                    quant(wld[:], 4096, qout[:])
                    trans_block(qout[:], 32, tstage)
                    sy.dma_start(
                        out=W2QT[hf * 4096:(hf + 1) * 4096,
                                 i * 128:(i + 1) * 128].rearrange(
                            "(c p) f -> p c f", p=128),
                        in_=tstage[:].rearrange("p (c f) -> p c f", f=128))
                    p.bar()

            for i in range(NTT):
                sy.dma_start(out=hb[:], in_=XNS[i * 128:(i + 1) * 128, :])
                p.bar()
                ve.tensor_copy(xf[:], hb[:])
                p.bar()
                ve.reduce_sum(stat[:, 0:1], xf[:], axis=AX.X)
                p.bar()
                ve.tensor_scalar(stat[:, 0:1], stat[:, 0:1], 1.0 / D, None,
                                 op0=ALU.mult)
                p.bar()
                ve.tensor_scalar(xf[:], xf[:], stat[:, 0:1], None, op0=ALU.subtract)
                p.bar()
                sc.activation(out=qscr[:, 0:D], in_=xf[:], func=AF.Square)
                p.bar()
                ve.reduce_sum(stat[:, 1:2], qscr[:, 0:D], axis=AX.X)
                p.bar()
                ve.tensor_scalar(stat[:, 1:2], stat[:, 1:2], 1.0 / D, 1e-5,
                                 op0=ALU.mult, op1=ALU.add)
                p.bar()
                sc.activation(out=stat[:, 2:3], in_=stat[:, 1:2], func=AF.Sqrt)
                p.bar()
                ve.reciprocal(stat[:, 3:4], stat[:, 2:3])
                p.bar()
                ve.tensor_scalar(xf[:], xf[:], stat[:, 3:4], None, op0=ALU.mult)
                p.bar()
                ve.tensor_mul(xf[:], xf[:], wbc[:])
                p.bar()
                ve.tensor_add(hb[:], xf[:], bbc[:])
                p.bar()
                quant(hb[:], D, hqtile[:])
                trans_block(hqtile[:], 16, tstage)
                sy.dma_start(
                    out=HQT[:, i * 128:(i + 1) * 128].rearrange(
                        "(c p) f -> p c f", p=128),
                    in_=tstage[:, 0:D].rearrange("p (c f) -> p c f", f=128))
                p.bar()

            for e8 in range(8):
                sy.dma_start(out=wbig[:].rearrange("p (c f) -> p c f", f=1024),
                             in_=W1QT[:, e8 * 1024:(e8 + 1) * 1024].rearrange(
                                 "(c p) f -> p c f", p=128))
                sc.dma_start(out=b1bc[:, 0:1024],
                             in_=B1[:, e8 * 1024:(e8 + 1) * 1024]
                             .to_broadcast([128, 1024]))
                p.bar()
                for i in range(NTT):
                    sy.dma_start(out=hqtile[:].rearrange("p (c f) -> p c f", f=128),
                                 in_=HQT[:, i * 128:(i + 1) * 128].rearrange(
                                     "(c p) f -> p c f", p=128))
                    p.bar()
                    for sub in range(2):
                        mm = mms[sub % 2]
                        for j in range(16):
                            te.matmul(mm[:],
                                      hqtile[:, j * 128:(j + 1) * 128],
                                      wbig[:, j * 1024 + sub * 512:
                                           j * 1024 + sub * 512 + 512],
                                      start=(j == 0), stop=(j == 15))
                        p.bar()
                        bsl = b1bc[:, sub * 512:(sub + 1) * 512]
                        ve.tensor_add(tb16[:], mm[:], bsl)
                        p.bar()
                        gp.tensor_mul(g1[:], tb16[:], tb16[:])
                        p.bar()
                        ve.tensor_scalar(g1[:], g1[:], 0.044715, 1.0,
                                         op0=ALU.mult, op1=ALU.add)
                        p.bar()
                        gp.tensor_mul(g2[:], tb16[:], g1[:])
                        p.bar()
                        sc.activation(out=g3[:], in_=g2[:], func=AF.Tanh,
                                      scale=GELU_C)
                        p.bar()
                        ve.tensor_scalar(g3[:], g3[:], 1.0, 0.5,
                                         op0=ALU.add, op1=ALU.mult)
                        p.bar()
                        ve.tensor_mul(tb16[:], tb16[:], g3[:])
                        p.bar()
                        quant(tb16[:], 512, qout[:, 0:512])
                        trans_block(qout[:, 0:512], 4, tstage)
                        sy.dma_start(
                            out=Y1QT[e8 * 1024 + sub * 512:
                                     e8 * 1024 + (sub + 1) * 512,
                                     i * 128:(i + 1) * 128].rearrange(
                                "(b pp) f -> pp b f", pp=128),
                            in_=tstage[:, 0:512].rearrange("p (b f) -> p b f", f=128))
                        p.bar()

            for e2 in range(8):
                sy.dma_start(out=wbig[:].rearrange("p (c f) -> p c f", f=256),
                             in_=W2QT[:, e2 * 256:(e2 + 1) * 256].rearrange(
                                 "(c p) f -> p c f", p=128))
                p.bar()
                for i in range(NTT):
                    sy.dma_start(out=y1tile[:].rearrange("p (c f) -> p c f", f=128),
                                 in_=Y1QT[:, i * 128:(i + 1) * 128].rearrange(
                                     "(c p) f -> p c f", p=128))
                    p.bar()
                    for j in range(64):
                        te.matmul(mms[i % 2][:, 0:256],
                                  y1tile[:, j * 128:(j + 1) * 128],
                                  wbig[:, j * 256:(j + 1) * 256],
                                  start=(j == 0), stop=(j == 63))
                    p.bar()
                    ve.tensor_copy(g1[:, 0:256], mms[i % 2][:, 0:256])
                    p.bar()
                    sy.dma_start(out=Y2[i * 128:(i + 1) * 128,
                                        e2 * 256:(e2 + 1) * 256], in_=g1[:, 0:256])
                    p.bar()

            for half in range(2):
                for cch in range(16):
                    sy.dma_start(out=wld[:, 0:1024],
                                 in_=WO[cch * 128:(cch + 1) * 128,
                                        half * 1024:(half + 1) * 1024])
                    p.bar()
                    ve.tensor_copy(wbig[:, cch * 1024:(cch + 1) * 1024],
                                   wld[:, 0:1024])
                    p.bar()
                for i in range(NTT):
                    sy.dma_start(out=ot16[:].rearrange("p (c f) -> p c f", f=128),
                                 in_=OTF[:, i * 128:(i + 1) * 128].rearrange(
                                     "(c p) f -> p c f", p=128))
                    sc.dma_start(out=xt[:], in_=X[i * 128:(i + 1) * 128, :])
                    sc.dma_start(out=y2t[:], in_=Y2[i * 128:(i + 1) * 128, :])
                    sc.dma_start(out=hb[:], in_=XNS[i * 128:(i + 1) * 128, :])
                    p.bar()
                    ve.tensor_add(fft[:], y2t[:], b2bc[:])
                    p.bar()
                    ve.tensor_add(fft[:], fft[:], hb[:])
                    p.bar()
                    for sub in range(2):
                        mm = mms[sub % 2]
                        for j in range(16):
                            te.matmul(mm[:], ot16[:, j * 128:(j + 1) * 128],
                                      wbig[:, j * 1024 + sub * 512:
                                           j * 1024 + sub * 512 + 512],
                                      start=(j == 0), stop=(j == 15))
                        p.bar()
                        sl = slice(half * 1024 + sub * 512,
                                   half * 1024 + (sub + 1) * 512)
                        ve.tensor_add(outt[:, sl], mm[:], xt[:, sl])
                        p.bar()
                        ve.tensor_add(outt[:, sl], outt[:, sl], fft[:, sl])
                        p.bar()
                    sy.dma_start(out=OUT[i * 128:(i + 1) * 128,
                                         half * 1024:(half + 1) * 1024],
                                 in_=outt[:, half * 1024:(half + 1) * 1024])
                    p.bar()
        p.finish()
        p.replay(block)
    return nc


def stage_l3(inputs, xn_full, ot_full):
    t = host_tables()
    x = np.asarray(inputs["L0_attn_hidden_states"]).reshape(S, D).astype(np.float32)
    maps = []
    for c in range(NCORES):
        maps.append({
            "x_rows": np.ascontiguousarray(x[c * TPC:(c + 1) * TPC]),
            "xn_sl": np.ascontiguousarray(xn_full[c * TPC:(c + 1) * TPC]),
            "ot_full": np.ascontiguousarray(ot_full[:, c * TPC:(c + 1) * TPC]),
            "wo": np.asarray(inputs["L0_attn_wo"], np.float32),
            "w1": np.asarray(inputs["L0_ffn_W1"], np.float32),
            "b1": np.asarray(inputs["L0_ffn_b1"], np.float32).reshape(1, DFF),
            "w2": np.asarray(inputs["L0_ffn_W2"], np.float32),
            "b2": np.asarray(inputs["L0_ffn_b2"], np.float32).reshape(1, D),
            "lnw": np.asarray(inputs["L0_ffn_ln_w"], np.float32).reshape(1, D),
            "lnb": np.asarray(inputs["L0_ffn_ln_b"], np.float32).reshape(1, D),
            "eye_bf": t["eye_bf"],
        })
    return maps


def kernel(**inputs):
    nc1 = build_l1()
    res1 = run_bass_kernel_spmd(nc1, stage_l1(inputs), core_ids=list(range(NCORES)))
    xn_full = np.concatenate([res1.results[c]["xn_sl"] for c in range(NCORES)], axis=0)
    xnt_full = np.concatenate([res1.results[c]["xnt_sl"] for c in range(NCORES)],
                              axis=1)
    nc2 = build_l2(safe=True)
    res2 = run_bass_kernel_spmd(nc2, stage_l2(inputs, xnt_full),
                                core_ids=list(range(NCORES)))
    ot_full = np.concatenate([res2.results[c]["ot_sl"] for c in range(NCORES)],
                             axis=0)
    nc3 = build_l3()
    res3 = run_bass_kernel_spmd(nc3, stage_l3(inputs, xn_full, ot_full),
                                core_ids=list(range(NCORES)))
    out = np.concatenate([res3.results[c]["out_rows"] for c in range(NCORES)],
                         axis=0)
    return out.reshape(1, S, D).astype(np.float32)


# version 03
# speedup vs baseline: 1.1003x; 1.0181x over previous
import contextlib
import numpy as np
import ml_dtypes
import concourse.bass as bass
import concourse.mybir as mybir
from concourse.bass_utils import run_bass_kernel_spmd

F32 = mybir.dt.float32
BF16 = mybir.dt.bfloat16
I32 = mybir.dt.int32
I16 = mybir.dt.int16
AF = mybir.ActivationFunctionType
ALU = mybir.AluOpType
AX = mybir.AxisListType

BF = ml_dtypes.bfloat16

S, D, HD, NH, NKV, CHK, NCHK = 8192, 2048, 128, 16, 4, 256, 32
NCORES = 8
TPC = S // NCORES
NTT = TPC // 128
DFF = 8192
NT = 224
MAGIC = 12582912.0
SCALE_QK = float(HD) ** -0.5
GELU_C = 0.7978845608028654
NEG = -1.0e30


def host_tables():
    t = {}
    t["eye_bf"] = np.eye(128).astype(BF)
    t["eye_f32"] = np.eye(128).astype(np.float32)
    t["lt_bf"] = (np.arange(128)[:, None] < np.arange(128)[None, :]).astype(BF)
    t["ut32_f32"] = (np.arange(32)[:, None] < np.arange(32)[None, :]).astype(np.float32)
    t["eye32_f32"] = np.eye(32).astype(np.float32)
    t["ones_col_bf"] = np.ones((128, 1), BF)
    t["ones_1x128_bf"] = np.ones((1, 128), BF)
    t["ones_1x128_f32"] = np.ones((1, 128), np.float32)
    t["ones_32c_f32"] = np.ones((32, 1), np.float32)
    t["ones_1x1_f32"] = np.ones((1, 1), np.float32)
    t["iota32"] = np.broadcast_to(np.arange(32, dtype=np.float32), (128, 32)).copy()
    t["iota128c"] = np.arange(128, dtype=np.float32).reshape(128, 1)
    col = np.arange(256)
    keyloc = (2 * (col % 128) + col // 128).astype(BF)
    t["keyloc_bf"] = np.broadcast_to(keyloc, (128, 256)).copy()
    t["cval"] = np.broadcast_to(np.arange(40, dtype=np.float32), (128, 40)).copy()
    t["bnd32"] = np.broadcast_to(128.0 * np.arange(NT, dtype=np.float32),
                                 (32, NT)).copy()
    sv = np.zeros((128, 192), np.float32)
    for qt in range(64):
        for j in (1, 2, 3):
            sv[:, qt * 3 + j - 1] = 4 * (qt * 128 + np.arange(128)) + j
    t["svaltab"] = sv
    t["qloc0"] = np.arange(128, dtype=np.float32).reshape(128, 1)
    t["qloc1"] = (np.arange(128, dtype=np.float32) + 128).reshape(128, 1)
    pos = np.arange(S, dtype=np.float64)
    inv = 1.0 / (10000.0 ** (np.arange(0, HD, 2, dtype=np.float64) / HD))
    ang = pos[:, None] * inv[None, :]
    t["cosT"] = (np.cos(ang) * SCALE_QK).astype(np.float32)
    t["sinT"] = (np.sin(ang) * SCALE_QK).astype(np.float32)
    t["cosTu"] = np.cos(ang).astype(np.float32)
    t["sinTu"] = np.sin(ang).astype(np.float32)
    return t


class Prog:

    ENGS = ("sync", "tensor", "vector", "scalar", "gpsimd")

    def __init__(self, nc):
        self.nc = nc
        self._sem_cm = nc.semaphore("bar")
        self.sem = self._sem_cm.__enter__()
        self.count = 0
        self.ops = {e: [] for e in self.ENGS}
        self.pending = {}
        self.wait_at = {e: None for e in self.ENGS}

    class _H:
        def __init__(self, p, name):
            object.__setattr__(self, "p", p)
            object.__setattr__(self, "name", name)

        def __getattr__(self, attr):
            p, name = self.p, self.name

            def wrapped(*a, **kw):
                if p.wait_at[name] is not None:
                    p.ops[name].append(("wait", p.wait_at[name]))
                    p.wait_at[name] = None
                isdma = attr in ("dma_start", "indirect_dma_start")
                p.ops[name].append(["op", attr, a, kw, 16 if isdma else 0])
                if isdma:
                    p.count += 16
                else:
                    p.pending[name] = len(p.ops[name]) - 1
            return wrapped

    def on(self, name):
        return Prog._H(self, name)

    def bar(self):
        for name, idx in self.pending.items():
            self.ops[name][idx][4] += 1
            self.count += 1
        self.pending = {}
        for e in self.ENGS:
            self.wait_at[e] = self.count

    def snap(self):
        for name, idx in list(self.pending.items()):
            self.ops[name][idx][4] += 1
            self.count += 1
        self.pending = {}
        return self.count

    def wait(self, eng, val):
        self.ops[eng].append(("wait", val))
        if self.wait_at[eng] is not None and self.wait_at[eng] <= val:
            self.wait_at[eng] = None

    def finish(self, eng="sync"):
        self.bar()
        self.ops[eng].append(("wait", self.count))
        self.wait_at[eng] = None

    def replay(self, block):
        def mk(name):
            def run(handle):
                eng = getattr(self.nc, name)
                for op in self.ops[name]:
                    if op[0] == "wait":
                        eng.wait_ge(self.sem, op[1])
                    else:
                        _, attr, a, kw, inc = op
                        ins = getattr(eng, attr)(*a, **kw)
                        if inc:
                            ins.then_inc(self.sem, inc)
            return run
        block.sync(mk("sync"))
        block.tensor(mk("tensor"))
        block.vector(mk("vector"))
        block.scalar(mk("scalar"))
        block.gpsimd(mk("gpsimd"))



def build_l1():
    nc = bass.Bass("TRN2", target_bir_lowering=False)
    X = nc.dram_tensor("x_rows", [TPC, D], F32, kind="ExternalInput").ap()
    EYE = nc.dram_tensor("eye_bf", [128, 128], BF16, kind="ExternalInput").ap()
    XN = nc.dram_tensor("xn_sl", [TPC, D], BF16, kind="ExternalOutput").ap()
    XNT = nc.dram_tensor("xnt_sl", [D, TPC], BF16, kind="ExternalOutput").ap()

    with (
        nc.sbuf_tensor("xt", [128, D], F32) as xt,
        nc.sbuf_tensor("scr", [128, D], F32) as scr,
        nc.sbuf_tensor("stat", [128, 4], F32) as stat,
        nc.sbuf_tensor("xnb", [128, D], BF16) as xnb,
        nc.sbuf_tensor("eye", [128, 128], BF16) as eye,
        nc.sbuf_tensor("tev", [128, 2048], BF16) as tev,
        nc.psum_tensor("pt0", [128, 128], BF16) as pt0,
        nc.psum_tensor("pt1", [128, 128], BF16) as pt1,
        nc.Block() as block,
    ):
        p = Prog(nc)
        sy, ve, sc, te = p.on("sync"), p.on("vector"), p.on("scalar"), p.on("tensor")
        sy.dma_start(out=eye[:], in_=EYE)
        p.bar()
        for i in range(NTT):
            sy.dma_start(out=xt[:], in_=X[i * 128:(i + 1) * 128, :])
            p.bar()
            ve.reduce_sum(stat[:, 0:1], xt[:], axis=AX.X)
            p.bar()
            ve.tensor_scalar(stat[:, 0:1], stat[:, 0:1], 1.0 / D, None, op0=ALU.mult)
            p.bar()
            ve.tensor_scalar(scr[:], xt[:], stat[:, 0:1], None, op0=ALU.subtract)
            p.bar()
            sc.activation(out=xt[:], in_=scr[:], func=AF.Square)
            p.bar()
            ve.reduce_sum(stat[:, 1:2], xt[:], axis=AX.X)
            p.bar()
            ve.tensor_scalar(stat[:, 1:2], stat[:, 1:2], 1.0 / D, 1e-5,
                             op0=ALU.mult, op1=ALU.add)
            p.bar()
            sc.activation(out=stat[:, 2:3], in_=stat[:, 1:2], func=AF.Sqrt)
            p.bar()
            ve.reciprocal(stat[:, 3:4], stat[:, 2:3])
            p.bar()
            ve.tensor_scalar(xnb[:], scr[:], stat[:, 3:4], None, op0=ALU.mult)
            p.bar()
            sy.dma_start(out=XN[i * 128:(i + 1) * 128, :], in_=xnb[:])
            for j in range(0, 16, 2):
                te.transpose(pt0[:], xnb[:, j * 128:(j + 1) * 128], eye[:])
                te.transpose(pt1[:], xnb[:, (j + 1) * 128:(j + 2) * 128], eye[:])
                p.bar()
                ve.tensor_copy(tev[:, j * 128:(j + 1) * 128], pt0[:])
                ve.tensor_copy(tev[:, (j + 1) * 128:(j + 2) * 128], pt1[:])
                p.bar()
            sc.dma_start(
                out=XNT[:, i * 128:(i + 1) * 128].rearrange("(c p) f -> p c f", p=128),
                in_=tev[:].rearrange("p (c f) -> p c f", f=128))
            p.bar()
        p.finish()
        p.replay(block)
    return nc


def stage_l1(inputs):
    t = host_tables()
    x = np.asarray(inputs["L0_attn_hidden_states"]).reshape(S, D)
    return [{
        "x_rows": np.ascontiguousarray(x[c * TPC:(c + 1) * TPC]).astype(np.float32),
        "eye_bf": t["eye_bf"],
    } for c in range(NCORES)]



TABLE_SPECS = [
    ("eye_bf", [128, 128], BF16), ("eye_f32", [128, 128], F32),
    ("lt_bf", [128, 128], BF16),
    ("ut32_f32", [32, 32], F32), ("eye32_f32", [32, 32], F32),
    ("ones_col_bf", [128, 1], BF16), ("ones_1x128_bf", [1, 128], BF16),
    ("ones_1x128_f32", [1, 128], F32), ("ones_32c_f32", [32, 1], F32),
    ("ones_1x1_f32", [1, 1], F32),
    ("iota32", [128, 32], F32), ("iota128c", [128, 1], F32),
    ("keyloc_bf", [128, 256], BF16), ("cval", [128, 40], F32),
    ("bnd32", [32, NT], F32), ("qloc0", [128, 1], F32), ("qloc1", [128, 1], F32),
    ("svaltab", [128, 192], F32),
]


def build_l2(debug=0, nown=64, nsel=NT, cut=99, safe=False):
    nc = bass.Bass("TRN2", target_bir_lowering=False)
    XNT = nc.dram_tensor("xnt_full", [D, S], BF16, kind="ExternalInput").ap()
    WQ = nc.dram_tensor("wq2", [D, 256], F32, kind="ExternalInput").ap()
    WK = nc.dram_tensor("wk1", [D, 128], F32, kind="ExternalInput").ap()
    WV = nc.dram_tensor("wv1", [D, 128], F32, kind="ExternalInput").ap()
    COST = nc.dram_tensor("cosT", [S, 64], F32, kind="ExternalInput").ap()
    SINT = nc.dram_tensor("sinT", [S, 64], F32, kind="ExternalInput").ap()
    COSU = nc.dram_tensor("cosTu", [S, 64], F32, kind="ExternalInput").ap()
    SINU = nc.dram_tensor("sinTu", [S, 64], F32, kind="ExternalInput").ap()
    tby = {nm: nc.dram_tensor(nm, shp, dt, kind="ExternalInput").ap()
           for nm, shp, dt in TABLE_SPECS}

    OT = nc.dram_tensor("ot_sl", [256, S], BF16, kind="ExternalOutput").ap()

    ik = "ExternalOutput" if debug else "Internal"
    QDs = [nc.dram_tensor(f"qd{h}", [S, 128], BF16, kind=ik).ap()
           for h in range(2)]
    KBL = nc.dram_tensor("kbl", [NCHK * 128, 256], BF16, kind=ik).ap()
    VP = nc.dram_tensor("vp", [NCHK * 128, 258], BF16, kind=ik).ap()
    LISTs = [nc.dram_tensor(f"list{h}", [NT * 128, 1], I16, kind=ik).ap()
             for h in range(2)]
    OSs = [nc.dram_tensor(f"os{h}", [4 * S, 129], F32,
                          kind=("ExternalOutput" if debug == 2 else "Internal")).ap()
           for h in range(2)]
    if debug:
        DBG = nc.dram_tensor("dbg", [128, 6 * NT], F32, kind="ExternalOutput").ap()
        DBGK = nc.dram_tensor("dbgk", [128, 32], F32, kind="ExternalOutput").ap()

    es = contextlib.ExitStack()

    def sb(name, shape, dt):
        return es.enter_context(nc.sbuf_tensor(name, shape, dt))

    with es:
        tb = {nm: sb("t_" + nm, shp, dt) for nm, shp, dt in TABLE_SPECS}
        wqb = sb("wqb", [128, 16 * 256], BF16)
        wkb = sb("wkb", [128, 16 * 128], BF16)
        wvb = sb("wvb", [128, 16 * 128], BF16)
        wstg = sb("wstg", [128, 16 * 256], F32)
        xnt_t = sb("xnt_t", [128, 2048], BF16)
        rop = sb("rop", [128, 128], F32)
        sc1 = sb("sc1", [128, 128], F32)
        sc2 = sb("sc2", [128, 128], F32)
        ctt = sb("ctt", [128, 64], F32)
        stt_ = sb("stt_", [128, 64], F32)
        cttu = sb("cttu", [128, 64], F32)
        sttu = sb("sttu", [128, 64], F32)
        ktokb = sb("ktokb", [128, 128], BF16)
        kfev = sb("kfev", [128, 128], BF16)
        kbev = sb("kbev", [128, 128], BF16)
        vev = sb("vev", [128, 129], BF16)
        qsbs = [sb(f"qsb{h}", [128, 64 * 128], BF16) for h in range(2)]
        kmacc = sb("kmacc", [128, 64], F32)
        kmean = sb("kmean", [128, 32], BF16)
        qft = sb("qft", [128, 128], BF16)
        gg = sb("gg", [128, 32], F32)
        tmask = sb("tmask", [128, 32], F32)
        m8 = sb("m8", [128, 8], F32)
        hraw = sb("hraw", [128, 32], F32)
        escr = sb("escr", [128, 32], F32)
        idxc = sb("idxc", [128, 1], F32)
        isbad = sb("isbad", [128, 1], I32)
        hcb = sb("hcb", [128, 3 * 32], BF16)
        hall = sb("hall", [128, 32], BF16)
        ssum = sb("ssum", [128, 32], F32)
        csrow = sb("csrow", [1, 32], BF16)
        csevf = sb("csevf", [1, 32], F32)
        ctabs = [sb(f"ctab{h}", [128, 192], F32) for h in range(2)]
        prels = [sb(f"prel{h}", [128, 192], F32) for h in range(2)]
        totf = sb("totf", [1, 32], F32)
        ceilr = sb("ceilr", [1, 32], F32)
        ceilc = sb("ceilc", [32, 1], F32)
        basec = sb("basec", [32, 1], F32)
        baser = sb("baser", [1, 32], F32)
        basebc = sb("basebc", [128, 32], F32)
        cmp32 = sb("cmp32", [32, NT], F32)
        tcrow = sb("tcrow", [1, NT], F32)
        tcb = sb("tcb", [128, NT], F32)
        hcf = sb("hcf", [128, 32], F32)
        pcol = sb("pcol", [128, 1], F32)
        posi = sb("posi", [128, 1], I32)
        svali = sb("svali", [128, 1], I16)
        svalf = sb("svalf", [128, 1], F32)
        zro16 = sb("zro16", [128, NT], I16)
        vtabs = [sb(f"vtab{h}", [128, NT], F32) for h in range(2)]
        qposs = [sb(f"qpos{h}", [128, NT], F32) for h in range(2)]
        gidxs = [sb(f"gidx{h}", [128, NT], I32) for h in range(2)]
        sidxs = [sb(f"sidx{h}", [128, NT], I32) for h in range(2)]
        kvidxs = [sb(f"kvidx{h}", [128, NT], I32) for h in range(2)]
        qrels = [sb(f"qrel{h}", [128, NT], F32) for h in range(2)]
        lraw = sb("lraw", [128, NT], I16)
        hcall = sb("hcall", [128, 192 * 32], F32)
        posi192 = sb("posi192", [128, 192], I32)
        svali192 = sb("svali192", [128, 192], I16)
        lrf = sb("lrf", [128, NT], F32)
        dscr = sb("dscr", [128, NT], F32)
        dscr2 = sb("dscr2", [128, NT], F32)
        qg = sb("qg", [128, 2 * 128], BF16)
        kst = sb("kst", [128, 2 * 256], BF16)
        vst = sb("vst", [128, 2 * 258], BF16)
        qftl = sb("qftl", [128, 2 * 128], BF16)
        expt = sb("expt", [128, 2 * 256], BF16)
        probs = sb("probs", [128, 2 * 256], BF16)
        ptev = sb("ptev", [128, 2 * 256], BF16)
        ostg = sb("ostg", [128, 2 * 129], F32)
        comb = sb("comb", [128, 4 * 129], F32)
        osum = sb("osum", [128, 129], F32)
        s01 = sb("s01", [128, 129], F32)
        s23 = sb("s23", [128, 129], F32)
        rz = sb("rz", [128, 1], F32)
        ofin = sb("ofin", [128, 128], BF16)
        otev = sb("otev", [128, 128], BF16)

        block = es.enter_context(nc.Block())
        p = Prog(nc)
        sy, te, ve, sc, gp = (p.on("sync"), p.on("tensor"), p.on("vector"),
                              p.on("scalar"), p.on("gpsimd"))

        for nm, _, _ in TABLE_SPECS:
            sy.dma_start(out=tb[nm][:], in_=tby[nm])
        sy.dma_start(out=wstg[:].rearrange("p (c f) -> p c f", f=256),
                     in_=WQ.rearrange("(c p) f -> p c f", p=128))
        p.bar()
        ve.tensor_copy(wqb[:], wstg[:])
        p.bar()
        sy.dma_start(out=wstg[:, 0:2048].rearrange("p (c f) -> p c f", f=128),
                     in_=WK.rearrange("(c p) f -> p c f", p=128))
        p.bar()
        ve.tensor_copy(wkb[:], wstg[:, 0:2048])
        p.bar()
        sy.dma_start(out=wstg[:, 0:2048].rearrange("p (c f) -> p c f", f=128),
                     in_=WV.rearrange("(c p) f -> p c f", p=128))
        p.bar()
        ve.tensor_copy(wvb[:], wstg[:, 0:2048])
        ve.tensor_copy(svali192[:], tb["svaltab"][:])
        ve.memset(vev[:, 128:129], 1.0)
        ve.memset(zro16[:], 0)
        ve.memset(kmacc[:], 0.0)
        p.bar()
        for hh in range(2):
            sy.dma_start(out=LISTs[hh].rearrange("(t p) one -> t (p one)", p=128)[0:128, :],
                         in_=zro16[:, 0:128])
            sy.dma_start(out=LISTs[hh].rearrange("(t p) one -> t (p one)", p=128)[128:224, :],
                         in_=zro16[0:96, 0:128])
        p.bar()

        with nc.psum_tensor("bk", [128, 128], F32) as bk, \
             nc.psum_tensor("bv", [128, 128], F32) as bv, \
             nc.psum_tensor("bqa", [128, 128], F32) as bqa, \
             nc.psum_tensor("bqb", [128, 128], F32) as bqb, \
             nc.psum_tensor("bkT", [128, 128], BF16) as bkT:
            for t in range(64):
                sy.dma_start(out=xnt_t[:].rearrange("p (c f) -> p c f", f=128),
                             in_=XNT[:, t * 128:(t + 1) * 128].rearrange(
                                 "(c p) f -> p c f", p=128))
                sc.dma_start(out=ctt[:], in_=COST[t * 128:(t + 1) * 128, :])
                sc.dma_start(out=stt_[:], in_=SINT[t * 128:(t + 1) * 128, :])
                sc.dma_start(out=cttu[:], in_=COSU[t * 128:(t + 1) * 128, :])
                sc.dma_start(out=sttu[:], in_=SINU[t * 128:(t + 1) * 128, :])
                p.bar()
                for j in range(16):
                    st, sp = (j == 0), (j == 15)
                    xb = xnt_t[:, j * 128:(j + 1) * 128]
                    te.matmul(bk[:], xb, wkb[:, j * 128:(j + 1) * 128],
                              start=st, stop=sp)
                    te.matmul(bv[:], xb, wvb[:, j * 128:(j + 1) * 128],
                              start=st, stop=sp)
                    te.matmul(bqa[:], xb, wqb[:, j * 256:j * 256 + 128],
                              start=st, stop=sp)
                    te.matmul(bqb[:], xb, wqb[:, j * 256 + 128:(j + 1) * 256],
                              start=st, stop=sp)
                p.bar()
                ve.tensor_mul(rop[:, 0:64], bk[:, 0:64], cttu[:])
                p.bar()
                ve.tensor_mul(rop[:, 64:128], bk[:, 64:128], cttu[:])
                p.bar()
                ve.tensor_mul(sc1[:, 0:64], bk[:, 64:128], sttu[:])
                p.bar()
                ve.tensor_mul(sc2[:, 0:64], bk[:, 0:64], sttu[:])
                p.bar()
                ve.tensor_sub(rop[:, 0:64], rop[:, 0:64], sc1[:, 0:64])
                p.bar()
                ve.tensor_add(rop[:, 64:128], rop[:, 64:128], sc2[:, 0:64])
                p.bar()
                ve.tensor_copy(ktokb[:], rop[:])
                p.bar()
                te.transpose(bkT[:], ktokb[:], tb["eye_bf"][:])
                p.bar()
                ve.tensor_copy(kfev[:], bkT[:])
                p.bar()
                ve.reduce_sum(kmacc[:, t:t + 1], kfev[:], axis=AX.X)
                ve.tensor_copy(kbev[:].rearrange("d (s k) -> d s k", s=2),
                               kfev[:].rearrange("d (k s) -> d s k", s=2))
                p.bar()
                ch_, hf = t // 2, t % 2
                sy.dma_start(
                    out=KBL[ch_ * 128:(ch_ + 1) * 128, :].rearrange(
                        "d (s k) -> d s k", s=2)[:, :, hf * 64:(hf + 1) * 64],
                    in_=kbev[:].rearrange("d (s k) -> d s k", s=2))
                ve.tensor_copy(vev[:, 0:128], bv[:])
                p.bar()
                sy.dma_start(
                    out=VP[t * 64:(t + 1) * 64, :].rearrange(
                        "r (s c) -> (r s) c", s=2),
                    in_=vev[:])
                for h, bq in ((0, bqa), (1, bqb)):
                    ve.tensor_mul(rop[:, 0:64], bq[:, 0:64], ctt[:])
                    p.bar()
                    ve.tensor_mul(rop[:, 64:128], bq[:, 64:128], ctt[:])
                    p.bar()
                    ve.tensor_mul(sc1[:, 0:64], bq[:, 64:128], stt_[:])
                    p.bar()
                    ve.tensor_mul(sc2[:, 0:64], bq[:, 0:64], stt_[:])
                    p.bar()
                    ve.tensor_sub(rop[:, 0:64], rop[:, 0:64], sc1[:, 0:64])
                    p.bar()
                    ve.tensor_add(rop[:, 64:128], rop[:, 64:128], sc2[:, 0:64])
                    p.bar()
                    ve.tensor_copy(qsbs[h][:, t * 128:(t + 1) * 128], rop[:])
                    p.bar()
                    sy.dma_start(out=QDs[h][t * 128:(t + 1) * 128, :],
                                 in_=qsbs[h][:, t * 128:(t + 1) * 128])
                p.bar()
        ve.tensor_add(dscr[:, 0:32], kmacc[:, 0:64:2], kmacc[:, 1:64:2])
        p.bar()
        ve.tensor_scalar(kmean[:], dscr[:, 0:32], 1.0 / 256, None, op0=ALU.mult)
        p.bar()

        with nc.psum_tensor("pqf", [128, 128], BF16) as pqf, \
             nc.psum_tensor("pg", [128, 32], F32) as pg, \
             nc.psum_tensor("pcs", [1, 32], F32) as pcs, \
             nc.psum_tensor("pr", [128, 32], F32) as pr, \
             nc.psum_tensor("ptot", [128, 32], F32) as ptot, \
             nc.psum_tensor("ptc", [128, NT], F32) as ptc, \
             nc.psum_tensor("ptr", [128, 128], F32) as ptr:
            for h in range(2):
                for qt in range(64):
                    qc = qt // 2
                    te.transpose(pqf[:], qsbs[h][:, qt * 128:(qt + 1) * 128],
                                 tb["eye_bf"][:])
                    p.bar()
                    ve.tensor_copy(qft[:], pqf[:])
                    p.bar()
                    te.matmul(pg[:], qft[:], kmean[:], start=True, stop=True)
                    if qt % 2 == 0:
                        ve.tensor_scalar(tmask[:], tb["iota32"][:], qc - 0.5, NEG,
                                         op0=ALU.is_ge, op1=ALU.mult)
                    p.bar()
                    ve.tensor_add(gg[:], pg[:], tmask[:])
                    p.bar()
                    ve.max(out=m8[:], in_=gg[:])
                    p.bar()
                    for j in (1, 2, 3):
                        cjc = m8[:, j - 1:j]
                        ve.tensor_scalar(hraw[:], gg[:], cjc, None, op0=ALU.is_equal)
                        p.bar()
                        ve.tensor_mul(escr[:], hraw[:], tb["iota32"][:])
                        ve.tensor_scalar(isbad[:], cjc, -1.0e29, None, op0=ALU.is_le)
                        p.bar()
                        ve.reduce_sum(idxc[:], escr[:], axis=AX.X)
                        p.bar()
                        ve.select(idxc[:], isbad[:],
                                  tb["cval"][:, qc + j:qc + j + 1], idxc[:])
                        p.bar()
                        ve.tensor_copy(
                            ctabs[h][:, qt * 3 + j - 1:qt * 3 + j], idxc[:])
                        ve.tensor_scalar(hcb[:, (j - 1) * 32:j * 32],
                                         tb["iota32"][:], idxc[:], None,
                                         op0=ALU.is_equal)
                        p.bar()
                    ve.tensor_add(hall[:], hcb[:, 0:32], hcb[:, 32:64])
                    p.bar()
                    ve.tensor_add(hall[:], hall[:], hcb[:, 64:96])
                    p.bar()
                    te.matmul(pr[:], tb["lt_bf"][:], hall[:], start=True, stop=True)
                    p.bar()
                    ve.tensor_copy(ssum[:], pr[:])
                    p.bar()
                    if qt > 0:
                        ve.tensor_add(ssum[:], ssum[:], ptot[:])
                        p.bar()
                    for j in (1, 2, 3):
                        ve.tensor_mul(hcf[:], hcb[:, (j - 1) * 32:j * 32], ssum[:])
                        p.bar()
                        ve.reduce_sum(prels[h][:, qt * 3 + j - 1:qt * 3 + j],
                                      hcf[:], axis=AX.X)
                        p.bar()
                    for j in (1, 2, 3):
                        te.matmul(pcs[0:1, :], tb["ones_col_bf"][:],
                                  hcb[:, (j - 1) * 32:j * 32],
                                  start=(j == 1), stop=(j == 3))
                    p.bar()
                    ve.tensor_copy(csevf[0:1, :], pcs[0:1, :])
                    p.bar()
                    ve.tensor_copy(csrow[0:1, :], csevf[0:1, :])
                    p.bar()
                    te.matmul(ptot[:], tb["ones_1x128_bf"][:], csrow[0:1, :],
                              start=(qt == 0), stop=(qt == 63))
                    p.bar()
                ve.tensor_copy(totf[0:1, :], ptot[0:1, :])
                p.bar()
                ve.tensor_scalar(ceilr[0:1, :], totf[0:1, :], 1.0 / 128,
                                 0.49609375, op0=ALU.mult, op1=ALU.add)
                p.bar()
                ve.tensor_scalar(ceilr[0:1, :], ceilr[0:1, :], MAGIC, None,
                                 op0=ALU.add)
                p.bar()
                ve.tensor_scalar(ceilr[0:1, :], ceilr[0:1, :], MAGIC, 128.0,
                                 op0=ALU.subtract, op1=ALU.mult)
                p.bar()
                te.matmul(pg[0:32, 0:1], ceilr[0:1, :], tb["ones_1x1_f32"][:],
                          start=True, stop=True)
                p.bar()
                ve.tensor_copy(ceilc[:], pg[0:32, 0:1])
                p.bar()
                te.matmul(pr[0:32, 0:1], tb["ut32_f32"][:], ceilc[:],
                          start=True, stop=True)
                p.bar()
                ve.tensor_copy(basec[:], pr[0:32, 0:1])
                p.bar()
                te.matmul(pg[0:1, 0:32], basec[:], tb["eye32_f32"][:],
                          start=True, stop=True)
                p.bar()
                ve.tensor_copy(baser[0:1, :], pg[0:1, 0:32])
                p.bar()
                te.matmul(pr[:], tb["ones_1x128_f32"][:], baser[0:1, :],
                          start=True, stop=True)
                p.bar()
                ve.tensor_copy(basebc[:], pr[:])
                ve.tensor_scalar(cmp32[:], tb["bnd32"][:], basec[:], None,
                                 op0=ALU.is_ge)
                p.bar()
                te.matmul(ptc[0:1, :], tb["ones_32c_f32"][:], cmp32[:],
                          start=True, stop=True)
                p.bar()
                ve.tensor_scalar(tcrow[0:1, :], ptc[0:1, :], 1.0, None,
                                 op0=ALU.subtract)
                p.bar()
                te.matmul(ptc[:], tb["ones_1x128_f32"][:], tcrow[0:1, :],
                          start=True, stop=True)
                p.bar()
                ve.tensor_copy(tcb[:], ptc[:])
                p.bar()
                hc3 = hcall[:].rearrange("p (t k) -> p t k", k=32)
                i3 = tb["iota32"][:].unsqueeze(1).to_broadcast([128, 192, 32])
                c3 = ctabs[h][:].unsqueeze(2).to_broadcast([128, 192, 32])
                b3 = basebc[:].unsqueeze(1).to_broadcast([128, 192, 32])
                ve.tensor_tensor(hc3, i3, c3, op=ALU.is_equal)
                p.bar()
                ve.tensor_mul(hc3, hc3, b3)
                p.bar()
                ve.tensor_reduce(dscr[:, 0:192], hc3, axis=AX.X, op=ALU.add)
                p.bar()
                ve.tensor_add(dscr[:, 0:192], dscr[:, 0:192], prels[h][:])
                p.bar()
                ve.tensor_copy(posi192[:], dscr[:, 0:192])
                p.bar()
                for cix in range(192):
                    gp.indirect_dma_start(
                        out=LISTs[h],
                        out_offset=bass.IndirectOffsetOnAxis(
                            posi192[:, cix:cix + 1], 0),
                        in_=svali192[:, cix:cix + 1],
                        in_offset=None,
                        oob_is_err=False)
                p.bar()
                for blk, nblk in ((0, 128), (1, NT - 128)):
                    sy.dma_start(out=lraw[0:nblk, 0:128],
                                 in_=LISTs[h].rearrange("(t p) 1 -> t p", p=128)[
                                     blk * 128:blk * 128 + nblk, :])
                    p.bar()
                    ve.tensor_copy(lrf[0:nblk, 0:128], lraw[0:nblk, 0:128])
                    p.bar()
                    te.transpose(ptr[:, 0:nblk], lrf[0:nblk, 0:128],
                                 tb["eye_f32"][0:nblk, 0:nblk])
                    p.bar()
                    ve.tensor_copy(vtabs[h][:, blk * 128:blk * 128 + nblk],
                                   ptr[:, 0:nblk])
                    p.bar()
                vt = vtabs[h]
                ve.tensor_scalar(dscr[:], vt[:], 1.5, 0.25,
                                 op0=ALU.subtract, op1=ALU.mult)
                p.bar()
                ve.tensor_scalar(dscr[:], dscr[:], MAGIC, None, op0=ALU.add)
                p.bar()
                ve.tensor_scalar(qposs[h][:], dscr[:], MAGIC, None, op0=ALU.subtract)
                p.bar()
                ve.tensor_scalar_max(dscr[:], qposs[h][:], 0.0)
                p.bar()
                ve.tensor_copy(gidxs[h][:], dscr[:])
                ve.tensor_scalar(dscr[:], vt[:], 0.5, 1.0e7,
                                 op0=ALU.is_le, op1=ALU.mult)
                p.bar()
                ve.tensor_add(dscr2[:], vt[:], dscr[:])
                p.bar()
                ve.tensor_copy(sidxs[h][:], dscr2[:])
                ve.tensor_scalar(dscr[:], tcb[:], 128.0, None, op0=ALU.mult)
                p.bar()
                ve.tensor_add(dscr[:], dscr[:],
                              tb["iota128c"][:].to_broadcast([128, NT]))
                p.bar()
                ve.tensor_copy(kvidxs[h][:], dscr[:])
                ve.tensor_scalar(dscr2[:], tcb[:], 256.0, None, op0=ALU.mult)
                p.bar()
                ve.tensor_sub(qrels[h][:], qposs[h][:], dscr2[:])
                p.bar()

        if debug == 1:
            ve.tensor_copy(dscr[:], gidxs[0][:])
            p.bar()
            sy.dma_start(out=DBG[:, 0:NT], in_=dscr[:])
            ve.tensor_copy(dscr[:], sidxs[0][:])
            p.bar()
            sy.dma_start(out=DBG[:, NT:2 * NT], in_=dscr[:])
            ve.tensor_copy(dscr[:], kvidxs[0][:])
            p.bar()
            sy.dma_start(out=DBG[:, 2 * NT:3 * NT], in_=dscr[:])
            sy.dma_start(out=DBG[:, 3 * NT:4 * NT], in_=qrels[0][:])
            sy.dma_start(out=DBG[:, 4 * NT:5 * NT], in_=vtabs[0][:])
            sy.dma_start(out=DBG[:, 5 * NT:6 * NT], in_=tcb[:])
            ve.tensor_copy(gg[:], kmean[:])
            p.bar()
            sy.dma_start(out=DBGK, in_=gg[:])
            p.finish()
            p.replay(block)
            return nc

        with nc.psum_tensor("bqf0", [128, 128], BF16) as bqf0, \
             nc.psum_tensor("bqf1", [128, 128], BF16) as bqf1, \
             nc.psum_tensor("bs0", [128, 256], F32) as bs0, \
             nc.psum_tensor("bs1", [128, 256], F32) as bs1, \
             nc.psum_tensor("bt0", [128, 256], BF16) as bt0, \
             nc.psum_tensor("bt1", [128, 256], BF16) as bt1, \
             nc.psum_tensor("bo0", [128, 129], F32) as bo0, \
             nc.psum_tensor("bo1", [128, 129], F32) as bo1:
            bss, bts, bos = [bs0, bs1], [bt0, bt1], [bo0, bo1]
            bqfs = [bqf0, bqf1]
            hist = {k: {} for k in ("gath", "qft", "smm", "exp", "prb",
                                    "ptv", "oev", "sct")}

            def attn_unit(u, h, kind, idx):
                par = u % 2
                qgp = qg[:, par * 128:(par + 1) * 128]
                kstp = kst[:, par * 256:(par + 1) * 256]
                vstp = vst[:, par * 258:(par + 1) * 258]
                qftp = qftl[:, par * 128:(par + 1) * 128]
                exptp = expt[:, par * 256:(par + 1) * 256]
                prbp = probs[:, par * 256:(par + 1) * 256]
                ptevp = ptev[:, par * 256:(par + 1) * 256]
                ostgp = ostg[:, par * 129:(par + 1) * 129]
                pqfp = bqfs[par][:]

                w = hist["sct"].get(u - 2)
                if kind == "sel":
                    if w is not None:
                        p.wait("gpsimd", w)
                    gp.indirect_dma_start(
                        out=qgp, out_offset=None, in_=QDs[h],
                        in_offset=bass.IndirectOffsetOnAxis(
                            gidxs[h][:, idx:idx + 1], 0))
                    gp.indirect_dma_start(
                        out=kstp, out_offset=None, in_=KBL,
                        in_offset=bass.IndirectOffsetOnAxis(
                            kvidxs[h][:, idx:idx + 1], 0))
                    gp.indirect_dma_start(
                        out=vstp, out_offset=None, in_=VP,
                        in_offset=bass.IndirectOffsetOnAxis(
                            kvidxs[h][:, idx:idx + 1], 0))
                else:
                    c0 = idx // 2
                    if w is not None:
                        p.wait("sync", w)
                        p.wait("scalar", w)
                    sy.dma_start(out=kstp, in_=KBL[c0 * 128:(c0 + 1) * 128, :])
                    sc.dma_start(out=vstp, in_=VP[c0 * 128:(c0 + 1) * 128, :])
                hist["gath"][u] = p.snap()
                if safe:
                    p.bar()
                if cut <= 1:
                    for k in hist:
                        hist[k][u] = p.snap()
                    return

                p.wait("tensor", hist["gath"][u])
                if u >= 2:
                    p.wait("tensor", hist["qft"][u - 2])
                if kind == "sel":
                    te.transpose(pqfp, qgp, tb["eye_bf"][:])
                else:
                    te.transpose(pqfp, qsbs[h][:, idx * 128:(idx + 1) * 128],
                                 tb["eye_bf"][:])
                s_tp = p.snap()
                p.wait("vector", s_tp)
                ve.tensor_copy(qftp, pqfp)
                hist["qft"][u] = p.snap()
                if safe:
                    p.bar()
                if cut <= 2:
                    for k in ("smm", "exp", "prb", "ptv", "oev", "sct"):
                        hist[k][u] = p.snap()
                    return

                p.wait("tensor", hist["qft"][u])
                if u >= 2:
                    p.wait("tensor", hist["exp"][u - 2])
                te.matmul(bss[par][:], qftp, kstp, start=True, stop=True)
                hist["smm"][u] = p.snap()
                if safe:
                    p.bar()
                if cut <= 3:
                    for k in ("exp", "prb", "ptv", "oev", "sct"):
                        hist[k][u] = p.snap()
                    return

                p.wait("scalar", hist["smm"][u])
                sc.activation(out=exptp, in_=bss[par][:], func=AF.Exp)
                hist["exp"][u] = p.snap()
                if safe:
                    p.bar()
                if cut <= 4:
                    for k in ("prb", "ptv", "oev", "sct"):
                        hist[k][u] = p.snap()
                    return

                p.wait("vector", hist["exp"][u])
                if kind == "sel":
                    ve.scalar_tensor_tensor(
                        out=prbp, in0=tb["keyloc_bf"][:],
                        scalar=qrels[h][:, idx:idx + 1], in1=exptp,
                        op0=ALU.is_le, op1=ALU.mult)
                else:
                    ql = tb["qloc0"] if idx % 2 == 0 else tb["qloc1"]
                    ve.scalar_tensor_tensor(
                        out=prbp, in0=tb["keyloc_bf"][:], scalar=ql[:],
                        in1=exptp, op0=ALU.is_le, op1=ALU.mult)
                hist["prb"][u] = p.snap()
                if safe:
                    p.bar()
                if cut <= 5:
                    for k in ("ptv", "oev", "sct"):
                        hist[k][u] = p.snap()
                    return

                p.wait("tensor", hist["prb"][u])
                if u >= 2:
                    p.wait("tensor", hist["ptv"][u - 2])
                te.transpose(bts[par][:, 0:128], prbp[:, 0:128], tb["eye_bf"][:])
                te.transpose(bts[par][:, 128:256], prbp[:, 128:256], tb["eye_bf"][:])
                s_pt = p.snap()
                p.wait("vector", s_pt)
                ve.tensor_copy(ptevp, bts[par][:])
                hist["ptv"][u] = p.snap()
                if safe:
                    p.bar()
                if cut <= 6:
                    for k in ("oev", "sct"):
                        hist[k][u] = p.snap()
                    return

                p.wait("tensor", hist["ptv"][u])
                if u >= 2:
                    p.wait("tensor", hist["oev"][u - 2])
                te.matmul(bos[par][:], ptevp[:, 0:128], vstp[:, 0:129],
                          start=True, stop=False)
                te.matmul(bos[par][:], ptevp[:, 128:256], vstp[:, 129:258],
                          start=False, stop=True)
                s_pv = p.snap()
                p.wait("vector", s_pv)
                if u >= 2:
                    p.wait("vector", hist["sct"][u - 2])
                ve.tensor_copy(ostgp, bos[par][:])
                hist["oev"][u] = p.snap()
                if safe:
                    p.bar()
                if cut <= 7:
                    hist["sct"][u] = p.snap()
                    return

                if kind == "sel":
                    p.wait("gpsimd", hist["oev"][u])
                    gp.indirect_dma_start(
                        out=OSs[h], out_offset=bass.IndirectOffsetOnAxis(
                            sidxs[h][:, idx:idx + 1], 0),
                        in_=ostgp, in_offset=None, oob_is_err=False)
                else:
                    p.wait("sync", hist["oev"][u])
                    q0 = idx * 128
                    sy.dma_start(
                        out=OSs[h][4 * q0:4 * q0 + 512, :].rearrange(
                            "(q j) e -> q j e", j=4)[:, 0, :],
                        in_=ostgp)
                hist["sct"][u] = p.snap()
                if safe:
                    p.bar()

            u = 0
            for h in range(2):
                for idx in range(nown):
                    attn_unit(u, h, "own", idx)
                    u += 1
                for idx in range(nsel):
                    attn_unit(u, h, "sel", idx)
                    u += 1
            p.bar()

            for h in range(2 if debug != 2 else 0):
                for qt in range(64):
                    sy.dma_start(out=comb[:].rearrange("q (j e) -> q j e", e=129),
                                 in_=OSs[h][qt * 512:(qt + 1) * 512, :].rearrange(
                                     "(q j) e -> q j e", j=4))
                    p.bar()
                    ve.tensor_add(s01[:], comb[:, 0:129], comb[:, 129:258])
                    p.bar()
                    ve.tensor_add(s23[:], comb[:, 258:387], comb[:, 387:516])
                    p.bar()
                    ve.tensor_add(osum[:], s01[:], s23[:])
                    p.bar()
                    ve.reciprocal(rz[:], osum[:, 128:129])
                    p.bar()
                    ve.tensor_scalar(ofin[:], osum[:, 0:128], rz[:], None,
                                     op0=ALU.mult)
                    p.bar()
                    te.transpose(bqf0[:], ofin[:], tb["eye_bf"][:])
                    p.bar()
                    ve.tensor_copy(otev[:], bqf0[:])
                    p.bar()
                    sy.dma_start(out=OT[h * 128:(h + 1) * 128,
                                        qt * 128:(qt + 1) * 128], in_=otev[:])
                    p.bar()
        p.finish()
        p.replay(block)
    return nc


def stage_l2(inputs, xnt_full):
    t = host_tables()
    wq = np.asarray(inputs["L0_attn_wq"], np.float32)
    wk = np.asarray(inputs["L0_attn_wk"], np.float32)
    wv = np.asarray(inputs["L0_attn_wv"], np.float32)
    maps = []
    for c in range(NCORES):
        m = {
            "xnt_full": xnt_full,
            "wq2": np.ascontiguousarray(wq[:, 2 * c * 128:(2 * c + 2) * 128]),
            "wk1": np.ascontiguousarray(wk[:, (c // 2) * 128:(c // 2 + 1) * 128]),
            "wv1": np.ascontiguousarray(wv[:, (c // 2) * 128:(c // 2 + 1) * 128]),
            "cosT": t["cosT"], "sinT": t["sinT"],
            "cosTu": t["cosTu"], "sinTu": t["sinTu"],
        }
        for nm, _, _ in TABLE_SPECS:
            m[nm] = t[nm]
        maps.append(m)
    return maps


def kernel(**inputs):
    raise NotImplementedError



def build_l3():
    nc = bass.Bass("TRN2", target_bir_lowering=False)
    X = nc.dram_tensor("x_rows", [TPC, D], F32, kind="ExternalInput").ap()
    XNS = nc.dram_tensor("xn_sl", [TPC, D], BF16, kind="ExternalInput").ap()
    OTF = nc.dram_tensor("ot_full", [D, TPC], BF16, kind="ExternalInput").ap()
    WO = nc.dram_tensor("wo", [D, D], F32, kind="ExternalInput").ap()
    W1 = nc.dram_tensor("w1", [DFF, D], F32, kind="ExternalInput").ap()
    B1 = nc.dram_tensor("b1", [1, DFF], F32, kind="ExternalInput").ap()
    W2 = nc.dram_tensor("w2", [D, DFF], F32, kind="ExternalInput").ap()
    B2 = nc.dram_tensor("b2", [1, D], F32, kind="ExternalInput").ap()
    LNW = nc.dram_tensor("lnw", [1, D], F32, kind="ExternalInput").ap()
    LNB = nc.dram_tensor("lnb", [1, D], F32, kind="ExternalInput").ap()
    EYE = nc.dram_tensor("eye_bf", [128, 128], BF16, kind="ExternalInput").ap()
    OUT = nc.dram_tensor("out_rows", [TPC, D], F32, kind="ExternalOutput").ap()

    W1QT = nc.dram_tensor("w1qt", [D, DFF], BF16, kind="Internal").ap()
    W2QT = nc.dram_tensor("w2qt", [DFF, D], BF16, kind="Internal").ap()
    HQT = nc.dram_tensor("hqt", [D, TPC], BF16, kind="Internal").ap()
    Y1QT = nc.dram_tensor("y1qt", [DFF, TPC], BF16, kind="Internal").ap()
    Y2 = nc.dram_tensor("y2", [TPC, D], F32, kind="Internal").ap()

    es = contextlib.ExitStack()

    def sb(name, shape, dt):
        return es.enter_context(nc.sbuf_tensor(name, shape, dt))

    with es:
        eye = sb("eye", [128, 128], BF16)
        wld = sb("wld", [128, 4096], F32)
        qscr = sb("qscr", [128, 4096], F32)
        amx = sb("amx", [128, 128], F32)
        amx2 = sb("amx2", [128, 128], F32)
        qout = sb("qout", [128, 4096], BF16)
        tstage = sb("tstage", [128, 4096], BF16)
        hqtile = sb("hqtile", [128, 2048], BF16)
        xf = sb("xf", [128, D], F32)
        stat = sb("stat", [128, 4], F32)
        wbc = sb("wbc", [128, D], F32)
        bbc = sb("bbc", [128, D], F32)
        b1bc = sb("b1bc", [128, 2048], F32)
        b2bc = sb("b2bc", [128, D], F32)
        hb = sb("hb", [128, D], BF16)
        wbig = sb("wbig", [128, 16 * 1024], BF16)
        tb16 = sb("tb16", [128, 1024], BF16)
        g1 = sb("g1", [128, 1024], F32)
        g2 = sb("g2", [128, 1024], F32)
        g3 = sb("g3", [128, 1024], F32)
        y1tile = sb("y1tile", [128, 64 * 128], BF16)
        xt = sb("xt", [128, D], F32)
        y2t = sb("y2t", [128, D], F32)
        fft = sb("fft", [128, D], BF16)
        ot16 = sb("ot16", [128, 16 * 128], BF16)
        outt = sb("outt", [128, D], F32)

        block = es.enter_context(nc.Block())
        p = Prog(nc)
        sy, te, ve, sc, gp = (p.on("sync"), p.on("tensor"), p.on("vector"),
                              p.on("scalar"), p.on("gpsimd"))

        def quant(src, wdt, dst):
            nw = wdt // 32
            src3 = src.rearrange("p (w k) -> p w k", k=32)
            ve.tensor_reduce(amx[:, 0:nw], src3, axis=AX.X, op=ALU.max,
                             apply_absolute_value=True)
            p.bar()
            ve.tensor_scalar_max(amx[:, 0:nw], amx[:, 0:nw], 1e-30)
            p.bar()
            ami = amx[:, 0:nw].bitcast(I32)
            ve.tensor_scalar(ami, ami, 0x7F800000, None, op0=ALU.bitwise_and)
            p.bar()
            am2i = amx2[:, 0:nw].bitcast(I32)
            ve.tensor_scalar(am2i, ami, -1, 2181038080, op0=ALU.mult, op1=ALU.add)
            p.bar()
            ve.tensor_scalar(ami, ami, 50331648, None, op0=ALU.subtract)
            p.bar()
            inv3 = amx2[:, 0:nw].unsqueeze(2).to_broadcast([128, nw, 32])
            scl3 = amx[:, 0:nw].unsqueeze(2).to_broadcast([128, nw, 32])
            t3 = qscr[:, 0:wdt].rearrange("p (w k) -> p w k", k=32)
            ve.tensor_mul(t3, src3, inv3)
            p.bar()
            ve.tensor_scalar(qscr[:, 0:wdt], qscr[:, 0:wdt], MAGIC, None, op0=ALU.add)
            p.bar()
            ve.tensor_scalar(qscr[:, 0:wdt], qscr[:, 0:wdt], MAGIC, None,
                             op0=ALU.subtract)
            p.bar()
            ve.tensor_scalar(qscr[:, 0:wdt], qscr[:, 0:wdt], 127.0, -128.0,
                             op0=ALU.min, op1=ALU.max)
            p.bar()
            ve.tensor_mul(dst.rearrange("p (w k) -> p w k", k=32), t3, scl3)
            p.bar()

        def trans_block(src_bf, n128, dststage):
            for j in range(0, n128, 2):
                te.transpose(tp0[:], src_bf[:, j * 128:(j + 1) * 128], eye[:])
                if j + 1 < n128:
                    te.transpose(tp1[:], src_bf[:, (j + 1) * 128:(j + 2) * 128],
                                 eye[:])
                p.bar()
                ve.tensor_copy(dststage[:, j * 128:(j + 1) * 128], tp0[:])
                if j + 1 < n128:
                    ve.tensor_copy(dststage[:, (j + 1) * 128:(j + 2) * 128], tp1[:])
                p.bar()

        sy.dma_start(out=eye[:], in_=EYE)
        sy.dma_start(out=wbc[:], in_=LNW.to_broadcast([128, D]))
        sy.dma_start(out=bbc[:], in_=LNB.to_broadcast([128, D]))
        sy.dma_start(out=b2bc[:], in_=B2.to_broadcast([128, D]))
        p.bar()

        with nc.psum_tensor("tp0", [128, 128], BF16) as tp0, \
             nc.psum_tensor("tp1", [128, 128], BF16) as tp1, \
             nc.psum_tensor("mm0", [128, 512], F32) as mm0, \
             nc.psum_tensor("mm1", [128, 512], F32) as mm1:
            mms = [mm0, mm1]

            for i in range(64):
                sy.dma_start(out=wld[:, 0:D], in_=W1[i * 128:(i + 1) * 128, :])
                p.bar()
                quant(wld[:, 0:D], D, qout[:, 0:D])
                trans_block(qout[:, 0:D], 16, tstage)
                sy.dma_start(
                    out=W1QT[:, i * 128:(i + 1) * 128].rearrange(
                        "(c p) f -> p c f", p=128),
                    in_=tstage[:, 0:D].rearrange("p (c f) -> p c f", f=128))
                p.bar()
            for i in range(16):
                for hf in range(2):
                    sy.dma_start(out=wld[:],
                                 in_=W2[i * 128:(i + 1) * 128,
                                        hf * 4096:(hf + 1) * 4096])
                    p.bar()
                    quant(wld[:], 4096, qout[:])
                    trans_block(qout[:], 32, tstage)
                    sy.dma_start(
                        out=W2QT[hf * 4096:(hf + 1) * 4096,
                                 i * 128:(i + 1) * 128].rearrange(
                            "(c p) f -> p c f", p=128),
                        in_=tstage[:].rearrange("p (c f) -> p c f", f=128))
                    p.bar()

            for i in range(NTT):
                sy.dma_start(out=hb[:], in_=XNS[i * 128:(i + 1) * 128, :])
                p.bar()
                ve.tensor_copy(xf[:], hb[:])
                p.bar()
                ve.reduce_sum(stat[:, 0:1], xf[:], axis=AX.X)
                p.bar()
                ve.tensor_scalar(stat[:, 0:1], stat[:, 0:1], 1.0 / D, None,
                                 op0=ALU.mult)
                p.bar()
                ve.tensor_scalar(xf[:], xf[:], stat[:, 0:1], None, op0=ALU.subtract)
                p.bar()
                sc.activation(out=qscr[:, 0:D], in_=xf[:], func=AF.Square)
                p.bar()
                ve.reduce_sum(stat[:, 1:2], qscr[:, 0:D], axis=AX.X)
                p.bar()
                ve.tensor_scalar(stat[:, 1:2], stat[:, 1:2], 1.0 / D, 1e-5,
                                 op0=ALU.mult, op1=ALU.add)
                p.bar()
                sc.activation(out=stat[:, 2:3], in_=stat[:, 1:2], func=AF.Sqrt)
                p.bar()
                ve.reciprocal(stat[:, 3:4], stat[:, 2:3])
                p.bar()
                ve.tensor_scalar(xf[:], xf[:], stat[:, 3:4], None, op0=ALU.mult)
                p.bar()
                ve.tensor_mul(xf[:], xf[:], wbc[:])
                p.bar()
                ve.tensor_add(hb[:], xf[:], bbc[:])
                p.bar()
                quant(hb[:], D, hqtile[:])
                trans_block(hqtile[:], 16, tstage)
                sy.dma_start(
                    out=HQT[:, i * 128:(i + 1) * 128].rearrange(
                        "(c p) f -> p c f", p=128),
                    in_=tstage[:, 0:D].rearrange("p (c f) -> p c f", f=128))
                p.bar()

            for e8 in range(8):
                sy.dma_start(out=wbig[:].rearrange("p (c f) -> p c f", f=1024),
                             in_=W1QT[:, e8 * 1024:(e8 + 1) * 1024].rearrange(
                                 "(c p) f -> p c f", p=128))
                sc.dma_start(out=b1bc[:, 0:1024],
                             in_=B1[:, e8 * 1024:(e8 + 1) * 1024]
                             .to_broadcast([128, 1024]))
                p.bar()
                for i in range(NTT):
                    sy.dma_start(out=hqtile[:].rearrange("p (c f) -> p c f", f=128),
                                 in_=HQT[:, i * 128:(i + 1) * 128].rearrange(
                                     "(c p) f -> p c f", p=128))
                    p.bar()
                    for sub in range(2):
                        mm = mms[sub]
                        for j in range(16):
                            te.matmul(mm[:],
                                      hqtile[:, j * 128:(j + 1) * 128],
                                      wbig[:, j * 1024 + sub * 512:
                                           j * 1024 + sub * 512 + 512],
                                      start=(j == 0), stop=(j == 15))
                    p.bar()
                    ve.tensor_add(tb16[:, 0:512], mms[0][:], b1bc[:, 0:512])
                    ve.tensor_add(tb16[:, 512:1024], mms[1][:], b1bc[:, 512:1024])
                    p.bar()
                    gp.tensor_mul(g1[:], tb16[:], tb16[:])
                    p.bar()
                    ve.tensor_scalar(g1[:], g1[:], 0.044715, 1.0,
                                     op0=ALU.mult, op1=ALU.add)
                    p.bar()
                    gp.tensor_mul(g2[:], tb16[:], g1[:])
                    p.bar()
                    sc.activation(out=g3[:], in_=g2[:], func=AF.Tanh,
                                  scale=GELU_C)
                    p.bar()
                    ve.tensor_scalar(g3[:], g3[:], 1.0, 0.5,
                                     op0=ALU.add, op1=ALU.mult)
                    p.bar()
                    ve.tensor_mul(tb16[:], tb16[:], g3[:])
                    p.bar()
                    quant(tb16[:], 1024, qout[:, 0:1024])
                    trans_block(qout[:, 0:1024], 8, tstage)
                    sy.dma_start(
                        out=Y1QT[e8 * 1024:(e8 + 1) * 1024,
                                 i * 128:(i + 1) * 128].rearrange(
                            "(b pp) f -> pp b f", pp=128),
                        in_=tstage[:, 0:1024].rearrange("p (b f) -> p b f", f=128))
                    p.bar()

            for e2 in range(8):
                sy.dma_start(out=wbig[:].rearrange("p (c f) -> p c f", f=256),
                             in_=W2QT[:, e2 * 256:(e2 + 1) * 256].rearrange(
                                 "(c p) f -> p c f", p=128))
                p.bar()
                for i in range(NTT):
                    sy.dma_start(out=y1tile[:].rearrange("p (c f) -> p c f", f=128),
                                 in_=Y1QT[:, i * 128:(i + 1) * 128].rearrange(
                                     "(c p) f -> p c f", p=128))
                    p.bar()
                    for j in range(64):
                        te.matmul(mms[i % 2][:, 0:256],
                                  y1tile[:, j * 128:(j + 1) * 128],
                                  wbig[:, j * 256:(j + 1) * 256],
                                  start=(j == 0), stop=(j == 63))
                    p.bar()
                    ve.tensor_copy(g1[:, 0:256], mms[i % 2][:, 0:256])
                    p.bar()
                    sy.dma_start(out=Y2[i * 128:(i + 1) * 128,
                                        e2 * 256:(e2 + 1) * 256], in_=g1[:, 0:256])
                    p.bar()

            for half in range(2):
                for cch in range(16):
                    sy.dma_start(out=wld[:, 0:1024],
                                 in_=WO[cch * 128:(cch + 1) * 128,
                                        half * 1024:(half + 1) * 1024])
                    p.bar()
                    ve.tensor_copy(wbig[:, cch * 1024:(cch + 1) * 1024],
                                   wld[:, 0:1024])
                    p.bar()
                for i in range(NTT):
                    sy.dma_start(out=ot16[:].rearrange("p (c f) -> p c f", f=128),
                                 in_=OTF[:, i * 128:(i + 1) * 128].rearrange(
                                     "(c p) f -> p c f", p=128))
                    sc.dma_start(out=xt[:], in_=X[i * 128:(i + 1) * 128, :])
                    sc.dma_start(out=y2t[:], in_=Y2[i * 128:(i + 1) * 128, :])
                    sc.dma_start(out=hb[:], in_=XNS[i * 128:(i + 1) * 128, :])
                    p.bar()
                    ve.tensor_add(fft[:], y2t[:], b2bc[:])
                    p.bar()
                    ve.tensor_add(fft[:], fft[:], hb[:])
                    p.bar()
                    for sub in range(2):
                        mm = mms[sub % 2]
                        for j in range(16):
                            te.matmul(mm[:], ot16[:, j * 128:(j + 1) * 128],
                                      wbig[:, j * 1024 + sub * 512:
                                           j * 1024 + sub * 512 + 512],
                                      start=(j == 0), stop=(j == 15))
                        p.bar()
                        sl = slice(half * 1024 + sub * 512,
                                   half * 1024 + (sub + 1) * 512)
                        ve.tensor_add(outt[:, sl], mm[:], xt[:, sl])
                        p.bar()
                        ve.tensor_add(outt[:, sl], outt[:, sl], fft[:, sl])
                        p.bar()
                    sy.dma_start(out=OUT[i * 128:(i + 1) * 128,
                                         half * 1024:(half + 1) * 1024],
                                 in_=outt[:, half * 1024:(half + 1) * 1024])
                    p.bar()
        p.finish()
        p.replay(block)
    return nc


def stage_l3(inputs, xn_full, ot_full):
    t = host_tables()
    x = np.asarray(inputs["L0_attn_hidden_states"]).reshape(S, D).astype(np.float32)
    maps = []
    for c in range(NCORES):
        maps.append({
            "x_rows": np.ascontiguousarray(x[c * TPC:(c + 1) * TPC]),
            "xn_sl": np.ascontiguousarray(xn_full[c * TPC:(c + 1) * TPC]),
            "ot_full": np.ascontiguousarray(ot_full[:, c * TPC:(c + 1) * TPC]),
            "wo": np.asarray(inputs["L0_attn_wo"], np.float32),
            "w1": np.asarray(inputs["L0_ffn_W1"], np.float32),
            "b1": np.asarray(inputs["L0_ffn_b1"], np.float32).reshape(1, DFF),
            "w2": np.asarray(inputs["L0_ffn_W2"], np.float32),
            "b2": np.asarray(inputs["L0_ffn_b2"], np.float32).reshape(1, D),
            "lnw": np.asarray(inputs["L0_ffn_ln_w"], np.float32).reshape(1, D),
            "lnb": np.asarray(inputs["L0_ffn_ln_b"], np.float32).reshape(1, D),
            "eye_bf": t["eye_bf"],
        })
    return maps


def kernel(**inputs):
    nc1 = build_l1()
    res1 = run_bass_kernel_spmd(nc1, stage_l1(inputs), core_ids=list(range(NCORES)))
    xn_full = np.concatenate([res1.results[c]["xn_sl"] for c in range(NCORES)], axis=0)
    xnt_full = np.concatenate([res1.results[c]["xnt_sl"] for c in range(NCORES)],
                              axis=1)
    nc2 = build_l2(safe=True)
    res2 = run_bass_kernel_spmd(nc2, stage_l2(inputs, xnt_full),
                                core_ids=list(range(NCORES)))
    ot_full = np.concatenate([res2.results[c]["ot_sl"] for c in range(NCORES)],
                             axis=0)
    nc3 = build_l3()
    res3 = run_bass_kernel_spmd(nc3, stage_l3(inputs, xn_full, ot_full),
                                core_ids=list(range(NCORES)))
    out = np.concatenate([res3.results[c]["out_rows"] for c in range(NCORES)],
                         axis=0)
    return out.reshape(1, S, D).astype(np.float32)


# version 04
# speedup vs baseline: 1.1285x; 1.0256x over previous
import contextlib
import numpy as np
import ml_dtypes
import concourse.bass as bass
import concourse.mybir as mybir
from concourse.bass_utils import run_bass_kernel_spmd

F32 = mybir.dt.float32
BF16 = mybir.dt.bfloat16
I32 = mybir.dt.int32
I16 = mybir.dt.int16
AF = mybir.ActivationFunctionType
ALU = mybir.AluOpType
AX = mybir.AxisListType

BF = ml_dtypes.bfloat16

S, D, HD, NH, NKV, CHK, NCHK = 8192, 2048, 128, 16, 4, 256, 32
NCORES = 8
TPC = S // NCORES
NTT = TPC // 128
DFF = 8192
NT = 224
MAGIC = 12582912.0
SCALE_QK = float(HD) ** -0.5
GELU_C = 0.7978845608028654
NEG = -1.0e30


def host_tables():
    t = {}
    t["eye_bf"] = np.eye(128).astype(BF)
    t["eye_f32"] = np.eye(128).astype(np.float32)
    t["lt_bf"] = (np.arange(128)[:, None] < np.arange(128)[None, :]).astype(BF)
    t["ut32_f32"] = (np.arange(32)[:, None] < np.arange(32)[None, :]).astype(np.float32)
    t["eye32_f32"] = np.eye(32).astype(np.float32)
    t["ones_col_bf"] = np.ones((128, 1), BF)
    t["ones_1x128_bf"] = np.ones((1, 128), BF)
    t["ones_1x128_f32"] = np.ones((1, 128), np.float32)
    t["ones_32c_f32"] = np.ones((32, 1), np.float32)
    t["ones_1x1_f32"] = np.ones((1, 1), np.float32)
    t["iota32"] = np.broadcast_to(np.arange(32, dtype=np.float32), (128, 32)).copy()
    t["iota128c"] = np.arange(128, dtype=np.float32).reshape(128, 1)
    col = np.arange(256)
    keyloc = (2 * (col % 128) + col // 128).astype(BF)
    t["keyloc_bf"] = np.broadcast_to(keyloc, (128, 256)).copy()
    t["cval"] = np.broadcast_to(np.arange(40, dtype=np.float32), (128, 40)).copy()
    t["bnd32"] = np.broadcast_to(128.0 * np.arange(NT, dtype=np.float32),
                                 (32, NT)).copy()
    sv = np.zeros((128, 192), np.float32)
    for qt in range(64):
        for j in (1, 2, 3):
            sv[:, qt * 3 + j - 1] = 4 * (qt * 128 + np.arange(128)) + j
    t["svaltab"] = sv
    t["qloc0"] = np.arange(128, dtype=np.float32).reshape(128, 1)
    t["qloc1"] = (np.arange(128, dtype=np.float32) + 128).reshape(128, 1)
    pos = np.arange(S, dtype=np.float64)
    inv = 1.0 / (10000.0 ** (np.arange(0, HD, 2, dtype=np.float64) / HD))
    ang = pos[:, None] * inv[None, :]
    t["cosT"] = (np.cos(ang) * SCALE_QK).astype(np.float32)
    t["sinT"] = (np.sin(ang) * SCALE_QK).astype(np.float32)
    t["cosTu"] = np.cos(ang).astype(np.float32)
    t["sinTu"] = np.sin(ang).astype(np.float32)
    return t


class Prog:

    ENGS = ("sync", "tensor", "vector", "scalar", "gpsimd")

    def __init__(self, nc):
        self.nc = nc
        self._sem_cm = nc.semaphore("bar")
        self.sem = self._sem_cm.__enter__()
        self.count = 0
        self.ops = {e: [] for e in self.ENGS}
        self.pending = {}
        self.wait_at = {e: None for e in self.ENGS}

    class _H:
        def __init__(self, p, name):
            object.__setattr__(self, "p", p)
            object.__setattr__(self, "name", name)

        def __getattr__(self, attr):
            p, name = self.p, self.name

            def wrapped(*a, **kw):
                if p.wait_at[name] is not None:
                    p.ops[name].append(("wait", p.wait_at[name]))
                    p.wait_at[name] = None
                isdma = attr in ("dma_start", "indirect_dma_start")
                p.ops[name].append(["op", attr, a, kw, 16 if isdma else 0])
                if isdma:
                    p.count += 16
                else:
                    p.pending[name] = len(p.ops[name]) - 1
            return wrapped

    def on(self, name):
        return Prog._H(self, name)

    def bar(self):
        for name, idx in self.pending.items():
            self.ops[name][idx][4] += 1
            self.count += 1
        self.pending = {}
        for e in self.ENGS:
            self.wait_at[e] = self.count

    def snap(self):
        for name, idx in list(self.pending.items()):
            self.ops[name][idx][4] += 1
            self.count += 1
        self.pending = {}
        return self.count

    def wait(self, eng, val):
        self.ops[eng].append(("wait", val))
        if self.wait_at[eng] is not None and self.wait_at[eng] <= val:
            self.wait_at[eng] = None

    def finish(self, eng="sync"):
        self.bar()
        self.ops[eng].append(("wait", self.count))
        self.wait_at[eng] = None

    def replay(self, block):
        def mk(name):
            def run(handle):
                eng = getattr(self.nc, name)
                for op in self.ops[name]:
                    if op[0] == "wait":
                        eng.wait_ge(self.sem, op[1])
                    else:
                        _, attr, a, kw, inc = op
                        ins = getattr(eng, attr)(*a, **kw)
                        if inc:
                            ins.then_inc(self.sem, inc)
            return run
        block.sync(mk("sync"))
        block.tensor(mk("tensor"))
        block.vector(mk("vector"))
        block.scalar(mk("scalar"))
        block.gpsimd(mk("gpsimd"))



def build_l1():
    nc = bass.Bass("TRN2", target_bir_lowering=False)
    X = nc.dram_tensor("x_rows", [TPC, D], F32, kind="ExternalInput").ap()
    EYE = nc.dram_tensor("eye_bf", [128, 128], BF16, kind="ExternalInput").ap()
    XN = nc.dram_tensor("xn_sl", [TPC, D], BF16, kind="ExternalOutput").ap()
    XNT = nc.dram_tensor("xnt_sl", [D, TPC], BF16, kind="ExternalOutput").ap()

    with (
        nc.sbuf_tensor("xt", [128, D], F32) as xt,
        nc.sbuf_tensor("scr", [128, D], F32) as scr,
        nc.sbuf_tensor("stat", [128, 4], F32) as stat,
        nc.sbuf_tensor("xnb", [128, D], BF16) as xnb,
        nc.sbuf_tensor("eye", [128, 128], BF16) as eye,
        nc.sbuf_tensor("tev", [128, 2048], BF16) as tev,
        nc.psum_tensor("pt0", [128, 128], BF16) as pt0,
        nc.psum_tensor("pt1", [128, 128], BF16) as pt1,
        nc.Block() as block,
    ):
        p = Prog(nc)
        sy, ve, sc, te = p.on("sync"), p.on("vector"), p.on("scalar"), p.on("tensor")
        sy.dma_start(out=eye[:], in_=EYE)
        p.bar()
        for i in range(NTT):
            sy.dma_start(out=xt[:], in_=X[i * 128:(i + 1) * 128, :])
            p.bar()
            ve.reduce_sum(stat[:, 0:1], xt[:], axis=AX.X)
            p.bar()
            ve.tensor_scalar(stat[:, 0:1], stat[:, 0:1], 1.0 / D, None, op0=ALU.mult)
            p.bar()
            ve.tensor_scalar(scr[:], xt[:], stat[:, 0:1], None, op0=ALU.subtract)
            p.bar()
            sc.activation(out=xt[:], in_=scr[:], func=AF.Square)
            p.bar()
            ve.reduce_sum(stat[:, 1:2], xt[:], axis=AX.X)
            p.bar()
            ve.tensor_scalar(stat[:, 1:2], stat[:, 1:2], 1.0 / D, 1e-5,
                             op0=ALU.mult, op1=ALU.add)
            p.bar()
            sc.activation(out=stat[:, 2:3], in_=stat[:, 1:2], func=AF.Sqrt)
            p.bar()
            ve.reciprocal(stat[:, 3:4], stat[:, 2:3])
            p.bar()
            ve.tensor_scalar(xnb[:], scr[:], stat[:, 3:4], None, op0=ALU.mult)
            p.bar()
            sy.dma_start(out=XN[i * 128:(i + 1) * 128, :], in_=xnb[:])
            for j in range(0, 16, 2):
                te.transpose(pt0[:], xnb[:, j * 128:(j + 1) * 128], eye[:])
                te.transpose(pt1[:], xnb[:, (j + 1) * 128:(j + 2) * 128], eye[:])
                p.bar()
                ve.tensor_copy(tev[:, j * 128:(j + 1) * 128], pt0[:])
                ve.tensor_copy(tev[:, (j + 1) * 128:(j + 2) * 128], pt1[:])
                p.bar()
            sc.dma_start(
                out=XNT[:, i * 128:(i + 1) * 128].rearrange("(c p) f -> p c f", p=128),
                in_=tev[:].rearrange("p (c f) -> p c f", f=128))
            p.bar()
        p.finish()
        p.replay(block)
    return nc


def stage_l1(inputs):
    t = host_tables()
    x = np.asarray(inputs["L0_attn_hidden_states"]).reshape(S, D)
    return [{
        "x_rows": np.ascontiguousarray(x[c * TPC:(c + 1) * TPC]).astype(np.float32),
        "eye_bf": t["eye_bf"],
    } for c in range(NCORES)]



TABLE_SPECS = [
    ("eye_bf", [128, 128], BF16), ("eye_f32", [128, 128], F32),
    ("lt_bf", [128, 128], BF16),
    ("ut32_f32", [32, 32], F32), ("eye32_f32", [32, 32], F32),
    ("ones_col_bf", [128, 1], BF16), ("ones_1x128_bf", [1, 128], BF16),
    ("ones_1x128_f32", [1, 128], F32), ("ones_32c_f32", [32, 1], F32),
    ("ones_1x1_f32", [1, 1], F32),
    ("iota32", [128, 32], F32), ("iota128c", [128, 1], F32),
    ("keyloc_bf", [128, 256], BF16), ("cval", [128, 40], F32),
    ("bnd32", [32, NT], F32), ("qloc0", [128, 1], F32), ("qloc1", [128, 1], F32),
    ("svaltab", [128, 192], F32),
]


def build_l2(debug=0, nown=64, nsel=NT, cut=99, safe=False):
    nc = bass.Bass("TRN2", target_bir_lowering=False)
    XNT = nc.dram_tensor("xnt_full", [D, S], BF16, kind="ExternalInput").ap()
    WQ = nc.dram_tensor("wq2", [D, 256], F32, kind="ExternalInput").ap()
    WK = nc.dram_tensor("wk1", [D, 128], F32, kind="ExternalInput").ap()
    WV = nc.dram_tensor("wv1", [D, 128], F32, kind="ExternalInput").ap()
    COST = nc.dram_tensor("cosT", [S, 64], F32, kind="ExternalInput").ap()
    SINT = nc.dram_tensor("sinT", [S, 64], F32, kind="ExternalInput").ap()
    COSU = nc.dram_tensor("cosTu", [S, 64], F32, kind="ExternalInput").ap()
    SINU = nc.dram_tensor("sinTu", [S, 64], F32, kind="ExternalInput").ap()
    tby = {nm: nc.dram_tensor(nm, shp, dt, kind="ExternalInput").ap()
           for nm, shp, dt in TABLE_SPECS}

    OT = nc.dram_tensor("ot_sl", [256, S], BF16, kind="ExternalOutput").ap()

    ik = "ExternalOutput" if debug else "Internal"
    QDs = [nc.dram_tensor(f"qd{h}", [S, 128], BF16, kind=ik).ap()
           for h in range(2)]
    KBL = nc.dram_tensor("kbl", [NCHK * 128, 256], BF16, kind=ik).ap()
    VP = nc.dram_tensor("vp", [NCHK * 128, 258], BF16, kind=ik).ap()
    LISTs = [nc.dram_tensor(f"list{h}", [NT * 128, 1], I16, kind=ik).ap()
             for h in range(2)]
    OSs = [nc.dram_tensor(f"os{h}", [4 * S, 129], F32,
                          kind=("ExternalOutput" if debug == 2 else "Internal")).ap()
           for h in range(2)]
    if debug:
        DBG = nc.dram_tensor("dbg", [128, 6 * NT], F32, kind="ExternalOutput").ap()
        DBGK = nc.dram_tensor("dbgk", [128, 32], F32, kind="ExternalOutput").ap()

    es = contextlib.ExitStack()

    def sb(name, shape, dt):
        return es.enter_context(nc.sbuf_tensor(name, shape, dt))

    with es:
        tb = {nm: sb("t_" + nm, shp, dt) for nm, shp, dt in TABLE_SPECS}
        wqb = sb("wqb", [128, 16 * 256], BF16)
        wkb = sb("wkb", [128, 16 * 128], BF16)
        wvb = sb("wvb", [128, 16 * 128], BF16)
        wstg = sb("wstg", [128, 16 * 256], F32)
        xnt_t = sb("xnt_t", [128, 2048], BF16)
        rop = sb("rop", [128, 128], F32)
        sc1 = sb("sc1", [128, 128], F32)
        sc2 = sb("sc2", [128, 128], F32)
        ctt = sb("ctt", [128, 64], F32)
        stt_ = sb("stt_", [128, 64], F32)
        cttu = sb("cttu", [128, 64], F32)
        sttu = sb("sttu", [128, 64], F32)
        ktokb = sb("ktokb", [128, 128], BF16)
        kfev = sb("kfev", [128, 128], BF16)
        kbev = sb("kbev", [128, 128], BF16)
        vev = sb("vev", [128, 129], BF16)
        qsbs = [sb(f"qsb{h}", [128, 64 * 128], BF16) for h in range(2)]
        kmacc = sb("kmacc", [128, 64], F32)
        kmean = sb("kmean", [128, 32], BF16)
        qft = sb("qft", [128, 128], BF16)
        gg = sb("gg", [128, 32], F32)
        tmask = sb("tmask", [128, 32], F32)
        m8 = sb("m8", [128, 8], F32)
        hraw = sb("hraw", [128, 96], F32)
        escr = sb("escr", [128, 96], F32)
        idxc = sb("idxc", [128, 3], F32)
        isbad = sb("isbad", [128, 3], I32)
        hcb = sb("hcb", [128, 3 * 32], BF16)
        hall = sb("hall", [128, 32], BF16)
        ssum = sb("ssum", [128, 32], F32)
        csrow = sb("csrow", [1, 32], BF16)
        csevf = sb("csevf", [1, 32], F32)
        ctabs = [sb(f"ctab{h}", [128, 192], F32) for h in range(2)]
        prels = [sb(f"prel{h}", [128, 192], F32) for h in range(2)]
        totf = sb("totf", [1, 32], F32)
        ceilr = sb("ceilr", [1, 32], F32)
        ceilc = sb("ceilc", [32, 1], F32)
        basec = sb("basec", [32, 1], F32)
        baser = sb("baser", [1, 32], F32)
        basebc = sb("basebc", [128, 32], F32)
        cmp32 = sb("cmp32", [32, NT], F32)
        tcrow = sb("tcrow", [1, NT], F32)
        tcb = sb("tcb", [128, NT], F32)
        hcf = sb("hcf", [128, 32], F32)
        pcol = sb("pcol", [128, 1], F32)
        posi = sb("posi", [128, 1], I32)
        svali = sb("svali", [128, 1], I16)
        svalf = sb("svalf", [128, 1], F32)
        zro16 = sb("zro16", [128, NT], I16)
        vtabs = [sb(f"vtab{h}", [128, NT], F32) for h in range(2)]
        qposs = [sb(f"qpos{h}", [128, NT], F32) for h in range(2)]
        gidxs = [sb(f"gidx{h}", [128, NT], I32) for h in range(2)]
        sidxs = [sb(f"sidx{h}", [128, NT], I32) for h in range(2)]
        kvidxs = [sb(f"kvidx{h}", [128, NT], I32) for h in range(2)]
        qrels = [sb(f"qrel{h}", [128, NT], F32) for h in range(2)]
        lraw = sb("lraw", [128, NT], I16)
        hcall = sb("hcall", [128, 192 * 32], F32)
        posi192 = sb("posi192", [128, 192], I32)
        svali192 = sb("svali192", [128, 192], I16)
        lrf = sb("lrf", [128, NT], F32)
        dscr = sb("dscr", [128, NT], F32)
        dscr2 = sb("dscr2", [128, NT], F32)
        qg = sb("qg", [128, 2 * 128], BF16)
        kst = sb("kst", [128, 2 * 256], BF16)
        vst = sb("vst", [128, 2 * 258], BF16)
        qftl = sb("qftl", [128, 2 * 128], BF16)
        expt = sb("expt", [128, 2 * 256], BF16)
        probs = sb("probs", [128, 2 * 256], BF16)
        ptev = sb("ptev", [128, 2 * 256], BF16)
        ostg = sb("ostg", [128, 2 * 129], F32)
        comb = sb("comb", [128, 4 * 129], F32)
        osum = sb("osum", [128, 129], F32)
        s01 = sb("s01", [128, 129], F32)
        s23 = sb("s23", [128, 129], F32)
        rz = sb("rz", [128, 1], F32)
        ofin = sb("ofin", [128, 128], BF16)
        otev = sb("otev", [128, 128], BF16)

        block = es.enter_context(nc.Block())
        p = Prog(nc)
        sy, te, ve, sc, gp = (p.on("sync"), p.on("tensor"), p.on("vector"),
                              p.on("scalar"), p.on("gpsimd"))

        for nm, _, _ in TABLE_SPECS:
            sy.dma_start(out=tb[nm][:], in_=tby[nm])
        sy.dma_start(out=wstg[:].rearrange("p (c f) -> p c f", f=256),
                     in_=WQ.rearrange("(c p) f -> p c f", p=128))
        p.bar()
        ve.tensor_copy(wqb[:], wstg[:])
        p.bar()
        sy.dma_start(out=wstg[:, 0:2048].rearrange("p (c f) -> p c f", f=128),
                     in_=WK.rearrange("(c p) f -> p c f", p=128))
        p.bar()
        ve.tensor_copy(wkb[:], wstg[:, 0:2048])
        p.bar()
        sy.dma_start(out=wstg[:, 0:2048].rearrange("p (c f) -> p c f", f=128),
                     in_=WV.rearrange("(c p) f -> p c f", p=128))
        p.bar()
        ve.tensor_copy(wvb[:], wstg[:, 0:2048])
        ve.tensor_copy(svali192[:], tb["svaltab"][:])
        ve.memset(vev[:, 128:129], 1.0)
        ve.memset(zro16[:], 0)
        ve.memset(kmacc[:], 0.0)
        p.bar()
        for hh in range(2):
            sy.dma_start(out=LISTs[hh].rearrange("(t p) one -> t (p one)", p=128)[0:128, :],
                         in_=zro16[:, 0:128])
            sy.dma_start(out=LISTs[hh].rearrange("(t p) one -> t (p one)", p=128)[128:224, :],
                         in_=zro16[0:96, 0:128])
        p.bar()

        with nc.psum_tensor("bk", [128, 128], F32) as bk, \
             nc.psum_tensor("bv", [128, 128], F32) as bv, \
             nc.psum_tensor("bqa", [128, 128], F32) as bqa, \
             nc.psum_tensor("bqb", [128, 128], F32) as bqb, \
             nc.psum_tensor("bkT", [128, 128], BF16) as bkT:
            for t in range(64):
                sy.dma_start(out=xnt_t[:].rearrange("p (c f) -> p c f", f=128),
                             in_=XNT[:, t * 128:(t + 1) * 128].rearrange(
                                 "(c p) f -> p c f", p=128))
                sc.dma_start(out=ctt[:], in_=COST[t * 128:(t + 1) * 128, :])
                sc.dma_start(out=stt_[:], in_=SINT[t * 128:(t + 1) * 128, :])
                sc.dma_start(out=cttu[:], in_=COSU[t * 128:(t + 1) * 128, :])
                sc.dma_start(out=sttu[:], in_=SINU[t * 128:(t + 1) * 128, :])
                p.bar()
                for j in range(16):
                    st, sp = (j == 0), (j == 15)
                    xb = xnt_t[:, j * 128:(j + 1) * 128]
                    te.matmul(bk[:], xb, wkb[:, j * 128:(j + 1) * 128],
                              start=st, stop=sp)
                    te.matmul(bv[:], xb, wvb[:, j * 128:(j + 1) * 128],
                              start=st, stop=sp)
                    te.matmul(bqa[:], xb, wqb[:, j * 256:j * 256 + 128],
                              start=st, stop=sp)
                    te.matmul(bqb[:], xb, wqb[:, j * 256 + 128:(j + 1) * 256],
                              start=st, stop=sp)
                p.bar()
                ve.tensor_mul(rop[:, 0:64], bk[:, 0:64], cttu[:])
                p.bar()
                ve.tensor_mul(rop[:, 64:128], bk[:, 64:128], cttu[:])
                p.bar()
                ve.tensor_mul(sc1[:, 0:64], bk[:, 64:128], sttu[:])
                p.bar()
                ve.tensor_mul(sc2[:, 0:64], bk[:, 0:64], sttu[:])
                p.bar()
                ve.tensor_sub(rop[:, 0:64], rop[:, 0:64], sc1[:, 0:64])
                p.bar()
                ve.tensor_add(rop[:, 64:128], rop[:, 64:128], sc2[:, 0:64])
                p.bar()
                ve.tensor_copy(ktokb[:], rop[:])
                p.bar()
                te.transpose(bkT[:], ktokb[:], tb["eye_bf"][:])
                p.bar()
                ve.tensor_copy(kfev[:], bkT[:])
                p.bar()
                ve.reduce_sum(kmacc[:, t:t + 1], kfev[:], axis=AX.X)
                ve.tensor_copy(kbev[:].rearrange("d (s k) -> d s k", s=2),
                               kfev[:].rearrange("d (k s) -> d s k", s=2))
                p.bar()
                ch_, hf = t // 2, t % 2
                sy.dma_start(
                    out=KBL[ch_ * 128:(ch_ + 1) * 128, :].rearrange(
                        "d (s k) -> d s k", s=2)[:, :, hf * 64:(hf + 1) * 64],
                    in_=kbev[:].rearrange("d (s k) -> d s k", s=2))
                ve.tensor_copy(vev[:, 0:128], bv[:])
                p.bar()
                sy.dma_start(
                    out=VP[t * 64:(t + 1) * 64, :].rearrange(
                        "r (s c) -> (r s) c", s=2),
                    in_=vev[:])
                for h, bq in ((0, bqa), (1, bqb)):
                    ve.tensor_mul(rop[:, 0:64], bq[:, 0:64], ctt[:])
                    p.bar()
                    ve.tensor_mul(rop[:, 64:128], bq[:, 64:128], ctt[:])
                    p.bar()
                    ve.tensor_mul(sc1[:, 0:64], bq[:, 64:128], stt_[:])
                    p.bar()
                    ve.tensor_mul(sc2[:, 0:64], bq[:, 0:64], stt_[:])
                    p.bar()
                    ve.tensor_sub(rop[:, 0:64], rop[:, 0:64], sc1[:, 0:64])
                    p.bar()
                    ve.tensor_add(rop[:, 64:128], rop[:, 64:128], sc2[:, 0:64])
                    p.bar()
                    ve.tensor_copy(qsbs[h][:, t * 128:(t + 1) * 128], rop[:])
                    p.bar()
                    sy.dma_start(out=QDs[h][t * 128:(t + 1) * 128, :],
                                 in_=qsbs[h][:, t * 128:(t + 1) * 128])
                p.bar()
        ve.tensor_add(dscr[:, 0:32], kmacc[:, 0:64:2], kmacc[:, 1:64:2])
        p.bar()
        ve.tensor_scalar(kmean[:], dscr[:, 0:32], 1.0 / 256, None, op0=ALU.mult)
        p.bar()

        with nc.psum_tensor("pqf", [128, 128], BF16) as pqf, \
             nc.psum_tensor("pg", [128, 32], F32) as pg, \
             nc.psum_tensor("pcs", [1, 32], F32) as pcs, \
             nc.psum_tensor("pr", [128, 32], F32) as pr, \
             nc.psum_tensor("ptot", [128, 32], F32) as ptot, \
             nc.psum_tensor("ptc", [128, NT], F32) as ptc, \
             nc.psum_tensor("ptr", [128, 128], F32) as ptr:
            for h in range(2):
                for qt in range(64):
                    qc = qt // 2
                    te.transpose(pqf[:], qsbs[h][:, qt * 128:(qt + 1) * 128],
                                 tb["eye_bf"][:])
                    p.bar()
                    ve.tensor_copy(qft[:], pqf[:])
                    p.bar()
                    te.matmul(pg[:], qft[:], kmean[:], start=True, stop=True)
                    if qt % 2 == 0:
                        ve.tensor_scalar(tmask[:], tb["iota32"][:], qc - 0.5, NEG,
                                         op0=ALU.is_ge, op1=ALU.mult)
                    p.bar()
                    ve.tensor_add(gg[:], pg[:], tmask[:])
                    p.bar()
                    ve.max(out=m8[:], in_=gg[:])
                    p.bar()
                    for j in (1, 2, 3):
                        sg_ = slice((j - 1) * 32, j * 32)
                        ve.tensor_scalar(hraw[:, sg_], gg[:], m8[:, j - 1:j],
                                         None, op0=ALU.is_equal)
                    p.bar()
                    for j in (1, 2, 3):
                        sg_ = slice((j - 1) * 32, j * 32)
                        ve.tensor_mul(escr[:, sg_], hraw[:, sg_], tb["iota32"][:])
                    ve.tensor_scalar(isbad[:], m8[:, 0:3], -1.0e29, None,
                                     op0=ALU.is_le)
                    p.bar()
                    ve.tensor_reduce(idxc[:], escr[:].rearrange(
                        "p (j k) -> p j k", k=32), axis=AX.X, op=ALU.add)
                    p.bar()
                    for j in (1, 2, 3):
                        ve.select(idxc[:, j - 1:j], isbad[:, j - 1:j],
                                  tb["cval"][:, qc + j:qc + j + 1],
                                  idxc[:, j - 1:j])
                    p.bar()
                    ve.tensor_copy(ctabs[h][:, qt * 3:qt * 3 + 3], idxc[:])
                    for j in (1, 2, 3):
                        ve.tensor_scalar(hcb[:, (j - 1) * 32:j * 32],
                                         tb["iota32"][:], idxc[:, j - 1:j], None,
                                         op0=ALU.is_equal)
                    p.bar()
                    ve.tensor_add(hall[:], hcb[:, 0:32], hcb[:, 32:64])
                    p.bar()
                    ve.tensor_add(hall[:], hall[:], hcb[:, 64:96])
                    p.bar()
                    te.matmul(pr[:], tb["lt_bf"][:], hall[:], start=True, stop=True)
                    p.bar()
                    ve.tensor_copy(ssum[:], pr[:])
                    p.bar()
                    if qt > 0:
                        ve.tensor_add(ssum[:], ssum[:], ptot[:])
                        p.bar()
                    for j in (1, 2, 3):
                        ve.tensor_mul(escr[:, (j - 1) * 32:j * 32],
                                      hcb[:, (j - 1) * 32:j * 32], ssum[:])
                    p.bar()
                    ve.tensor_reduce(prels[h][:, qt * 3:qt * 3 + 3],
                                     escr[:].rearrange("p (j k) -> p j k", k=32),
                                     axis=AX.X, op=ALU.add)
                    p.bar()
                    for j in (1, 2, 3):
                        te.matmul(pcs[0:1, :], tb["ones_col_bf"][:],
                                  hcb[:, (j - 1) * 32:j * 32],
                                  start=(j == 1), stop=(j == 3))
                    p.bar()
                    ve.tensor_copy(csevf[0:1, :], pcs[0:1, :])
                    p.bar()
                    ve.tensor_copy(csrow[0:1, :], csevf[0:1, :])
                    p.bar()
                    te.matmul(ptot[:], tb["ones_1x128_bf"][:], csrow[0:1, :],
                              start=(qt == 0), stop=(qt == 63))
                    p.bar()
                ve.tensor_copy(totf[0:1, :], ptot[0:1, :])
                p.bar()
                ve.tensor_scalar(ceilr[0:1, :], totf[0:1, :], 1.0 / 128,
                                 0.49609375, op0=ALU.mult, op1=ALU.add)
                p.bar()
                ve.tensor_scalar(ceilr[0:1, :], ceilr[0:1, :], MAGIC, None,
                                 op0=ALU.add)
                p.bar()
                ve.tensor_scalar(ceilr[0:1, :], ceilr[0:1, :], MAGIC, 128.0,
                                 op0=ALU.subtract, op1=ALU.mult)
                p.bar()
                te.matmul(pg[0:32, 0:1], ceilr[0:1, :], tb["ones_1x1_f32"][:],
                          start=True, stop=True)
                p.bar()
                ve.tensor_copy(ceilc[:], pg[0:32, 0:1])
                p.bar()
                te.matmul(pr[0:32, 0:1], tb["ut32_f32"][:], ceilc[:],
                          start=True, stop=True)
                p.bar()
                ve.tensor_copy(basec[:], pr[0:32, 0:1])
                p.bar()
                te.matmul(pg[0:1, 0:32], basec[:], tb["eye32_f32"][:],
                          start=True, stop=True)
                p.bar()
                ve.tensor_copy(baser[0:1, :], pg[0:1, 0:32])
                p.bar()
                te.matmul(pr[:], tb["ones_1x128_f32"][:], baser[0:1, :],
                          start=True, stop=True)
                p.bar()
                ve.tensor_copy(basebc[:], pr[:])
                ve.tensor_scalar(cmp32[:], tb["bnd32"][:], basec[:], None,
                                 op0=ALU.is_ge)
                p.bar()
                te.matmul(ptc[0:1, :], tb["ones_32c_f32"][:], cmp32[:],
                          start=True, stop=True)
                p.bar()
                ve.tensor_scalar(tcrow[0:1, :], ptc[0:1, :], 1.0, None,
                                 op0=ALU.subtract)
                p.bar()
                te.matmul(ptc[:], tb["ones_1x128_f32"][:], tcrow[0:1, :],
                          start=True, stop=True)
                p.bar()
                ve.tensor_copy(tcb[:], ptc[:])
                p.bar()
                hc3 = hcall[:].rearrange("p (t k) -> p t k", k=32)
                i3 = tb["iota32"][:].unsqueeze(1).to_broadcast([128, 192, 32])
                c3 = ctabs[h][:].unsqueeze(2).to_broadcast([128, 192, 32])
                b3 = basebc[:].unsqueeze(1).to_broadcast([128, 192, 32])
                ve.tensor_tensor(hc3, i3, c3, op=ALU.is_equal)
                p.bar()
                ve.tensor_mul(hc3, hc3, b3)
                p.bar()
                ve.tensor_reduce(dscr[:, 0:192], hc3, axis=AX.X, op=ALU.add)
                p.bar()
                ve.tensor_add(dscr[:, 0:192], dscr[:, 0:192], prels[h][:])
                p.bar()
                ve.tensor_copy(posi192[:], dscr[:, 0:192])
                p.bar()
                for cix in range(192):
                    gp.indirect_dma_start(
                        out=LISTs[h],
                        out_offset=bass.IndirectOffsetOnAxis(
                            posi192[:, cix:cix + 1], 0),
                        in_=svali192[:, cix:cix + 1],
                        in_offset=None,
                        oob_is_err=False)
                p.bar()
                for blk, nblk in ((0, 128), (1, NT - 128)):
                    sy.dma_start(out=lraw[0:nblk, 0:128],
                                 in_=LISTs[h].rearrange("(t p) 1 -> t p", p=128)[
                                     blk * 128:blk * 128 + nblk, :])
                    p.bar()
                    ve.tensor_copy(lrf[0:nblk, 0:128], lraw[0:nblk, 0:128])
                    p.bar()
                    te.transpose(ptr[:, 0:nblk], lrf[0:nblk, 0:128],
                                 tb["eye_f32"][0:nblk, 0:nblk])
                    p.bar()
                    ve.tensor_copy(vtabs[h][:, blk * 128:blk * 128 + nblk],
                                   ptr[:, 0:nblk])
                    p.bar()
                vt = vtabs[h]
                ve.tensor_scalar(dscr[:], vt[:], 1.5, 0.25,
                                 op0=ALU.subtract, op1=ALU.mult)
                p.bar()
                ve.tensor_scalar(dscr[:], dscr[:], MAGIC, None, op0=ALU.add)
                p.bar()
                ve.tensor_scalar(qposs[h][:], dscr[:], MAGIC, None, op0=ALU.subtract)
                p.bar()
                ve.tensor_scalar_max(dscr[:], qposs[h][:], 0.0)
                p.bar()
                ve.tensor_copy(gidxs[h][:], dscr[:])
                ve.tensor_scalar(dscr[:], vt[:], 0.5, 1.0e7,
                                 op0=ALU.is_le, op1=ALU.mult)
                p.bar()
                ve.tensor_add(dscr2[:], vt[:], dscr[:])
                p.bar()
                ve.tensor_copy(sidxs[h][:], dscr2[:])
                ve.tensor_scalar(dscr[:], tcb[:], 128.0, None, op0=ALU.mult)
                p.bar()
                ve.tensor_add(dscr[:], dscr[:],
                              tb["iota128c"][:].to_broadcast([128, NT]))
                p.bar()
                ve.tensor_copy(kvidxs[h][:], dscr[:])
                ve.tensor_scalar(dscr2[:], tcb[:], 256.0, None, op0=ALU.mult)
                p.bar()
                ve.tensor_sub(qrels[h][:], qposs[h][:], dscr2[:])
                p.bar()

        if debug == 1:
            ve.tensor_copy(dscr[:], gidxs[0][:])
            p.bar()
            sy.dma_start(out=DBG[:, 0:NT], in_=dscr[:])
            ve.tensor_copy(dscr[:], sidxs[0][:])
            p.bar()
            sy.dma_start(out=DBG[:, NT:2 * NT], in_=dscr[:])
            ve.tensor_copy(dscr[:], kvidxs[0][:])
            p.bar()
            sy.dma_start(out=DBG[:, 2 * NT:3 * NT], in_=dscr[:])
            sy.dma_start(out=DBG[:, 3 * NT:4 * NT], in_=qrels[0][:])
            sy.dma_start(out=DBG[:, 4 * NT:5 * NT], in_=vtabs[0][:])
            sy.dma_start(out=DBG[:, 5 * NT:6 * NT], in_=tcb[:])
            ve.tensor_copy(gg[:], kmean[:])
            p.bar()
            sy.dma_start(out=DBGK, in_=gg[:])
            p.finish()
            p.replay(block)
            return nc

        with nc.psum_tensor("bqf0", [128, 128], BF16) as bqf0, \
             nc.psum_tensor("bqf1", [128, 128], BF16) as bqf1, \
             nc.psum_tensor("bs0", [128, 256], F32) as bs0, \
             nc.psum_tensor("bs1", [128, 256], F32) as bs1, \
             nc.psum_tensor("bt0", [128, 256], BF16) as bt0, \
             nc.psum_tensor("bt1", [128, 256], BF16) as bt1, \
             nc.psum_tensor("bo0", [128, 129], F32) as bo0, \
             nc.psum_tensor("bo1", [128, 129], F32) as bo1:
            bss, bts, bos = [bs0, bs1], [bt0, bt1], [bo0, bo1]
            bqfs = [bqf0, bqf1]
            hist = {k: {} for k in ("gath", "qft", "smm", "exp", "prb",
                                    "ptv", "oev", "sct")}

            def attn_unit(u, h, kind, idx):
                par = u % 2
                qgp = qg[:, par * 128:(par + 1) * 128]
                kstp = kst[:, par * 256:(par + 1) * 256]
                vstp = vst[:, par * 258:(par + 1) * 258]
                qftp = qftl[:, par * 128:(par + 1) * 128]
                exptp = expt[:, par * 256:(par + 1) * 256]
                prbp = probs[:, par * 256:(par + 1) * 256]
                ptevp = ptev[:, par * 256:(par + 1) * 256]
                ostgp = ostg[:, par * 129:(par + 1) * 129]
                pqfp = bqfs[par][:]

                w = hist["sct"].get(u - 2)
                if kind == "sel":
                    if w is not None:
                        p.wait("gpsimd", w)
                    gp.indirect_dma_start(
                        out=qgp, out_offset=None, in_=QDs[h],
                        in_offset=bass.IndirectOffsetOnAxis(
                            gidxs[h][:, idx:idx + 1], 0))
                    gp.indirect_dma_start(
                        out=kstp, out_offset=None, in_=KBL,
                        in_offset=bass.IndirectOffsetOnAxis(
                            kvidxs[h][:, idx:idx + 1], 0))
                    gp.indirect_dma_start(
                        out=vstp, out_offset=None, in_=VP,
                        in_offset=bass.IndirectOffsetOnAxis(
                            kvidxs[h][:, idx:idx + 1], 0))
                else:
                    c0 = idx // 2
                    if w is not None:
                        p.wait("sync", w)
                        p.wait("scalar", w)
                    sy.dma_start(out=kstp, in_=KBL[c0 * 128:(c0 + 1) * 128, :])
                    sc.dma_start(out=vstp, in_=VP[c0 * 128:(c0 + 1) * 128, :])
                hist["gath"][u] = p.snap()
                if safe:
                    p.bar()
                if cut <= 1:
                    for k in hist:
                        hist[k][u] = p.snap()
                    return

                p.wait("tensor", hist["gath"][u])
                if u >= 2:
                    p.wait("tensor", hist["qft"][u - 2])
                if kind == "sel":
                    te.transpose(pqfp, qgp, tb["eye_bf"][:])
                else:
                    te.transpose(pqfp, qsbs[h][:, idx * 128:(idx + 1) * 128],
                                 tb["eye_bf"][:])
                s_tp = p.snap()
                p.wait("vector", s_tp)
                ve.tensor_copy(qftp, pqfp)
                hist["qft"][u] = p.snap()
                if safe:
                    p.bar()
                if cut <= 2:
                    for k in ("smm", "exp", "prb", "ptv", "oev", "sct"):
                        hist[k][u] = p.snap()
                    return

                p.wait("tensor", hist["qft"][u])
                if u >= 2:
                    p.wait("tensor", hist["exp"][u - 2])
                te.matmul(bss[par][:], qftp, kstp, start=True, stop=True)
                hist["smm"][u] = p.snap()
                if safe:
                    p.bar()
                if cut <= 3:
                    for k in ("exp", "prb", "ptv", "oev", "sct"):
                        hist[k][u] = p.snap()
                    return

                p.wait("scalar", hist["smm"][u])
                sc.activation(out=exptp, in_=bss[par][:], func=AF.Exp)
                hist["exp"][u] = p.snap()
                if safe:
                    p.bar()
                if cut <= 4:
                    for k in ("prb", "ptv", "oev", "sct"):
                        hist[k][u] = p.snap()
                    return

                p.wait("vector", hist["exp"][u])
                if kind == "sel":
                    ve.scalar_tensor_tensor(
                        out=prbp, in0=tb["keyloc_bf"][:],
                        scalar=qrels[h][:, idx:idx + 1], in1=exptp,
                        op0=ALU.is_le, op1=ALU.mult)
                else:
                    ql = tb["qloc0"] if idx % 2 == 0 else tb["qloc1"]
                    ve.scalar_tensor_tensor(
                        out=prbp, in0=tb["keyloc_bf"][:], scalar=ql[:],
                        in1=exptp, op0=ALU.is_le, op1=ALU.mult)
                hist["prb"][u] = p.snap()
                if safe:
                    p.bar()
                if cut <= 5:
                    for k in ("ptv", "oev", "sct"):
                        hist[k][u] = p.snap()
                    return

                p.wait("tensor", hist["prb"][u])
                if u >= 2:
                    p.wait("tensor", hist["ptv"][u - 2])
                te.transpose(bts[par][:, 0:128], prbp[:, 0:128], tb["eye_bf"][:])
                te.transpose(bts[par][:, 128:256], prbp[:, 128:256], tb["eye_bf"][:])
                s_pt = p.snap()
                p.wait("vector", s_pt)
                ve.tensor_copy(ptevp, bts[par][:])
                hist["ptv"][u] = p.snap()
                if safe:
                    p.bar()
                if cut <= 6:
                    for k in ("oev", "sct"):
                        hist[k][u] = p.snap()
                    return

                p.wait("tensor", hist["ptv"][u])
                if u >= 2:
                    p.wait("tensor", hist["oev"][u - 2])
                te.matmul(bos[par][:], ptevp[:, 0:128], vstp[:, 0:129],
                          start=True, stop=False)
                te.matmul(bos[par][:], ptevp[:, 128:256], vstp[:, 129:258],
                          start=False, stop=True)
                s_pv = p.snap()
                p.wait("vector", s_pv)
                if u >= 2:
                    p.wait("vector", hist["sct"][u - 2])
                ve.tensor_copy(ostgp, bos[par][:])
                hist["oev"][u] = p.snap()
                if safe:
                    p.bar()
                if cut <= 7:
                    hist["sct"][u] = p.snap()
                    return

                if kind == "sel":
                    p.wait("gpsimd", hist["oev"][u])
                    gp.indirect_dma_start(
                        out=OSs[h], out_offset=bass.IndirectOffsetOnAxis(
                            sidxs[h][:, idx:idx + 1], 0),
                        in_=ostgp, in_offset=None, oob_is_err=False)
                else:
                    p.wait("sync", hist["oev"][u])
                    q0 = idx * 128
                    sy.dma_start(
                        out=OSs[h][4 * q0:4 * q0 + 512, :].rearrange(
                            "(q j) e -> q j e", j=4)[:, 0, :],
                        in_=ostgp)
                hist["sct"][u] = p.snap()
                if safe:
                    p.bar()

            u = 0
            for h in range(2):
                for idx in range(nown):
                    attn_unit(u, h, "own", idx)
                    u += 1
                for idx in range(nsel):
                    attn_unit(u, h, "sel", idx)
                    u += 1
            p.bar()

            for h in range(2 if debug != 2 else 0):
                for qt in range(64):
                    sy.dma_start(out=comb[:].rearrange("q (j e) -> q j e", e=129),
                                 in_=OSs[h][qt * 512:(qt + 1) * 512, :].rearrange(
                                     "(q j) e -> q j e", j=4))
                    p.bar()
                    ve.tensor_add(s01[:], comb[:, 0:129], comb[:, 129:258])
                    p.bar()
                    ve.tensor_add(s23[:], comb[:, 258:387], comb[:, 387:516])
                    p.bar()
                    ve.tensor_add(osum[:], s01[:], s23[:])
                    p.bar()
                    ve.reciprocal(rz[:], osum[:, 128:129])
                    p.bar()
                    ve.tensor_scalar(ofin[:], osum[:, 0:128], rz[:], None,
                                     op0=ALU.mult)
                    p.bar()
                    te.transpose(bqf0[:], ofin[:], tb["eye_bf"][:])
                    p.bar()
                    ve.tensor_copy(otev[:], bqf0[:])
                    p.bar()
                    sy.dma_start(out=OT[h * 128:(h + 1) * 128,
                                        qt * 128:(qt + 1) * 128], in_=otev[:])
                    p.bar()
        p.finish()
        p.replay(block)
    return nc


def stage_l2(inputs, xnt_full):
    t = host_tables()
    wq = np.asarray(inputs["L0_attn_wq"], np.float32)
    wk = np.asarray(inputs["L0_attn_wk"], np.float32)
    wv = np.asarray(inputs["L0_attn_wv"], np.float32)
    maps = []
    for c in range(NCORES):
        m = {
            "xnt_full": xnt_full,
            "wq2": np.ascontiguousarray(wq[:, 2 * c * 128:(2 * c + 2) * 128]),
            "wk1": np.ascontiguousarray(wk[:, (c // 2) * 128:(c // 2 + 1) * 128]),
            "wv1": np.ascontiguousarray(wv[:, (c // 2) * 128:(c // 2 + 1) * 128]),
            "cosT": t["cosT"], "sinT": t["sinT"],
            "cosTu": t["cosTu"], "sinTu": t["sinTu"],
        }
        for nm, _, _ in TABLE_SPECS:
            m[nm] = t[nm]
        maps.append(m)
    return maps


def kernel(**inputs):
    raise NotImplementedError



def build_l3():
    nc = bass.Bass("TRN2", target_bir_lowering=False)
    X = nc.dram_tensor("x_rows", [TPC, D], F32, kind="ExternalInput").ap()
    XNS = nc.dram_tensor("xn_sl", [TPC, D], BF16, kind="ExternalInput").ap()
    OTF = nc.dram_tensor("ot_full", [D, TPC], BF16, kind="ExternalInput").ap()
    WO = nc.dram_tensor("wo", [D, D], F32, kind="ExternalInput").ap()
    W1 = nc.dram_tensor("w1", [DFF, D], F32, kind="ExternalInput").ap()
    B1 = nc.dram_tensor("b1", [1, DFF], F32, kind="ExternalInput").ap()
    W2 = nc.dram_tensor("w2", [D, DFF], F32, kind="ExternalInput").ap()
    B2 = nc.dram_tensor("b2", [1, D], F32, kind="ExternalInput").ap()
    LNW = nc.dram_tensor("lnw", [1, D], F32, kind="ExternalInput").ap()
    LNB = nc.dram_tensor("lnb", [1, D], F32, kind="ExternalInput").ap()
    EYE = nc.dram_tensor("eye_bf", [128, 128], BF16, kind="ExternalInput").ap()
    OUT = nc.dram_tensor("out_rows", [TPC, D], F32, kind="ExternalOutput").ap()

    W1QT = nc.dram_tensor("w1qt", [D, DFF], BF16, kind="Internal").ap()
    W2QT = nc.dram_tensor("w2qt", [DFF, D], BF16, kind="Internal").ap()
    HQT = nc.dram_tensor("hqt", [D, TPC], BF16, kind="Internal").ap()
    Y1QT = nc.dram_tensor("y1qt", [DFF, TPC], BF16, kind="Internal").ap()
    Y2 = nc.dram_tensor("y2", [TPC, D], F32, kind="Internal").ap()

    es = contextlib.ExitStack()

    def sb(name, shape, dt):
        return es.enter_context(nc.sbuf_tensor(name, shape, dt))

    with es:
        eye = sb("eye", [128, 128], BF16)
        wld = sb("wld", [128, 4096], F32)
        qscr = sb("qscr", [128, 4096], F32)
        amx = sb("amx", [128, 128], F32)
        amx2 = sb("amx2", [128, 128], F32)
        qout = sb("qout", [128, 4096], BF16)
        tstage = sb("tstage", [128, 4096], BF16)
        hqtile = sb("hqtile", [128, 2048], BF16)
        xf = sb("xf", [128, D], F32)
        stat = sb("stat", [128, 4], F32)
        wbc = sb("wbc", [128, D], F32)
        bbc = sb("bbc", [128, D], F32)
        b1bc = sb("b1bc", [128, 2048], F32)
        b2bc = sb("b2bc", [128, D], F32)
        hb = sb("hb", [128, D], BF16)
        wbig = sb("wbig", [128, 16 * 1024], BF16)
        tb16 = sb("tb16", [128, 1024], BF16)
        g1 = sb("g1", [128, 1024], F32)
        g2 = sb("g2", [128, 1024], F32)
        g3 = sb("g3", [128, 1024], F32)
        y1tile = sb("y1tile", [128, 64 * 128], BF16)
        xt = sb("xt", [128, D], F32)
        y2t = sb("y2t", [128, D], F32)
        fft = sb("fft", [128, D], BF16)
        ot16 = sb("ot16", [128, 16 * 128], BF16)
        outt = sb("outt", [128, D], F32)

        block = es.enter_context(nc.Block())
        p = Prog(nc)
        sy, te, ve, sc, gp = (p.on("sync"), p.on("tensor"), p.on("vector"),
                              p.on("scalar"), p.on("gpsimd"))

        def quant(src, wdt, dst):
            nw = wdt // 32
            src3 = src.rearrange("p (w k) -> p w k", k=32)
            ve.tensor_reduce(amx[:, 0:nw], src3, axis=AX.X, op=ALU.max,
                             apply_absolute_value=True)
            p.bar()
            ve.tensor_scalar_max(amx[:, 0:nw], amx[:, 0:nw], 1e-30)
            p.bar()
            ami = amx[:, 0:nw].bitcast(I32)
            ve.tensor_scalar(ami, ami, 0x7F800000, None, op0=ALU.bitwise_and)
            p.bar()
            am2i = amx2[:, 0:nw].bitcast(I32)
            ve.tensor_scalar(am2i, ami, -1, 2181038080, op0=ALU.mult, op1=ALU.add)
            p.bar()
            ve.tensor_scalar(ami, ami, 50331648, None, op0=ALU.subtract)
            p.bar()
            inv3 = amx2[:, 0:nw].unsqueeze(2).to_broadcast([128, nw, 32])
            scl3 = amx[:, 0:nw].unsqueeze(2).to_broadcast([128, nw, 32])
            t3 = qscr[:, 0:wdt].rearrange("p (w k) -> p w k", k=32)
            ve.tensor_mul(t3, src3, inv3)
            p.bar()
            ve.tensor_scalar(qscr[:, 0:wdt], qscr[:, 0:wdt], MAGIC, None, op0=ALU.add)
            p.bar()
            ve.tensor_scalar(qscr[:, 0:wdt], qscr[:, 0:wdt], MAGIC, None,
                             op0=ALU.subtract)
            p.bar()
            ve.tensor_scalar(qscr[:, 0:wdt], qscr[:, 0:wdt], 127.0, -128.0,
                             op0=ALU.min, op1=ALU.max)
            p.bar()
            ve.tensor_mul(dst.rearrange("p (w k) -> p w k", k=32), t3, scl3)
            p.bar()

        def trans_block(src_bf, n128, dststage):
            for j in range(0, n128, 2):
                te.transpose(tp0[:], src_bf[:, j * 128:(j + 1) * 128], eye[:])
                if j + 1 < n128:
                    te.transpose(tp1[:], src_bf[:, (j + 1) * 128:(j + 2) * 128],
                                 eye[:])
                p.bar()
                ve.tensor_copy(dststage[:, j * 128:(j + 1) * 128], tp0[:])
                if j + 1 < n128:
                    ve.tensor_copy(dststage[:, (j + 1) * 128:(j + 2) * 128], tp1[:])
                p.bar()

        sy.dma_start(out=eye[:], in_=EYE)
        sy.dma_start(out=wbc[:], in_=LNW.to_broadcast([128, D]))
        sy.dma_start(out=bbc[:], in_=LNB.to_broadcast([128, D]))
        sy.dma_start(out=b2bc[:], in_=B2.to_broadcast([128, D]))
        p.bar()

        with nc.psum_tensor("tp0", [128, 128], BF16) as tp0, \
             nc.psum_tensor("tp1", [128, 128], BF16) as tp1, \
             nc.psum_tensor("mm0", [128, 512], F32) as mm0, \
             nc.psum_tensor("mm1", [128, 512], F32) as mm1:
            mms = [mm0, mm1]

            for i in range(64):
                sy.dma_start(out=wld[:, 0:D], in_=W1[i * 128:(i + 1) * 128, :])
                p.bar()
                quant(wld[:, 0:D], D, qout[:, 0:D])
                trans_block(qout[:, 0:D], 16, tstage)
                sy.dma_start(
                    out=W1QT[:, i * 128:(i + 1) * 128].rearrange(
                        "(c p) f -> p c f", p=128),
                    in_=tstage[:, 0:D].rearrange("p (c f) -> p c f", f=128))
                p.bar()
            for i in range(16):
                for hf in range(2):
                    sy.dma_start(out=wld[:],
                                 in_=W2[i * 128:(i + 1) * 128,
                                        hf * 4096:(hf + 1) * 4096])
                    p.bar()
                    quant(wld[:], 4096, qout[:])
                    trans_block(qout[:], 32, tstage)
                    sy.dma_start(
                        out=W2QT[hf * 4096:(hf + 1) * 4096,
                                 i * 128:(i + 1) * 128].rearrange(
                            "(c p) f -> p c f", p=128),
                        in_=tstage[:].rearrange("p (c f) -> p c f", f=128))
                    p.bar()

            for i in range(NTT):
                sy.dma_start(out=hb[:], in_=XNS[i * 128:(i + 1) * 128, :])
                p.bar()
                ve.tensor_copy(xf[:], hb[:])
                p.bar()
                ve.reduce_sum(stat[:, 0:1], xf[:], axis=AX.X)
                p.bar()
                ve.tensor_scalar(stat[:, 0:1], stat[:, 0:1], 1.0 / D, None,
                                 op0=ALU.mult)
                p.bar()
                ve.tensor_scalar(xf[:], xf[:], stat[:, 0:1], None, op0=ALU.subtract)
                p.bar()
                sc.activation(out=qscr[:, 0:D], in_=xf[:], func=AF.Square)
                p.bar()
                ve.reduce_sum(stat[:, 1:2], qscr[:, 0:D], axis=AX.X)
                p.bar()
                ve.tensor_scalar(stat[:, 1:2], stat[:, 1:2], 1.0 / D, 1e-5,
                                 op0=ALU.mult, op1=ALU.add)
                p.bar()
                sc.activation(out=stat[:, 2:3], in_=stat[:, 1:2], func=AF.Sqrt)
                p.bar()
                ve.reciprocal(stat[:, 3:4], stat[:, 2:3])
                p.bar()
                ve.tensor_scalar(xf[:], xf[:], stat[:, 3:4], None, op0=ALU.mult)
                p.bar()
                ve.tensor_mul(xf[:], xf[:], wbc[:])
                p.bar()
                ve.tensor_add(hb[:], xf[:], bbc[:])
                p.bar()
                quant(hb[:], D, hqtile[:])
                trans_block(hqtile[:], 16, tstage)
                sy.dma_start(
                    out=HQT[:, i * 128:(i + 1) * 128].rearrange(
                        "(c p) f -> p c f", p=128),
                    in_=tstage[:, 0:D].rearrange("p (c f) -> p c f", f=128))
                p.bar()

            for e8 in range(8):
                sy.dma_start(out=wbig[:].rearrange("p (c f) -> p c f", f=1024),
                             in_=W1QT[:, e8 * 1024:(e8 + 1) * 1024].rearrange(
                                 "(c p) f -> p c f", p=128))
                sc.dma_start(out=b1bc[:, 0:1024],
                             in_=B1[:, e8 * 1024:(e8 + 1) * 1024]
                             .to_broadcast([128, 1024]))
                p.bar()
                for i in range(NTT):
                    sy.dma_start(out=hqtile[:].rearrange("p (c f) -> p c f", f=128),
                                 in_=HQT[:, i * 128:(i + 1) * 128].rearrange(
                                     "(c p) f -> p c f", p=128))
                    p.bar()
                    for sub in range(2):
                        mm = mms[sub]
                        for j in range(16):
                            te.matmul(mm[:],
                                      hqtile[:, j * 128:(j + 1) * 128],
                                      wbig[:, j * 1024 + sub * 512:
                                           j * 1024 + sub * 512 + 512],
                                      start=(j == 0), stop=(j == 15))
                    p.bar()
                    ve.tensor_add(tb16[:, 0:512], mms[0][:], b1bc[:, 0:512])
                    ve.tensor_add(tb16[:, 512:1024], mms[1][:], b1bc[:, 512:1024])
                    p.bar()
                    gp.tensor_mul(g1[:], tb16[:], tb16[:])
                    p.bar()
                    ve.tensor_scalar(g1[:], g1[:], 0.044715, 1.0,
                                     op0=ALU.mult, op1=ALU.add)
                    p.bar()
                    gp.tensor_mul(g2[:], tb16[:], g1[:])
                    p.bar()
                    sc.activation(out=g3[:], in_=g2[:], func=AF.Tanh,
                                  scale=GELU_C)
                    p.bar()
                    ve.tensor_scalar(g3[:], g3[:], 1.0, 0.5,
                                     op0=ALU.add, op1=ALU.mult)
                    p.bar()
                    ve.tensor_mul(tb16[:], tb16[:], g3[:])
                    p.bar()
                    quant(tb16[:], 1024, qout[:, 0:1024])
                    trans_block(qout[:, 0:1024], 8, tstage)
                    sy.dma_start(
                        out=Y1QT[e8 * 1024:(e8 + 1) * 1024,
                                 i * 128:(i + 1) * 128].rearrange(
                            "(b pp) f -> pp b f", pp=128),
                        in_=tstage[:, 0:1024].rearrange("p (b f) -> p b f", f=128))
                    p.bar()

            for e2 in range(8):
                sy.dma_start(out=wbig[:].rearrange("p (c f) -> p c f", f=256),
                             in_=W2QT[:, e2 * 256:(e2 + 1) * 256].rearrange(
                                 "(c p) f -> p c f", p=128))
                p.bar()
                for i in range(NTT):
                    sy.dma_start(out=y1tile[:].rearrange("p (c f) -> p c f", f=128),
                                 in_=Y1QT[:, i * 128:(i + 1) * 128].rearrange(
                                     "(c p) f -> p c f", p=128))
                    p.bar()
                    for j in range(64):
                        te.matmul(mms[i % 2][:, 0:256],
                                  y1tile[:, j * 128:(j + 1) * 128],
                                  wbig[:, j * 256:(j + 1) * 256],
                                  start=(j == 0), stop=(j == 63))
                    p.bar()
                    ve.tensor_copy(g1[:, 0:256], mms[i % 2][:, 0:256])
                    p.bar()
                    sy.dma_start(out=Y2[i * 128:(i + 1) * 128,
                                        e2 * 256:(e2 + 1) * 256], in_=g1[:, 0:256])
                    p.bar()

            for half in range(2):
                for cch in range(16):
                    sy.dma_start(out=wld[:, 0:1024],
                                 in_=WO[cch * 128:(cch + 1) * 128,
                                        half * 1024:(half + 1) * 1024])
                    p.bar()
                    ve.tensor_copy(wbig[:, cch * 1024:(cch + 1) * 1024],
                                   wld[:, 0:1024])
                    p.bar()
                for i in range(NTT):
                    sy.dma_start(out=ot16[:].rearrange("p (c f) -> p c f", f=128),
                                 in_=OTF[:, i * 128:(i + 1) * 128].rearrange(
                                     "(c p) f -> p c f", p=128))
                    sc.dma_start(out=xt[:], in_=X[i * 128:(i + 1) * 128, :])
                    sc.dma_start(out=y2t[:], in_=Y2[i * 128:(i + 1) * 128, :])
                    sc.dma_start(out=hb[:], in_=XNS[i * 128:(i + 1) * 128, :])
                    p.bar()
                    ve.tensor_add(fft[:], y2t[:], b2bc[:])
                    p.bar()
                    ve.tensor_add(fft[:], fft[:], hb[:])
                    p.bar()
                    for sub in range(2):
                        mm = mms[sub % 2]
                        for j in range(16):
                            te.matmul(mm[:], ot16[:, j * 128:(j + 1) * 128],
                                      wbig[:, j * 1024 + sub * 512:
                                           j * 1024 + sub * 512 + 512],
                                      start=(j == 0), stop=(j == 15))
                        p.bar()
                        sl = slice(half * 1024 + sub * 512,
                                   half * 1024 + (sub + 1) * 512)
                        ve.tensor_add(outt[:, sl], mm[:], xt[:, sl])
                        p.bar()
                        ve.tensor_add(outt[:, sl], outt[:, sl], fft[:, sl])
                        p.bar()
                    sy.dma_start(out=OUT[i * 128:(i + 1) * 128,
                                         half * 1024:(half + 1) * 1024],
                                 in_=outt[:, half * 1024:(half + 1) * 1024])
                    p.bar()
        p.finish()
        p.replay(block)
    return nc


def stage_l3(inputs, xn_full, ot_full):
    t = host_tables()
    x = np.asarray(inputs["L0_attn_hidden_states"]).reshape(S, D).astype(np.float32)
    maps = []
    for c in range(NCORES):
        maps.append({
            "x_rows": np.ascontiguousarray(x[c * TPC:(c + 1) * TPC]),
            "xn_sl": np.ascontiguousarray(xn_full[c * TPC:(c + 1) * TPC]),
            "ot_full": np.ascontiguousarray(ot_full[:, c * TPC:(c + 1) * TPC]),
            "wo": np.asarray(inputs["L0_attn_wo"], np.float32),
            "w1": np.asarray(inputs["L0_ffn_W1"], np.float32),
            "b1": np.asarray(inputs["L0_ffn_b1"], np.float32).reshape(1, DFF),
            "w2": np.asarray(inputs["L0_ffn_W2"], np.float32),
            "b2": np.asarray(inputs["L0_ffn_b2"], np.float32).reshape(1, D),
            "lnw": np.asarray(inputs["L0_ffn_ln_w"], np.float32).reshape(1, D),
            "lnb": np.asarray(inputs["L0_ffn_ln_b"], np.float32).reshape(1, D),
            "eye_bf": t["eye_bf"],
        })
    return maps


def kernel(**inputs):
    nc1 = build_l1()
    res1 = run_bass_kernel_spmd(nc1, stage_l1(inputs), core_ids=list(range(NCORES)))
    xn_full = np.concatenate([res1.results[c]["xn_sl"] for c in range(NCORES)], axis=0)
    xnt_full = np.concatenate([res1.results[c]["xnt_sl"] for c in range(NCORES)],
                              axis=1)
    nc2 = build_l2(safe=True)
    res2 = run_bass_kernel_spmd(nc2, stage_l2(inputs, xnt_full),
                                core_ids=list(range(NCORES)))
    ot_full = np.concatenate([res2.results[c]["ot_sl"] for c in range(NCORES)],
                             axis=0)
    nc3 = build_l3()
    res3 = run_bass_kernel_spmd(nc3, stage_l3(inputs, xn_full, ot_full),
                                core_ids=list(range(NCORES)))
    out = np.concatenate([res3.results[c]["out_rows"] for c in range(NCORES)],
                         axis=0)
    return out.reshape(1, S, D).astype(np.float32)
